# version 12
# baseline (speedup 1.0000x reference)
import numpy as np
import ml_dtypes
from contextlib import ExitStack

import concourse.bass as bass
import concourse.mybir as mybir
from concourse.bass_utils import run_bass_kernel_spmd

F32 = mybir.dt.float32
BF16 = mybir.dt.bfloat16
AF = mybir.ActivationFunctionType
ALU = mybir.AluOpType
AX = mybir.AxisListType

ENGINES = ("tensor", "scalar", "vector", "gpsimd", "sync")
SAME_ENGINE_SYNC = True
NOSYNC_ENGINES = ("tensor",)

NB = 2
T = 2048
D = 1024
NT = T // 128
NC = D // 128



class DmaQueue:
    def __init__(self, prog, name, issue_engine, nlanes=8):
        self.prog = prog
        self.name = name
        self.eng = issue_engine
        self.nlanes = nlanes
        self.sems = [prog.nc.alloc_semaphore(f"dq_{name}_{i}") for i in range(nlanes)]
        self.lane_cnt = [0] * nlanes
        self.n = 0

    def key(self, lane):
        return f"__dmaq_{self.name}_{lane}"


class Prog:
    def __init__(self, nc, same_engine_sync=True):
        self.nc = nc
        self.sems = {e: nc.alloc_semaphore(f"S_{e}") for e in ENGINES}
        self.cnt = {e: 0 for e in ENGINES}
        self.items = {e: [] for e in ENGINES}
        self.writer = {}
        self.readers = {}
        self.frontier = {e: {} for e in ENGINES}
        self.same_sync = same_engine_sync
        self.queues = {}
        self.nops = 0

    def queue(self, name, issue_engine, nlanes=8):
        q = DmaQueue(self, name, issue_engine, nlanes)
        self.queues[name] = q
        return q

    def _src_id(self, src):
        return ("eng", src[1]) if src[0] == "eng" else ("dma", src[1], src[2])

    def _sem_of(self, sid):
        return self.sems[sid[1]] if sid[0] == "eng" else self.queues[sid[1]].sems[sid[2]]

    def _collect_waits(self, eng, reads, writes):
        waits = {}

        def need(src):
            cnt = src[-1]
            sid = self._src_id(src)
            if sid == ("eng", eng) and (not self.same_sync or eng in NOSYNC_ENGINES):
                return
            if self.frontier[eng].get(sid, 0) >= cnt:
                return
            old = waits.get(sid)
            if old is None or old < cnt:
                waits[sid] = cnt

        for k in reads:
            w = self.writer.get(k)
            if w is not None:
                need(w)
        for k in writes:
            w = self.writer.get(k)
            if w is not None:
                need(w)
            for sid, cnt in self.readers.get(k, {}).items():
                need((sid[0],) + sid[1:] + (cnt,))
        out = []
        for sid, val in waits.items():
            self.frontier[eng][sid] = val
            out.append((sid, val))
        return out

    def op(self, eng, fn, reads=(), writes=()):
        waits = self._collect_waits(eng, reads, writes)
        self.cnt[eng] += 1
        my = self.cnt[eng]
        self.items[eng].append((waits, fn, ("eng", eng), 1))
        me = ("eng", eng)
        for k in reads:
            self.readers.setdefault(k, {})[me] = my
        for k in writes:
            self.writer[k] = ("eng", eng, my)
            self.readers[k] = {}
        self.nops += 1
        return ("eng", eng, my)

    def dma(self, q, fn, reads=(), writes=()):
        eng = q.eng
        lane = q.n % q.nlanes
        reads = list(reads) + [q.key(lane)]
        waits = self._collect_waits(eng, reads, writes)
        q.n += 1
        q.lane_cnt[lane] += 16
        my = q.lane_cnt[lane]
        self.items[eng].append((waits, fn, ("dma", q.name, lane), 16))
        src = ("dma", q.name, lane, my)
        me = self._src_id(src)
        for k in reads:
            self.readers.setdefault(k, {})[me] = my
        self.writer[q.key(lane)] = src
        self.readers[q.key(lane)] = {}
        for k in writes:
            self.writer[k] = src
            self.readers[k] = {}
        self.nops += 1
        return src

    def barrier(self):
        targets = [(("eng", e), self.sems[e], self.cnt[e]) for e in ENGINES if self.cnt[e] > 0]
        for q in self.queues.values():
            for lane in range(q.nlanes):
                if q.lane_cnt[lane] > 0:
                    targets.append((("dma", q.name, lane), q.sems[lane], q.lane_cnt[lane]))
        for e in ENGINES:
            waits = []
            for sid, sem, val in targets:
                if sid == ("eng", e):
                    continue
                if self.frontier[e].get(sid, 0) >= val:
                    continue
                self.frontier[e][sid] = val
                waits.append((sid, val))
            if waits:
                self.items[e].append((waits, None, None, 0))

    def finish(self):
        eng = "gpsimd"
        for q in self.queues.values():
            for lane in range(q.nlanes):
                if q.lane_cnt[lane] > 0:
                    sid = ("dma", q.name, lane)
                    if self.frontier[eng].get(sid, 0) >= q.lane_cnt[lane]:
                        continue
                    self.items[eng].append(
                        ([(sid, q.lane_cnt[lane])], None, None, 0))
                    self.frontier[eng][sid] = q.lane_cnt[lane]

    def _prepare(self):
        pass

    def _run(self, eng_obj, eng):
        for waits, fn, dst, inc in self.items[eng]:
            for sid, wval in waits:
                if sid[0] == "eng":
                    eng_obj.wait_ge(self.sems[sid[1]], wval)
                else:
                    eng_obj.wait_ge(self.queues[sid[1]].sems[sid[2]], wval)
            if fn is not None:
                inst = fn(eng_obj)
                if dst[0] == "eng":
                    inst.then_inc(self.sems[eng], 1)
                else:
                    inst.then_inc(self.queues[dst[1]].sems[dst[2]], 16)

    def build_block(self):
        self.finish()
        self._prepare()
        nc = self.nc
        with nc.Block() as block:
            @block.sync
            def _(e):
                self._run(e, "sync")

            @block.tensor
            def _(e):
                self._run(e, "tensor")

            @block.scalar
            def _(e):
                self._run(e, "scalar")

            @block.vector
            def _(e):
                self._run(e, "vector")

            @block.gpsimd
            def _(e):
                self._run(e, "gpsimd")


class Ctx:

    def __init__(self, nc):
        self.nc = nc
        self.p = Prog(nc, same_engine_sync=SAME_ENGINE_SYNC)
        self.qs = self.p.queue("sp", "sync", nlanes=8)
        self.qw = self.p.queue("gw", "gpsimd", nlanes=8)
        self.st = ExitStack()
        self._scopes = []
        self._uid = 0

    def sb(self, name, shape, dtype):
        return self.st.enter_context(self.nc.sbuf_tensor(f"{name}_{self.uid()}", shape, dtype))

    def push_scope(self):
        self._scopes.append(self.st)
        self.st = ExitStack()

    def pop_scope(self):
        self.st.close()
        self.st = self._scopes.pop()
        self.p.barrier()

    def ps(self, name, shape, dtype=F32):
        return self.st.enter_context(self.nc.psum_tensor(name, shape, dtype))

    def uid(self):
        self._uid += 1
        return self._uid

    def shared(self):
        nc = self.nc
        self.ident_bf = self.sb("ident_bf", [128, 128], BF16)
        self.ident_f32 = self.sb("ident_f32", [128, 128], F32)
        self.mask64 = self.sb("mask64", [64, 64], F32)
        self.ones_col_bf = self.sb("ones_col_bf", [128, 1], BF16)
        self.ones_row_bf = self.sb("ones_row_bf", [1, 128], BF16)
        self.ones_row_f32 = self.sb("ones_row_f32", [1, 128], F32)
        self.zero_col = self.sb("zero_col", [128, 1], F32)
        ident_np = np.eye(128, dtype=ml_dtypes.bfloat16)
        identf_np = np.eye(128, dtype=np.float32)
        mask_np = np.triu(np.ones((64, 64), dtype=np.float32))
        d_ident = nc.inline_tensor(ident_np, "c_ident").ap()
        d_identf = nc.inline_tensor(identf_np, "c_identf").ap()
        d_mask = nc.inline_tensor(mask_np, "c_mask").ap()
        p, qs = self.p, self.qs
        self.dma(qs, d_ident, self.ident_bf[:, :], "ident_bf")
        self.dma(qs, d_identf, self.ident_f32[:, :], "ident_f32")
        self.dma(qs, d_mask, self.mask64[:, :], "mask64")
        p.op("vector", lambda e: e.memset(self.ones_col_bf[:, :], 1.0), writes=["ones_col_bf"])
        p.op("vector", lambda e: e.memset(self.ones_row_bf[:, :], 1.0), writes=["ones_row_bf"])
        p.op("vector", lambda e: e.memset(self.ones_row_f32[:, :], 1.0), writes=["ones_row_f32"])
        p.op("vector", lambda e: e.memset(self.zero_col[:, :], 0.0), writes=["zero_col"])
        self.cconst = self.sb("cconst", [128, 4], F32)
        self._cmap = {}
        for j, v in enumerate([1e-5, -1.0, 0.125, 0.5]):
            p.op("vector", lambda e, j=j, v=v: e.memset(self.cconst[:, j:j + 1], v),
                 writes=[f"cconst{j}"])
            self._cmap[v] = j
        self.PS = [self.ps(f"PSAR{i}", [128, 1024], F32) for i in range(4)]

    def cc(self, val, parts=128):
        j = self._cmap[val]
        return self.cconst[0:parts, j:j + 1], f"cconst{j}"

    def psk(self, i, half):
        return self.PS[i][:, half * 512:(half + 1) * 512], f"PS{i}_{half}"

    def psk8(self, idx):
        return self.psk(idx % 4, (idx // 4) % 2)

    def dma(self, q, src_ap, dst_ap, key, reads=(), transpose=False):
        if transpose:
            return self.p.dma(q, lambda e, o=dst_ap, i=src_ap: e.dma_start(out=o, in_=i, transpose=True),
                              reads=reads, writes=[key])
        return self.p.dma(q, lambda e, o=dst_ap, i=src_ap: e.dma_start(out=o, in_=i),
                          reads=reads, writes=[key])

    def store(self, q, dst_ap, src_ap, reads, writes=()):
        return self.p.dma(q, lambda e, o=dst_ap, i=src_ap: e.dma_start(out=o, in_=i),
                          reads=reads, writes=writes)



class XnLoader:

    def __init__(self, ctx, xn_dram, name):
        self.ctx = ctx
        self.dram = xn_dram
        self.t = [ctx.sb(f"{name}{c}", [128, T], BF16) for c in range(NC)]
        self.k = [f"{name}{c}" for c in range(NC)]

    def load(self, b):
        for c in range(NC):
            src = self.dram[b, :, c * 128:(c + 1) * 128]
            self.ctx.dma(self.ctx.qs, src, self.t[c][:, :], self.k[c], transpose=True)


def bias_cols_prep(ctx, dram_vec, n, name):
    stage = ctx.sb(f"bcs_{name}", [n, 128], F32)
    out = ctx.sb(f"bc_{name}", [128, n], F32)
    kst, kout = f"bcs_{name}", f"bc_{name}"
    ctx.dma(ctx.qs, dram_vec, stage[:, :], kst)
    psum = ctx.PS[3].bitcast(F32)[:, 0:128]
    ctx.p.op("tensor", lambda e: e.transpose(psum[0:128, 0:n], stage[:, :], ctx.ident_f32[0:n, 0:n]),
             reads=[kst, "ident_f32"], writes=["PS3_0"])
    ctx.p.op("vector", lambda e: e.tensor_copy(out=out[:, :], in_=psum[0:128, 0:n]),
             reads=["PS3_0"], writes=[kout])
    return out, kout


def bias_row(ctx, dram_vec, n, name, dtype=BF16):
    t = ctx.sb(f"br_{name}", [1, n], dtype)
    ctx.dma(ctx.qw if dtype == BF16 else ctx.qs, dram_vec, t[:, :], f"br_{name}")
    return t, f"br_{name}"



def emit_ln(ctx, xin, xout, xout32=None):
    p, qs = ctx.p, ctx.qs
    XT = [ctx.sb(f"ln_x{i}", [128, D], F32) for i in range(3)]
    XN = [ctx.sb(f"ln_o{i}", [128, D], BF16) for i in range(2)]
    XN32 = [ctx.sb(f"ln_p{i}", [128, D], F32) for i in range(2)] if xout32 is not None else None
    SQ = ctx.sb("ln_sq", [128, D], F32)
    C = ctx.sb("ln_c", [128, 8], F32)
    s1, s2, m, e2, nv, sd, r, nb = [C[:, j:j + 1] for j in range(8)]
    for b in range(NB):
        for i in range(NT):
            xk = f"ln_x{(b * NT + i) % 3}"
            ok = f"ln_o{(b * NT + i) % 2}"
            xt = XT[(b * NT + i) % 3]
            xo = XN[(b * NT + i) % 2]
            ctx.dma(qs, xin[b, i * 128:(i + 1) * 128, :], xt[:, :], xk)
            p.op("scalar", lambda e, xt=xt: e.activation(out=SQ[:, :], in_=xt[:, :], func=AF.Square,
                                                         accum_out=s2),
                 reads=[xk], writes=["ln_sq", "ln_s2"])
            p.op("vector", lambda e, xt=xt: e.reduce_sum(s1, xt[:, :], axis=AX.X),
                 reads=[xk], writes=["ln_s1"])
            p.op("vector", lambda e: e.tensor_scalar(out=m, in0=s1, scalar1=1.0 / D, scalar2=None, op0=ALU.mult),
                 reads=["ln_s1"], writes=["ln_m"])
            p.op("vector", lambda e: e.tensor_scalar(out=e2, in0=s2, scalar1=1.0 / D, scalar2=None, op0=ALU.mult),
                 reads=["ln_s2"], writes=["ln_e2"])
            p.op("vector", lambda e: e.scalar_tensor_tensor(out=nv, in0=m, scalar=m, in1=e2,
                                                            op0=ALU.mult, op1=ALU.subtract),
                 reads=["ln_m", "ln_e2"], writes=["ln_nv"])
            eps_ap, eps_k = ctx.cc(1e-5)
            neg1_ap, neg1_k = ctx.cc(-1.0)
            p.op("scalar", lambda e: e.activation(out=sd, in_=nv, func=AF.Sqrt, bias=eps_ap, scale=neg1_ap),
                 reads=["ln_nv", eps_k, neg1_k], writes=["ln_sd"])
            p.op("vector", lambda e: e.reciprocal(r, sd), reads=["ln_sd"], writes=["ln_r"])
            p.op("vector", lambda e: e.scalar_tensor_tensor(out=nb, in0=m, scalar=-1.0, in1=r,
                                                            op0=ALU.mult, op1=ALU.mult),
                 reads=["ln_m", "ln_r"], writes=["ln_nb"])
            p.op("scalar", lambda e, xt=xt, xo=xo: e.activation(out=xo[:, :], in_=xt[:, :], func=AF.Identity,
                                                                bias=nb, scale=r),
                 reads=[xk, "ln_nb", "ln_r"], writes=[ok])
            ctx.store(qs, xout[b, i * 128:(i + 1) * 128, :], xo[:, :], [ok])
            if xout32 is not None:
                ok32 = f"ln_p{(b * NT + i) % 2}"
                xo32 = XN32[(b * NT + i) % 2]
                p.op("vector", lambda e, xt=xt, xo32=xo32: e.tensor_scalar(
                    out=xo32[:, :], in0=xt[:, :], scalar1=r, scalar2=nb, op0=ALU.mult, op1=ALU.add),
                    reads=[xk, "ln_nb", "ln_r"], writes=[ok32])
                ctx.store(qs, xout32[b, i * 128:(i + 1) * 128, :], xo32[:, :], [ok32])



def emit_comb(ctx, xin, sin, gin, xout):
    p, qs = ctx.p, ctx.qs
    BX = [ctx.sb(f"cb_x{i}", [128, D], F32) for i in range(2)]
    BS = [ctx.sb(f"cb_s{i}", [128, D], F32) for i in range(2)]
    BG = [ctx.sb(f"cb_g{i}", [128, D], F32) for i in range(2)]
    BO = [ctx.sb(f"cb_o{i}", [128, D], F32) for i in range(2)]
    for b in range(NB):
        for i in range(NT):
            j = (b * NT + i) % 2
            kx, ks, kg, ko = f"cb_x{j}", f"cb_s{j}", f"cb_g{j}", f"cb_o{j}"
            sl = slice(i * 128, (i + 1) * 128)
            ctx.dma(qs, xin[b, sl, :], BX[j][:, :], kx)
            ctx.dma(qs, sin[b, sl, :], BS[j][:, :], ks)
            ctx.dma(qs, gin[b, sl, :], BG[j][:, :], kg)
            p.op("vector", lambda e, j=j: e.tensor_mul(BO[j][:, :], BS[j][:, :], BG[j][:, :]),
                 reads=[ks, kg], writes=[ko])
            p.op("vector", lambda e, j=j: e.tensor_add(BO[j][:, :], BO[j][:, :], BX[j][:, :]),
                 reads=[ko, kx], writes=[ko])
            ctx.store(qs, xout[b, sl, :], BO[j][:, :], [ko])



def emit_ffn0(ctx, xn, xn32, Wg, bg, W1, b1, W2, b2, gout):
    p, qs, qw = ctx.p, ctx.qs, ctx.qw
    nH = 16
    WGS = [ctx.sb(f"f0_wg{c}", [128, 8], F32) for c in range(NC)]
    for c in range(NC):
        ctx.dma(qs, Wg[c * 128:(c + 1) * 128, :], WGS[c][:, :], f"f0_wg{c}")
    bg_row, bg_k = bias_row(ctx, bg.rearrange("(o e) -> o e", o=1), 8, "f0bg", dtype=F32)
    XT32 = [ctx.sb(f"f0_xt{i}", [128, D], F32) for i in range(2)]
    XF32 = [ctx.sb(f"f0_xf{c}", [128, 128], F32) for c in range(NC)]
    b1T, b1T_k = bias_cols_prep(ctx, b1.rearrange("e (a p) -> (e a) p", p=128), 128, "f0b1")
    W1S = [ctx.sb(f"f0_w1_{c}", [128, 2048], BF16) for c in range(NC)]
    W2S = [ctx.sb(f"f0_w2_{h}", [128, 1024], BF16) for h in range(nH)]
    B2R = ctx.sb("f0_b2r", [1, 1024], BF16)
    H = [ctx.sb(f"f0_h{h}", [128, 512], BF16) for h in range(nH)]
    ACC = [ctx.sb(f"f0_acc{i}", [128, 1024], F32) for i in range(NT)]
    COMB = ctx.sb("f0_comb", [128, 8 * NT], F32)
    SC = ctx.sb("f0_sc", [128, 8], F32)
    em1, em2, den, rden, nm1, ex1 = [SC[:, j:j + 1] for j in range(6)]
    ET = ctx.sb("f0_et", [128, 8], F32)
    EQ = ctx.sb("f0_eq", [128, 8], F32)
    EMK = ctx.sb("f0_emk", [128, 8], F32)
    KP = ctx.sb("f0_kp", [128, 8], F32)
    GS = [ctx.sb(f"f0_gs{i}", [128, D], F32) for i in range(2)]

    xnl = XnLoader(ctx, xn, "f0xn")
    XN, xnk = xnl.t, xnl.k
    for b in range(NB):
        xnl.load(b)
        for i in range(NT):
            xti = i % 2
            xtk = f"f0_xt{xti}"
            ctx.dma(qs, xn32[b, i * 128:(i + 1) * 128, :], XT32[xti][:, :], xtk)
            for c in range(NC):
                pt = ctx.PS[3][:, 512 + (c % 4) * 128:512 + (c % 4) * 128 + 128]
                ptk = "PS3_1"
                p.op("tensor", lambda e, c=c, xti=xti, pt=pt: e.transpose(
                    pt, XT32[xti][:, c * 128:(c + 1) * 128], ctx.ident_f32[:, :]),
                    reads=[xtk, "ident_f32"], writes=[ptk])
                p.op("vector", lambda e, c=c, pt=pt: e.tensor_copy(out=XF32[c][:, :], in_=pt),
                     reads=[ptk], writes=[f"f0_xf{c}"])
            pg, pgk = ctx.psk(3, 0)
            pg = pg[:, 0:8]
            for c in range(NC):
                p.op("tensor", lambda e, c=c, pg=pg: e.matmul(
                    pg, XF32[c][:, :], WGS[c][:, :],
                    start=(c == 0), stop=False),
                    reads=[f"f0_xf{c}", f"f0_wg{c}"], writes=[pgk])
            p.op("tensor", lambda e, pg=pg: e.matmul(pg, ctx.ones_row_f32[0:1, :], bg_row[0:1, :],
                                                     start=False, stop=True),
                 reads=["ones_row_f32", bg_k], writes=[pgk])
            p.op("vector", lambda e, pg=pg: e.reduce_max(em1, pg, axis=AX.X),
                 reads=[pgk], writes=["f0_em1"])
            p.op("vector", lambda e: e.tensor_scalar(out=nm1, in0=em1, scalar1=-1.0, scalar2=None, op0=ALU.mult),
                 reads=["f0_em1"], writes=["f0_nm1"])
            p.op("scalar", lambda e, pg=pg: e.activation(out=ET[:, :], in_=pg, func=AF.Exp, bias=nm1),
                 reads=[pgk, "f0_nm1"], writes=["f0_et"])
            p.op("vector", lambda e: e.reduce_max(ex1, ET[:, :], axis=AX.X),
                 reads=["f0_et"], writes=["f0_ex1"])
            p.op("vector", lambda e: e.tensor_scalar(out=EQ[:, :], in0=ET[:, :], scalar1=ex1, scalar2=None,
                                                     op0=ALU.is_ge),
                 reads=["f0_et", "f0_ex1"], writes=["f0_eq"])
            p.op("vector", lambda e: e.scalar_tensor_tensor(out=EMK[:, :], in0=EQ[:, :], scalar=-2.0, in1=ET[:, :],
                                                            op0=ALU.mult, op1=ALU.add),
                 reads=["f0_eq", "f0_et"], writes=["f0_emk"])
            p.op("vector", lambda e: e.reduce_max(em2, EMK[:, :], axis=AX.X),
                 reads=["f0_emk"], writes=["f0_em2"])
            p.op("vector", lambda e: e.tensor_tensor(out=den, in0=ex1, in1=em2, op=ALU.add),
                 reads=["f0_ex1", "f0_em2"], writes=["f0_den"])
            p.op("vector", lambda e: e.reciprocal(rden, den), reads=["f0_den"], writes=["f0_rden"])
            p.op("vector", lambda e: e.tensor_scalar(out=KP[:, :], in0=ET[:, :], scalar1=em2, scalar2=None,
                                                     op0=ALU.is_ge),
                 reads=["f0_et", "f0_em2"], writes=["f0_kp"])
            p.op("vector", lambda e: e.tensor_mul(KP[:, :], KP[:, :], ET[:, :]),
                 reads=["f0_kp", "f0_et"], writes=["f0_kp"])
            p.op("vector", lambda e, i=i: e.tensor_scalar(out=COMB[:, i * 8:(i + 1) * 8], in0=KP[:, :],
                                                          scalar1=rden, scalar2=None, op0=ALU.mult),
                 reads=["f0_kp", "f0_rden"], writes=[f"f0_comb{i}"])
        for ex in range(8):
            for c in range(NC):
                ctx.dma(qw, W1[ex, c * 128:(c + 1) * 128, :], W1S[c][:, :], f"f0_w1_{c}")
            for h in range(nH):
                ctx.dma(qw, W2[ex, h * 128:(h + 1) * 128, :], W2S[h][:, :], f"f0_w2_{h}")
            ctx.dma(qw, b2[ex:ex + 1, :], B2R[:, :], "f0_b2r")
            for sl in range(4):
                tsl = slice(sl * 512, (sl + 1) * 512)
                H6 = [(0, 0), (1, 0), (3, 0), (0, 1), (1, 1), (3, 1)]
                for h in range(nH):
                    ph, phk = ctx.psk(*H6[h % 6])
                    for c in range(NC):
                        p.op("tensor", lambda e, c=c, h=h, ph=ph, tsl=tsl: e.matmul(
                            ph, W1S[c][:, h * 128:(h + 1) * 128], XN[c][:, tsl],
                            start=(c == 0), stop=(c == NC - 1)),
                            reads=[f"f0_w1_{c}", xnk[c]], writes=[phk])
                    p.op("scalar", lambda e, h=h, ph=ph, ex=ex: e.activation(
                        out=H[h][:, :], in_=ph, func=AF.Relu, bias=b1T[:, ex * 16 + h:ex * 16 + h + 1]),
                        reads=[phk, b1T_k], writes=[f"f0_h{h}"])
                for m in range(4):
                    i = sl * 4 + m
                    py = ctx.PS[2][:, :]
                    pyk = ["PS2_0", "PS2_1"]
                    for h in range(nH):
                        for nhalf in range(2):
                            p.op("tensor", lambda e, h=h, m=m, nhalf=nhalf, py=py: e.matmul(
                                py[:, nhalf * 512:(nhalf + 1) * 512],
                                H[h][:, m * 128:(m + 1) * 128],
                                W2S[h][:, nhalf * 512:(nhalf + 1) * 512],
                                start=(h == 0), stop=False),
                                reads=[f"f0_h{h}", f"f0_w2_{h}"], writes=[pyk[nhalf]])
                    for nhalf in range(2):
                        p.op("tensor", lambda e, nhalf=nhalf, py=py: e.matmul(
                            py[:, nhalf * 512:(nhalf + 1) * 512], ctx.ones_row_bf[0:1, :],
                            B2R[0:1, nhalf * 512:(nhalf + 1) * 512], start=False, stop=True),
                            reads=["ones_row_bf", "f0_b2r"], writes=[pyk[nhalf]])
                    cc = COMB[:, i * 8 + ex:i * 8 + ex + 1]
                    if ex == 0:
                        p.op("vector", lambda e, i=i, py=py, cc=cc: e.tensor_scalar(
                            out=ACC[i][:, :], in0=py, scalar1=cc, scalar2=None, op0=ALU.mult),
                            reads=pyk + [f"f0_comb{i}"], writes=[f"f0_acc{i}"])
                    else:
                        p.op("vector", lambda e, i=i, py=py, cc=cc: e.scalar_tensor_tensor(
                            out=ACC[i][:, :], in0=py, scalar=cc, in1=ACC[i][:, :],
                            op0=ALU.mult, op1=ALU.add),
                            reads=pyk + [f"f0_comb{i}", f"f0_acc{i}"], writes=[f"f0_acc{i}"])
        for i in range(NT):
            j = i % 2
            p.op("scalar", lambda e, i=i, j=j: e.activation(out=GS[j][:, :], in_=ACC[i][:, :], func=AF.Silu),
                 reads=[f"f0_acc{i}"], writes=[f"f0_gs{j}"])
            ctx.store(qs, gout[b, i * 128:(i + 1) * 128, :], GS[j][:, :], [f"f0_gs{j}"])



def emit_attn0(ctx, xn, wq, wk, wv, wo, wqb, wkb, wvb, wob, thr, sharp, sout):
    p, qs, qw = ctx.p, ctx.qs, ctx.qw
    TR = ctx.sb("a0_tr", [1, 16], F32)
    SR = ctx.sb("a0_sr", [1, 16], F32)
    TV = ctx.sb("a0_tv", [1, 16], F32)
    NSR = ctx.sb("a0_nsr", [1, 16], F32)
    SHARPC = ctx.sb("a0_shc", [128, 16], F32)
    NSTC = ctx.sb("a0_nst", [128, 16], F32)
    ctx.dma(qs, thr.rearrange("(o e) -> o e", o=1), TR[:, :], "a0_tr")
    ctx.dma(qs, sharp.rearrange("(o e) -> o e", o=1), SR[:, :], "a0_sr")
    p.op("scalar", lambda e: e.activation(out=TV[:, :], in_=TR[:, :], func=AF.Sigmoid),
         reads=["a0_tr"], writes=["a0_tv"])
    p.op("vector", lambda e: e.tensor_mul(NSR[:, :], SR[:, :], TV[:, :]),
         reads=["a0_sr", "a0_tv"], writes=["a0_nsr"])
    p.op("vector", lambda e: e.tensor_scalar(out=NSR[:, :], in0=NSR[:, :], scalar1=-1.0, scalar2=None,
                                             op0=ALU.mult),
         reads=["a0_nsr"], writes=["a0_nsr"])
    pb = ctx.PS[3][:, 0:16]
    p.op("tensor", lambda e: e.matmul(pb[:, 0:16], ctx.ones_row_f32[0:1, :], SR[0:1, :],
                                      start=True, stop=True),
         reads=["ones_row_f32", "a0_sr"], writes=["PS3_0"])
    p.op("vector", lambda e: e.tensor_copy(out=SHARPC[:, :], in_=pb[:, 0:16]),
         reads=["PS3_0"], writes=["a0_shc"])
    p.op("tensor", lambda e: e.matmul(pb[:, 0:16], ctx.ones_row_f32[0:1, :], NSR[0:1, :],
                                      start=True, stop=True),
         reads=["ones_row_f32", "a0_nsr"], writes=["PS3_0"])
    p.op("vector", lambda e: e.tensor_copy(out=NSTC[:, :], in_=pb[:, 0:16]),
         reads=["PS3_0"], writes=["a0_nst"])
    qbc, qbc_k = bias_cols_prep(ctx, wqb.rearrange("(n p) -> n p", p=128), 8, "a0qb")
    kbc, kbc_k = bias_cols_prep(ctx, wkb.rearrange("(n p) -> n p", p=128), 8, "a0kb")
    vbr, vbr_k = bias_row(ctx, wvb.rearrange("(o e) -> o e", o=1), D, "a0vb")
    obr, obr_k = bias_row(ctx, wob.rearrange("(o e) -> o e", o=1), D, "a0ob")
    WB = [ctx.sb(f"a0_wb{c}", [128, D], BF16) for c in range(NC)]
    QF = [ctx.sb(f"a0_qf{c}", [128, T], BF16) for c in range(NC)]
    KF = [ctx.sb(f"a0_kf{c}", [128, T], BF16) for c in range(NC)]
    VT = [ctx.sb(f"a0_vt{i}", [128, D], BF16) for i in range(NT)]
    AOF = [ctx.sb(f"a0_ao{c}", [128, T], BF16) for c in range(NC)]
    RZ = ctx.sb("a0_rz", [1, 1024], F32)
    RZB = ctx.sb("a0_rzb", [1, 1024], BF16)
    RBS = ctx.sb("a0_rbs", [128, 1024], BF16)
    GB = [ctx.sb(f"a0_gb{i}", [128, 1024], BF16) for i in range(4)]
    SO = [ctx.sb(f"a0_so{i}", [128, D], F32) for i in range(2)]

    xnl = XnLoader(ctx, xn, "a0xn")
    XN, xnk = xnl.t, xnl.k
    PT = [XN[kt // 2][:, (kt % 2) * 1024:(kt % 2) * 1024 + 1024] for kt in range(NT)]
    PTK = [xnk[kt // 2] for kt in range(NT)]

    def load_w(wdram, tag):
        for c in range(NC):
            ctx.dma(qw, wdram[c * 128:(c + 1) * 128, :], WB[c][:, :], f"a0_wb{c}")

    sc125, sc125k = ctx.cc(0.125)
    for b in range(NB):
        xnl.load(b)
        for (dst, dk, bias_cols, biask, wdram) in ((QF, "a0_qf", qbc, qbc_k, wq),
                                                   (KF, "a0_kf", kbc, kbc_k, wk)):
            load_w(wdram, dk)
            for dc in range(NC):
                for sl in range(4):
                    pp, ppk = ctx.psk8(dc * 4 + sl)
                    tsl = slice(sl * 512, (sl + 1) * 512)
                    for c in range(NC):
                        p.op("tensor", lambda e, c=c, dc=dc, pp=pp, tsl=tsl: e.matmul(
                            pp, WB[c][:, dc * 128:(dc + 1) * 128], XN[c][:, tsl],
                            start=(c == 0), stop=(c == NC - 1)),
                            reads=[f"a0_wb{c}", xnk[c]], writes=[ppk])
                    p.op("scalar", lambda e, dst=dst, dc=dc, pp=pp, tsl=tsl, bias_cols=bias_cols: e.activation(
                        out=dst[dc][:, tsl], in_=pp, func=AF.Identity,
                        bias=bias_cols[:, dc:dc + 1]),
                        reads=[ppk, biask], writes=[f"{dk}{dc}"])
        load_w(wv, "a0_vt")
        for i in range(NT):
            pp = ctx.PS[i % 4][:, :]
            ppk = [f"PS{i % 4}_0", f"PS{i % 4}_1"]
            for nh in range(2):
                nsl = slice(nh * 512, (nh + 1) * 512)
                for c in range(NC):
                    p.op("tensor", lambda e, c=c, i=i, pp=pp, nsl=nsl: e.matmul(
                        pp[:, nsl], XN[c][:, i * 128:(i + 1) * 128], WB[c][:, nsl],
                        start=(c == 0), stop=False),
                        reads=[xnk[c], f"a0_wb{c}"], writes=[ppk[nh]])
                p.op("tensor", lambda e, nh=nh, pp=pp, nsl=nsl: e.matmul(
                    pp[:, nsl], ctx.ones_row_bf[0:1, :],
                    vbr[0:1, nsl], start=False, stop=True),
                    reads=["ones_row_bf", vbr_k], writes=[ppk[nh]])
            p.op("vector", lambda e, i=i, pp=pp: e.tensor_copy(out=VT[i][:, :], in_=pp),
                 reads=ppk, writes=[f"a0_vt{i}"])
        def emit_st_exp(h, qh, kt):
            hc, hp = h // 2, (h % 2) * 64
            pp = ctx.PS[kt % 2][:, :]
            ppk = [f"PS{kt % 2}_0", f"PS{kt % 2}_1"]
            for nh in range(2):
                p.op("tensor", lambda e, kt=kt, nh=nh, pp=pp, hc=hc, hp=hp, qh=qh: e.matmul(
                    pp[:, nh * 512:(nh + 1) * 512],
                    KF[hc][hp:hp + 64, kt * 128:(kt + 1) * 128],
                    QF[hc][hp:hp + 64, qh * 1024 + nh * 512:qh * 1024 + (nh + 1) * 512],
                    start=True, stop=True),
                    reads=[f"a0_kf{hc}", f"a0_qf{hc}"], writes=[ppk[nh]])
            p.op("scalar", lambda e, kt=kt, pp=pp: e.activation(
                out=PT[kt], in_=pp, func=AF.Exp, scale=sc125),
                reads=ppk + [sc125k], writes=[PTK[kt]])

        def emit_z(kt):
            for nh in range(2):
                p.op("tensor", lambda e, kt=kt, nh=nh: e.matmul(
                    ctx.PS[2][0:1, nh * 512:(nh + 1) * 512],
                    ctx.ones_col_bf[:, 0:1], PT[kt][:, nh * 512:(nh + 1) * 512],
                    start=(kt == 0), stop=(kt == NT - 1)),
                    reads=["ones_col_bf", PTK[kt]], writes=[f"PS2_{nh}"])

        def emit_ph2():
            p.op("vector", lambda e: e.reciprocal(RZ[0:1, :], ctx.PS[2][0:1, :]),
                 reads=["PS2_0", "PS2_1"], writes=["a0_rz"])
            p.op("vector", lambda e: e.tensor_copy(out=RZB[0:1, :], in_=RZ[0:1, :]),
                 reads=["a0_rz"], writes=["a0_rzb"])
            for nh in range(2):
                p.op("tensor", lambda e, nh=nh: e.matmul(
                    ctx.PS[2][:, nh * 512:(nh + 1) * 512], ctx.ones_row_bf[0:1, :],
                    RZB[0:1, nh * 512:(nh + 1) * 512], start=True, stop=True),
                    reads=["ones_row_bf", "a0_rzb"], writes=[f"PS2_{nh}"])
            p.op("scalar", lambda e: e.activation(out=RBS[:, :], in_=ctx.PS[2][:, :], func=AF.Copy),
                 reads=["PS2_0", "PS2_1"], writes=["a0_rbs"])

        def emit_multA(kt):
            p.op("vector", lambda e, kt=kt: e.tensor_mul(PT[kt], PT[kt], RBS[:, :]),
                 reads=[PTK[kt], "a0_rbs"], writes=[PTK[kt]])

        def emit_ph3_step(h, kt):
            jg = kt % 4
            p.op("scalar", lambda e, kt=kt, jg=jg, h=h: e.activation(
                out=GB[jg][:, :], in_=PT[kt], func=AF.Sigmoid,
                bias=NSTC[:, h:h + 1], scale=SHARPC[:, h:h + 1]),
                reads=[PTK[kt], "a0_nst", "a0_shc"], writes=[f"a0_gb{jg}"])
            if kt + 3 < NT:
                emit_multA(kt + 3)
            p.op("vector", lambda e, kt=kt, jg=jg: e.tensor_mul(PT[kt], PT[kt], GB[jg][:, :]),
                 reads=[PTK[kt], f"a0_gb{jg}"], writes=[PTK[kt]])
            for nh in range(2):
                p.op("tensor", lambda e, kt=kt, nh=nh, h=h: e.matmul(
                    ctx.PS[3][0:64, nh * 512:(nh + 1) * 512],
                    VT[kt][:, h * 64:(h + 1) * 64],
                    PT[kt][:, nh * 512:(nh + 1) * 512],
                    start=(kt == 0), stop=(kt == NT - 1)),
                    reads=[PTK[kt], f"a0_vt{kt}"], writes=[f"PS3_{nh}"])

        iters = [(h, qh) for h in range(16) for qh in range(2)]
        for idx, (h, qh) in enumerate(iters):
            hc, hp = h // 2, (h % 2) * 64
            qsl = slice(qh * 1024, (qh + 1) * 1024)
            for kt in range(NT):
                emit_st_exp(h, qh, kt)
                if kt > 0:
                    emit_z(kt - 1)
            emit_z(NT - 1)
            emit_ph2()
            emit_multA(0)
            emit_multA(1)
            emit_multA(2)
            for kt in range(NT):
                emit_ph3_step(h, kt)
            p.op("scalar", lambda e, hc=hc, hp=hp, qsl=qsl: e.activation(
                out=AOF[hc][hp:hp + 64, qsl], in_=ctx.PS[3][0:64, :], func=AF.Copy),
                reads=["PS3_0", "PS3_1"], writes=[f"a0_ao{hc}"])
        load_w(wo, "a0_wo")
        for i in range(NT):
            pp = ctx.PS[i % 4][:, :]
            ppk = [f"PS{i % 4}_0", f"PS{i % 4}_1"]
            for nh in range(2):
                nsl = slice(nh * 512, (nh + 1) * 512)
                for c in range(NC):
                    p.op("tensor", lambda e, c=c, i=i, pp=pp, nsl=nsl: e.matmul(
                        pp[:, nsl], AOF[c][:, i * 128:(i + 1) * 128], WB[c][:, nsl],
                        start=(c == 0), stop=False),
                        reads=[f"a0_ao{c}", f"a0_wb{c}"], writes=[ppk[nh]])
                p.op("tensor", lambda e, pp=pp, nsl=nsl: e.matmul(
                    pp[:, nsl], ctx.ones_row_bf[0:1, :],
                    obr[0:1, nsl], start=False, stop=True),
                    reads=["ones_row_bf", obr_k], writes=[ppk[nh]])
            j = i % 2
            p.op("vector", lambda e, j=j, pp=pp: e.tensor_copy(out=SO[j][:, :], in_=pp),
                 reads=ppk, writes=[f"a0_so{j}"])
            ctx.store(qs, sout[b, i * 128:(i + 1) * 128, :], SO[j][:, :], [f"a0_so{j}"])



def emit_gla(ctx, xn, Wq, bq, Wk, bk, Wv, bv, Wg, bg, Wo, bo, sout, debug=None):
    p, qs, qw = ctx.p, ctx.qs, ctx.qw
    CH = 64
    NCH = T // CH
    K2 = 2
    WQS = [ctx.sb(f"g1_wq{c}", [128, 256], BF16) for c in range(NC)]
    WKS = [ctx.sb(f"g1_wk{c}", [128, 256], BF16) for c in range(NC)]
    WGS = [ctx.sb(f"g1_wg{c}", [128, 256], BF16) for c in range(NC)]
    WVS = [ctx.sb(f"g1_wv{c}", [128, 256], BF16) for c in range(NC)]
    for c in range(NC):
        sl = slice(c * 128, (c + 1) * 128)
        ctx.dma(qw, Wq[sl, :], WQS[c][:, :], f"g1_wq{c}")
        ctx.dma(qw, Wk[sl, :], WKS[c][:, :], f"g1_wk{c}")
        ctx.dma(qw, Wg[sl, :], WGS[c][:, :], f"g1_wg{c}")
        ctx.dma(qw, Wv[sl, :], WVS[c][:, :], f"g1_wv{c}")
    WOS = [ctx.sb(f"g1_wo{v}", [128, 1024], BF16) for v in range(K2)]
    for v in range(K2):
        ctx.dma(qw, Wo[v * 128:(v + 1) * 128, :], WOS[v][:, :], f"g1_wo{v}")
    bqc, bqc_k = bias_cols_prep(ctx, bq.rearrange("(n p) -> n p", p=128), 2, "g1bq")
    bkc, bkc_k = bias_cols_prep(ctx, bk.rearrange("(n p) -> n p", p=128), 2, "g1bk")
    bgc, bgc_k = bias_cols_prep(ctx, bg.rearrange("(n p) -> n p", p=128), 2, "g1bg")
    bvr, bvr_k = bias_row(ctx, bv.rearrange("(o e) -> o e", o=1), 256, "g1bv")
    bor, bor_k = bias_row(ctx, bo.rearrange("(o e) -> o e", o=1), D, "g1bo")
    QFm = [ctx.sb(f"g1_q{kc}", [128, T], BF16) for kc in range(K2)]
    KS = [ctx.sb(f"g1_ks{kc}", [128, T], BF16) for kc in range(K2)]
    GSIG = [ctx.sb(f"g1_gs{kc}", [128, T], F32) for kc in range(K2)]
    LG = [ctx.sb(f"g1_lg{kc}", [128, T], F32) for kc in range(K2)]
    VT = [ctx.sb(f"g1_vt{i}", [64, 256], BF16) for i in range(2 * NT)]
    OF = [ctx.sb(f"g1_of{vc}", [128, T], BF16) for vc in range(K2)]
    S = [ctx.sb(f"g1_s{kc}", [128, 256], F32) for kc in range(K2)]
    SB = [ctx.sb(f"g1_sb{kc}", [128, 256], BF16) for kc in range(K2)]
    Z512 = ctx.sb("g1_z512", [128, 512], F32)
    QT = [ctx.sb(f"g1_qt{kc}", [128, CH], BF16) for kc in range(2 * K2)]
    F2 = [ctx.sb(f"g1_f2{kc}", [128, CH], BF16) for kc in range(2 * K2)]
    KT = [ctx.sb(f"g1_kt{kc}", [128, CH], BF16) for kc in range(2 * K2)]
    KTT = [ctx.sb(f"g1_ktt{i}", [64, 256], BF16) for i in range(2)]
    ATm = ctx.sb("g1_at", [64, 64], BF16)
    EPOS = [ctx.sb(f"g1_ep{kc}", [128, CH], F32) for kc in range(K2)]
    ENEG = [ctx.sb(f"g1_en{kc}", [128, CH], F32) for kc in range(K2)]
    COLS = ctx.sb("g1_cols", [128, 16], F32)
    SO = [ctx.sb(f"g1_so{i}", [128, D], F32) for i in range(2)]
    p.op("vector", lambda e: e.memset(Z512[:, :], 0.0), writes=["g1_z512"])

    xnl = XnLoader(ctx, xn, "g1xn")
    XN, xnk = xnl.t, xnl.k
    neg1, neg1k = ctx.cc(-1.0)

    for b in range(NB):
        xnl.load(b)
        for kc in range(K2):
            p.op("vector", lambda e, kc=kc: e.memset(S[kc][:, :], 0.0), writes=[f"g1_s{kc}"])
        pidx = 0
        for kc in range(K2):
            for sl4 in range(4):
                tsl = slice(sl4 * 512, (sl4 + 1) * 512)
                for (WS, wsk, dstf, dstk, func, bc, bck) in (
                        (WQS, "g1_wq", QFm, "g1_q", AF.Identity, bqc, bqc_k),
                        (WKS, "g1_wk", KS, "g1_ks", AF.Sigmoid, bkc, bkc_k),
                        (WGS, "g1_wg", GSIG, "g1_gs", AF.Sigmoid, bgc, bgc_k)):
                    pp, ppk = ctx.psk8(pidx); pidx += 1
                    for c in range(NC):
                        p.op("tensor", lambda e, c=c, kc=kc, pp=pp, tsl=tsl, WS=WS, wsk=wsk: e.matmul(
                            pp, WS[c][:, kc * 128:(kc + 1) * 128], XN[c][:, tsl],
                            start=(c == 0), stop=(c == NC - 1)),
                            reads=[f"{wsk}{c}", xnk[c]], writes=[ppk])
                    p.op("scalar", lambda e, kc=kc, pp=pp, tsl=tsl, dstf=dstf, func=func, bc=bc: e.activation(
                        out=dstf[kc][:, tsl], in_=pp, func=func, bias=bc[:, kc:kc + 1]),
                        reads=[ppk, bck], writes=[f"{dstk}{kc}"])
        for kc in range(K2):
            p.op("scalar", lambda e, kc=kc: e.activation(out=GSIG[kc][:, :], in_=GSIG[kc][:, :], func=AF.Ln),
                 reads=[f"g1_gs{kc}"], writes=[f"g1_gs{kc}"])
        for i in range(2 * NT):
            pp, ppk = ctx.psk8(i)
            pv = pp[0:64, 0:256]
            for c in range(NC):
                p.op("tensor", lambda e, c=c, i=i, pv=pv: e.matmul(
                    pv, XN[c][:, i * 64:(i + 1) * 64], WVS[c][:, :],
                    start=(c == 0), stop=False),
                    reads=[xnk[c], f"g1_wv{c}"], writes=[ppk])
            p.op("tensor", lambda e, pv=pv: e.matmul(pv, ctx.ones_row_bf[0:1, 0:64], bvr[0:1, :],
                                                     start=False, stop=True),
                 reads=["ones_row_bf", bvr_k], writes=[ppk])
            p.op("vector", lambda e, i=i, pv=pv: e.tensor_copy(out=VT[i][:, :], in_=pv),
                 reads=[ppk], writes=[f"g1_vt{i}"])
        for kc in range(K2):
            for blk in range(4):
                tsl = slice(blk * 512, (blk + 1) * 512)
                init = 0.0 if blk == 0 else LG[kc][:, blk * 512 - 1:blk * 512]
                p.op("vector", lambda e, kc=kc, tsl=tsl, init=init: e.tensor_tensor_scan(
                    out=LG[kc][:, tsl], data0=GSIG[kc][:, tsl], data1=Z512[:, :],
                    initial=init, op0=ALU.add, op1=ALU.add),
                    reads=[f"g1_gs{kc}", "g1_z512", f"g1_lg{kc}"], writes=[f"g1_lg{kc}"])
        def emit_elem(c):
            par = c % 2
            t0 = c * CH
            cols = slice(t0, t0 + CH)
            for kc in range(K2):
                s2 = 2 * par + kc
                lg0 = ctx.zero_col[:, 0:1] if c == 0 else LG[kc][:, t0 - 1:t0]
                lg0k = "zero_col" if c == 0 else f"g1_lg{kc}"
                nl0 = COLS[:, 8 * par + kc:8 * par + kc + 1]
                ecol = COLS[:, 8 * par + 2 + kc:8 * par + 3 + kc]
                dcol = COLS[:, 8 * par + 4 + kc:8 * par + 5 + kc]
                p.op("vector", lambda e, nl0=nl0, lg0=lg0: e.tensor_scalar(
                    out=nl0, in0=lg0, scalar1=-1.0, scalar2=None, op0=ALU.mult),
                    reads=[lg0k], writes=[f"g1_nl0{par}{kc}"])
                p.op("scalar", lambda e, kc=kc, cols=cols, nl0=nl0: e.activation(
                    out=EPOS[kc][:, :], in_=LG[kc][:, cols], func=AF.Exp, bias=nl0),
                    reads=[f"g1_lg{kc}", f"g1_nl0{par}{kc}"], writes=[f"g1_ep{kc}"])
                p.op("scalar", lambda e, kc=kc, cols=cols, lg0=lg0: e.activation(
                    out=ENEG[kc][:, :], in_=LG[kc][:, cols], func=AF.Exp, bias=lg0, scale=neg1),
                    reads=[f"g1_lg{kc}", lg0k, neg1k], writes=[f"g1_en{kc}"])
                p.op("vector", lambda e, kc=kc, t0=t0, dcol=dcol, lg0=lg0: e.tensor_tensor(
                    out=dcol, in0=LG[kc][:, t0 + CH - 1:t0 + CH], in1=lg0, op=ALU.subtract),
                    reads=[f"g1_lg{kc}", lg0k], writes=[f"g1_dc{par}{kc}"])
                p.op("scalar", lambda e, ecol=ecol, dcol=dcol: e.activation(
                    out=ecol, in_=dcol, func=AF.Exp),
                    reads=[f"g1_dc{par}{kc}"], writes=[f"g1_ec{par}{kc}"])
                p.op("vector", lambda e, kc=kc, s2=s2, cols=cols: e.tensor_mul(
                    QT[s2][:, :], QFm[kc][:, cols], EPOS[kc][:, :]),
                    reads=[f"g1_q{kc}", f"g1_ep{kc}"], writes=[f"g1_qt{s2}"])
                p.op("vector", lambda e, kc=kc, s2=s2, cols=cols: e.tensor_mul(
                    F2[s2][:, :], KS[kc][:, cols], ENEG[kc][:, :]),
                    reads=[f"g1_ks{kc}", f"g1_en{kc}"], writes=[f"g1_f2{s2}"])
                p.op("vector", lambda e, s2=s2, ecol=ecol: e.tensor_scalar(
                    out=KT[s2][:, :], in0=F2[s2][:, :], scalar1=ecol, scalar2=None, op0=ALU.mult),
                    reads=[f"g1_f2{s2}", f"g1_ec{par}{kc}"], writes=[f"g1_kt{s2}"])

        def emit_heavy(c):
            par = c % 2
            t0 = c * CH
            cols = slice(t0, t0 + CH)
            vt_t = VT[c]
            vkey = f"g1_vt{c}"
            for kc in range(K2):
                s2 = 2 * par + kc
                ptr = ctx.PS[1].bitcast(BF16)[0:64, kc * 1024:kc * 1024 + 128]
                ptk = f"PS1_{kc}"
                p.op("tensor", lambda e, s2=s2, ptr=ptr: e.transpose(
                    ptr, KT[s2][:, :], ctx.ident_bf[:, :]),
                    reads=[f"g1_kt{s2}", "ident_bf"], writes=[ptk])
                p.op("vector", lambda e, kc=kc, par=par, ptr=ptr: e.tensor_copy(
                    out=KTT[par][0:64, kc * 128:(kc + 1) * 128], in_=ptr),
                    reads=[ptk], writes=[f"g1_ktt{par}"])
            pa = ctx.PS[3][0:64, 0:64]
            for kc in range(K2):
                p.op("tensor", lambda e, s2=2 * par + kc, kc=kc, pa=pa: e.matmul(
                    pa, F2[s2][:, :], QT[s2][:, :], start=(kc == 0), stop=(kc == K2 - 1)),
                    reads=[f"g1_f2{2 * par + kc}", f"g1_qt{2 * par + kc}"], writes=["PS3_0"])
            p.op("vector", lambda e, pa=pa: e.tensor_mul(ATm[0:64, :], pa, ctx.mask64[:, :]),
                 reads=["PS3_0", "mask64"], writes=["g1_at"])
            for kc in range(K2):
                pd = ctx.PS[0][:, kc * 512:kc * 512 + 256]
                pdk = f"PS0_{kc}"
                p.op("tensor", lambda e, kc=kc, par=par, pd=pd, vt_t=vt_t: e.matmul(
                    pd, KTT[par][0:64, kc * 128:(kc + 1) * 128], vt_t[0:64, :],
                    start=True, stop=True),
                    reads=[f"g1_ktt{par}", vkey], writes=[pdk])
            for vc in range(K2):
                po = ctx.PS[2][:, vc * 512:vc * 512 + 64]
                pok = f"PS2_{vc}"
                p.op("tensor", lambda e, vc=vc, po=po, vt_t=vt_t, c=c: e.matmul(
                    po, vt_t[0:64, vc * 128:(vc + 1) * 128], ATm[0:64, :],
                    start=True, stop=(c == 0)),
                    reads=[vkey, "g1_at"], writes=[pok])
                if c > 0:
                    for kc in range(K2):
                        p.op("tensor", lambda e, vc=vc, kc=kc, s2=2 * par + kc, po=po: e.matmul(
                            po, SB[kc][:, vc * 128:(vc + 1) * 128], QT[s2][:, :],
                            start=False, stop=(kc == K2 - 1)),
                            reads=[f"g1_sb{kc}", f"g1_qt{2 * par + kc}"], writes=[pok])
                p.op("scalar", lambda e, vc=vc, po=po, cols=cols: e.activation(
                    out=OF[vc][:, cols], in_=po, func=AF.Copy),
                    reads=[pok], writes=[f"g1_of{vc}"])
            for kc in range(K2):
                pd = ctx.PS[0][:, kc * 512:kc * 512 + 256]
                pdk = f"PS0_{kc}"
                ecol = COLS[:, 8 * par + 2 + kc:8 * par + 3 + kc]
                p.op("vector", lambda e, kc=kc, pd=pd, ecol=ecol: e.scalar_tensor_tensor(
                    out=S[kc][:, :], in0=S[kc][:, :], scalar=ecol, in1=pd,
                    op0=ALU.mult, op1=ALU.add),
                    reads=[f"g1_s{kc}", f"g1_ec{par}{kc}", pdk], writes=[f"g1_s{kc}"])
                p.op("vector", lambda e, kc=kc: e.tensor_copy(out=SB[kc][:, :], in_=S[kc][:, :]),
                     reads=[f"g1_s{kc}"], writes=[f"g1_sb{kc}"])

        emit_elem(0)
        for c in range(NCH):
            if c + 1 < NCH:
                emit_elem(c + 1)
            emit_heavy(c)
            if debug is not None and b == 0 and c < 4:
                for kc in range(K2):
                    ctx.store(qs, debug["S"][c, kc], S[kc][:, :], [f"g1_s{kc}"])
                    ctx.store(qs, debug["QT"][c, kc], QT[kc][:, :], [f"g1_qt{kc}"])
                    ctx.store(qs, debug["F2"][c, kc], F2[kc][:, :], [f"g1_f2{kc}"])
                    ctx.store(qs, debug["KTq"][c, kc], KT[kc][:, :], [f"g1_kt{kc}"])
                allcols = [f"g1_{pfx}{kc}" for pfx in ("nl0", "ec", "dc") for kc in range(K2)]
                ctx.store(qs, debug["COLS"][c], COLS[:, :], allcols)
                ctx.store(qs, debug["AT"][c], ATm[0:64, :], ["g1_at"])
        for i in range(NT):
            pp = ctx.PS[1][:, :]
            ppk = ["PS1_0", "PS1_1"]
            for nh in range(2):
                nsl = slice(nh * 512, (nh + 1) * 512)
                for vc in range(K2):
                    p.op("tensor", lambda e, vc=vc, i=i, pp=pp, nsl=nsl: e.matmul(
                        pp[:, nsl], OF[vc][:, i * 128:(i + 1) * 128], WOS[vc][:, nsl],
                        start=(vc == 0), stop=False),
                        reads=[f"g1_of{vc}", f"g1_wo{vc}"], writes=[ppk[nh]])
                p.op("tensor", lambda e, pp=pp, nsl=nsl: e.matmul(
                    pp[:, nsl], ctx.ones_row_bf[0:1, :], bor[0:1, nsl], start=False, stop=True),
                    reads=["ones_row_bf", bor_k], writes=[ppk[nh]])
            j = i % 2
            p.op("vector", lambda e, j=j, pp=pp: e.tensor_copy(out=SO[j][:, :], in_=pp),
                 reads=ppk, writes=[f"g1_so{j}"])
            ctx.store(qs, sout[b, i * 128:(i + 1) * 128, :], SO[j][:, :], [f"g1_so{j}"])
        if debug is not None:
            for kc in range(K2):
                ctx.store(qs, debug["LG"][b, kc], LG[kc][:, :], [f"g1_lg{kc}"])
            for vc in range(K2):
                ctx.store(qs, debug["OF"][b, vc], OF[vc][:, :], [f"g1_of{vc}"])



def emit_ffn1(ctx, xn, xn32, wi, bi, wo, bo, lora_A, lora_B, gate_w, gate_b,
              base_sp, u_sp, loraBT, gout):
    p, qs, qw = ctx.p, ctx.qs, ctx.qw
    NF = 32
    W64 = [ctx.sb(f"f1_w{dc}", [128, 4096], BF16) for dc in range(NC)]
    wk64 = [f"f1_w{dc}" for dc in range(NC)]
    WSTG = [ctx.sb(f"f1_stg{i}", [128, 1024], BF16) for i in range(2)]
    ATall = ctx.sb("f1_at", [128, 1024], BF16)
    A16 = ctx.sb("f1_a16", [16, 1024], BF16)
    B128 = ctx.sb("f1_b128", [128, 64], BF16)
    BTW = ctx.sb("f1_btw", [16, 4096], BF16)
    GWT = ctx.sb("f1_gwt", [128, 64], F32)
    GW8 = ctx.sb("f1_gw8", [8, 1024], F32)
    gbr, gbr_k = bias_row(ctx, gate_b.rearrange("(o e) -> o e", o=1), 8, "f1gb", dtype=F32)
    biT, biT_k = bias_cols_prep(ctx, bi.rearrange("(n p) -> n p", p=128), 32, "f1bi")
    bor, bor_k = bias_row(ctx, bo.rearrange("(o e) -> o e", o=1), D, "f1bo")
    XT32 = [ctx.sb(f"f1_xt{i}", [128, D], F32) for i in range(2)]
    XF32 = [ctx.sb(f"f1_xf{c}", [128, 128], F32) for c in range(NC)]
    GATES = ctx.sb("f1_gates", [128, 8 * NT * NB], F32)
    SC = ctx.sb("f1_sc", [128, 8], F32)
    m1c, em2, den, rden, nm1, ex1, zc, gv = [SC[:, j:j + 1] for j in range(8)]
    ET = ctx.sb("f1_et", [128, 8], F32)
    EQ = ctx.sb("f1_eq", [128, 8], F32)
    EMK = ctx.sb("f1_emk", [128, 8], F32)
    KP = ctx.sb("f1_kp", [128, 8], F32)
    BB = ctx.sb("f1_bb", [128, T], BF16)
    UTB = ctx.sb("f1_utb", [16, T], BF16)
    BSL = ctx.sb("f1_bsl", [128, 8192], BF16)
    UTe = ctx.sb("f1_ute", [16, 256], BF16)
    HF = [ctx.sb(f"f1_hf{i}", [128, 256], F32) for i in range(2)]
    HB = [ctx.sb(f"f1_hb{i}", [128, 256], BF16) for i in range(4)]
    ACC2 = [ctx.sb(f"f1_acc{i}", [128, D], F32) for i in range(2)]
    GBUF = [ctx.sb(f"f1_gb{i}", [128, D], F32) for i in range(2)]

    xnl = XnLoader(ctx, xn, "f1xn")
    XN, xnk = xnl.t, xnl.k

    def transpose_128(src_ap, src_key, dst_ap, dst_key, pslot):
        bank = pslot % 2
        pt = ctx.PS[3].bitcast(BF16)[:, bank * 1024:bank * 1024 + 128]
        ptk = f"PS3_{bank}"
        p.op("tensor", lambda e: e.transpose(pt, src_ap, ctx.ident_bf[:, :]),
             reads=[src_key, "ident_bf"], writes=[ptk])
        p.op("vector", lambda e: e.tensor_copy(out=dst_ap, in_=pt),
             reads=[ptk], writes=[dst_key])

    for fc in range(NF):
        j = fc % 2
        ctx.dma(qw, wi[fc * 128:(fc + 1) * 128, :], WSTG[j][:, :], f"f1_stg{j}")
        for dc in range(NC):
            transpose_128(WSTG[j][:, dc * 128:(dc + 1) * 128], f"f1_stg{j}",
                          W64[dc][:, fc * 128:(fc + 1) * 128], wk64[dc], fc % 8)
    for ex in range(8):
        ctx.dma(qw, lora_A[ex], A16[:, :], "f1_a16")
        for dc in range(NC):
            bank = dc % 2
            pt = ctx.PS[3].bitcast(BF16)[:, bank * 1024:bank * 1024 + 16]
            ptk = f"PS3_{bank}"
            p.op("tensor", lambda e, dc=dc, pt=pt: e.transpose(
                pt[0:128, 0:16], A16[0:16, dc * 128:(dc + 1) * 128], ctx.ident_bf[0:16, 0:16]),
                reads=["f1_a16", "ident_bf"], writes=[ptk])
            p.op("vector", lambda e, ex=ex, dc=dc, pt=pt: e.tensor_copy(
                out=ATall[:, (ex * 8 + dc) * 16:(ex * 8 + dc) * 16 + 16], in_=pt[0:128, 0:16]),
                reads=[ptk], writes=["f1_at"])
    for ex in range(8):
        for half in range(8):
            ctx.dma(qw, lora_B[ex, half * 512:(half + 1) * 512, :].rearrange("(a p) r -> p a r", p=128),
                    B128.rearrange("p (a r) -> p a r", r=16), "f1_b128")
            for a in range(4):
                bank = a % 2
                pt = ctx.PS[3].bitcast(BF16)[0:16, bank * 1024:bank * 1024 + 128]
                ptk = f"PS3_{bank}"
                p.op("tensor", lambda e, a=a, pt=pt: e.transpose(
                    pt, B128[:, a * 16:(a + 1) * 16], ctx.ident_bf[:, :]),
                    reads=["f1_b128", "ident_bf"], writes=[ptk])
                p.op("vector", lambda e, a=a, pt=pt: e.tensor_copy(
                    out=BTW[0:16, a * 128:(a + 1) * 128], in_=pt),
                    reads=[ptk], writes=["f1_btw"])
            ctx.store(qs, loraBT[ex, :, half * 512:(half + 1) * 512],
                      BTW[0:16, 0:512], ["f1_btw"], writes=[f"f1bt{ex}_{half}"])
    ctx.dma(qs, gate_w, GW8[:, :], "f1_gw8")
    for dc in range(NC):
        pt = ctx.PS[3][:, 512:512 + 8]
        p.op("tensor", lambda e, dc=dc, pt=pt: e.transpose(
            pt[0:128, 0:8], GW8[0:8, dc * 128:(dc + 1) * 128], ctx.ident_f32[0:8, 0:8]),
            reads=["f1_gw8", "ident_f32"], writes=["PS3_1"])
        p.op("vector", lambda e, dc=dc, pt=pt: e.tensor_copy(
            out=GWT[:, dc * 8:dc * 8 + 8], in_=pt[0:128, 0:8]),
            reads=["PS3_1"], writes=["f1_gwt"])

    for b in range(NB):
        xnl.load(b)
        for i in range(NT):
            xti = i % 2
            xtk = f"f1_xt{xti}"
            ctx.dma(qs, xn32[b, i * 128:(i + 1) * 128, :], XT32[xti][:, :], xtk)
            for c in range(NC):
                pt = ctx.PS[3][:, 512 + (c % 4) * 128:512 + (c % 4) * 128 + 128]
                p.op("tensor", lambda e, c=c, xti=xti, pt=pt: e.transpose(
                    pt, XT32[xti][:, c * 128:(c + 1) * 128], ctx.ident_f32[:, :]),
                    reads=[xtk, "ident_f32"], writes=["PS3_1"])
                p.op("vector", lambda e, c=c, pt=pt: e.tensor_copy(out=XF32[c][:, :], in_=pt),
                     reads=["PS3_1"], writes=[f"f1_xf{c}"])
            pg = ctx.PS[3][:, 0:8]
            pgk = "PS3_0"
            for c in range(NC):
                p.op("tensor", lambda e, c=c, pg=pg: e.matmul(
                    pg, XF32[c][:, :], GWT[:, c * 8:c * 8 + 8],
                    start=(c == 0), stop=False),
                    reads=[f"f1_xf{c}", "f1_gwt"], writes=[pgk])
            p.op("tensor", lambda e, pg=pg: e.matmul(pg, ctx.ones_row_f32[0:1, :], gbr[0:1, :],
                                                     start=False, stop=True),
                 reads=["ones_row_f32", gbr_k], writes=[pgk])
            p.op("vector", lambda e, pg=pg: e.reduce_max(m1c, pg, axis=AX.X),
                 reads=[pgk], writes=["f1_m1"])
            p.op("vector", lambda e: e.tensor_scalar(out=nm1, in0=m1c, scalar1=-1.0, scalar2=None, op0=ALU.mult),
                 reads=["f1_m1"], writes=["f1_nm1"])
            p.op("scalar", lambda e, pg=pg: e.activation(out=ET[:, :], in_=pg, func=AF.Exp, bias=nm1,
                                                         accum_out=zc),
                 reads=[pgk, "f1_nm1"], writes=["f1_et", "f1_z"])
            p.op("vector", lambda e: e.reduce_max(ex1, ET[:, :], axis=AX.X),
                 reads=["f1_et"], writes=["f1_ex1"])
            p.op("vector", lambda e: e.tensor_scalar(out=EQ[:, :], in0=ET[:, :], scalar1=ex1, scalar2=None,
                                                     op0=ALU.is_ge),
                 reads=["f1_et", "f1_ex1"], writes=["f1_eq"])
            p.op("vector", lambda e: e.scalar_tensor_tensor(out=EMK[:, :], in0=EQ[:, :], scalar=-2.0, in1=ET[:, :],
                                                            op0=ALU.mult, op1=ALU.add),
                 reads=["f1_eq", "f1_et"], writes=["f1_emk"])
            p.op("vector", lambda e: e.reduce_max(em2, EMK[:, :], axis=AX.X),
                 reads=["f1_emk"], writes=["f1_em2"])
            p.op("vector", lambda e: e.tensor_tensor(out=den, in0=ex1, in1=em2, op=ALU.add),
                 reads=["f1_ex1", "f1_em2"], writes=["f1_den"])
            p.op("vector", lambda e: e.reciprocal(rden, zc), reads=["f1_z"], writes=["f1_rz"])
            p.op("vector", lambda e: e.tensor_mul(gv, den, rden),
                 reads=["f1_den", "f1_rz"], writes=["f1_gv"])
            p.op("vector", lambda e: e.tensor_scalar(out=KP[:, :], in0=ET[:, :], scalar1=em2, scalar2=None,
                                                     op0=ALU.is_ge),
                 reads=["f1_et", "f1_em2"], writes=["f1_kp"])
            p.op("vector", lambda e, b=b, i=i: e.tensor_scalar(
                out=GATES[:, (b * NT + i) * 8:(b * NT + i) * 8 + 8], in0=KP[:, :],
                scalar1=gv, scalar2=None, op0=ALU.mult),
                reads=["f1_kp", "f1_gv"], writes=[f"f1_gate{b}_{i}"])
        for fc in range(NF):
            for sl4 in range(4):
                tsl = slice(sl4 * 512, (sl4 + 1) * 512)
                pp, ppk = ctx.psk8(fc * 4 + sl4)
                for dc in range(NC):
                    p.op("tensor", lambda e, dc=dc, fc=fc, pp=pp, tsl=tsl: e.matmul(
                        pp, W64[dc][:, fc * 128:(fc + 1) * 128], XN[dc][:, tsl],
                        start=(dc == 0), stop=(dc == NC - 1)),
                        reads=[wk64[dc], xnk[dc]], writes=[ppk])
                p.op("scalar", lambda e, fc=fc, pp=pp, tsl=tsl: e.activation(
                    out=BB[:, tsl], in_=pp, func=AF.Identity, bias=biT[:, fc:fc + 1]),
                    reads=[ppk, biT_k], writes=["f1_bb"])
            ctx.store(qs, base_sp[b, fc], BB[:, :], ["f1_bb"], writes=[f"f1base{b}_{fc}"])
        for ex in range(8):
            for sl4 in range(4):
                tsl = slice(sl4 * 512, (sl4 + 1) * 512)
                pu = ctx.PS[2][0:16, 0:512]
                for dc in range(NC):
                    p.op("tensor", lambda e, dc=dc, ex=ex, pu=pu, tsl=tsl: e.matmul(
                        pu, ATall[:, (ex * 8 + dc) * 16:(ex * 8 + dc) * 16 + 16], XN[dc][:, tsl],
                        start=(dc == 0), stop=(dc == NC - 1)),
                        reads=["f1_at", xnk[dc]], writes=["PS2_0"])
                p.op("vector", lambda e, pu=pu, tsl=tsl: e.tensor_copy(out=UTB[0:16, tsl], in_=pu),
                     reads=["PS2_0"], writes=["f1_utb"])
            ctx.store(qs, u_sp[b, ex], UTB[0:16, :], ["f1_utb"], writes=[f"f1u{b}_{ex}"])

    WOT = [(W64[fc // 4][:, (fc % 4) * 1024:(fc % 4) * 1024 + 1024], wk64[fc // 4]) for fc in range(NF)]
    for dc in range(NC):
        j = dc % 2
        for q4 in range(4):
            ctx.dma(qw, wo[dc * 128:(dc + 1) * 128, q4 * 1024:(q4 + 1) * 1024], WSTG[j][:, :], f"f1_stg{j}")
            for fq in range(8):
                fc = q4 * 8 + fq
                dst, dstk = WOT[fc]
                transpose_128(WSTG[j][:, fq * 128:(fq + 1) * 128], f"f1_stg{j}",
                              dst[:, dc * 128:(dc + 1) * 128], dstk, fq)
    for b in range(NB):
        for sl8 in range(8):
            t0 = sl8 * 256
            p.dma(qs, lambda e, b=b, t0=t0: e.dma_start(
                out=BSL.rearrange("p (f t) -> p f t", t=256),
                in_=base_sp[b, :, :, t0:t0 + 256].rearrange("f p t -> p f t")),
                reads=[f"f1base{b}_{fc}" for fc in range(NF)], writes=["f1_bsl"])
            for ex in range(8):
                p.dma(qw, lambda e, ex=ex: e.dma_start(out=BTW[0:16, :], in_=loraBT[ex]),
                      reads=[f"f1bt{ex}_{h2}" for h2 in range(8)], writes=["f1_btw"])
                p.dma(qs, lambda e, b=b, ex=ex, t0=t0: e.dma_start(
                    out=UTe[0:16, :], in_=u_sp[b, ex, :, t0:t0 + 256]),
                    reads=[f"f1u{b}_{ex}"], writes=["f1_ute"])

                def emit_lora(fc):
                    sl4 = fc % 4
                    pl = ctx.PS[2 + sl4 // 2][:, (fc % 2) * 512:(fc % 2) * 512 + 256]
                    plk = f"PS{2 + sl4 // 2}_{fc % 2}"
                    p.op("tensor", lambda e, fc=fc, pl=pl: e.matmul(
                        pl, BTW[0:16, fc * 128:(fc + 1) * 128], UTe[0:16, :],
                        start=True, stop=True),
                        reads=["f1_btw", "f1_ute"], writes=[plk])
                    jf = fc % 2
                    p.op("vector", lambda e, fc=fc, pl=pl, jf=jf: e.tensor_tensor(
                        out=HF[jf][:, :], in0=BSL[:, fc * 256:(fc + 1) * 256], in1=pl, op=ALU.add),
                        reads=["f1_bsl", plk], writes=[f"f1_hf{jf}"])
                    p.op("scalar", lambda e, j=fc % 4, jf=jf: e.activation(
                        out=HB[j][:, :], in_=HF[jf][:, :], func=AF.Relu),
                        reads=[f"f1_hf{jf}"], writes=[f"f1_hb{fc % 4}"])

                emit_lora(0)
                emit_lora(1)
                for fc in range(NF):
                    if fc + 2 < NF:
                        emit_lora(fc + 2)
                    for ts2 in range(2):
                        py = ctx.PS[ts2][:, :]
                        pyk = [f"PS{ts2}_0", f"PS{ts2}_1"]
                        for nh2 in range(2):
                            p.op("tensor", lambda e, fc=fc, j=fc % 4, ts2=ts2, nh2=nh2, py=py: e.matmul(
                                py[:, nh2 * 512:(nh2 + 1) * 512],
                                HB[j][:, ts2 * 128:(ts2 + 1) * 128],
                                WOT[fc][0][:, nh2 * 512:(nh2 + 1) * 512],
                                start=(fc == 0), stop=False),
                                reads=[f"f1_hb{fc % 4}", WOT[fc][1]], writes=[pyk[nh2]])
                for ts2 in range(2):
                    py = ctx.PS[ts2][:, :]
                    pyk = [f"PS{ts2}_0", f"PS{ts2}_1"]
                    for nh2 in range(2):
                        p.op("tensor", lambda e, ts2=ts2, nh2=nh2, py=py: e.matmul(
                            py[:, nh2 * 512:(nh2 + 1) * 512], ctx.ones_row_bf[0:1, :],
                            bor[0:1, nh2 * 512:(nh2 + 1) * 512], start=False, stop=True),
                            reads=["ones_row_bf", bor_k], writes=[pyk[nh2]])
                    i2 = sl8 * 2 + ts2
                    gcol = GATES[:, (b * NT + i2) * 8 + ex:(b * NT + i2) * 8 + ex + 1]
                    gkey = f"f1_gate{b}_{i2}"
                    if ex == 0:
                        p.op("vector", lambda e, ts2=ts2, py=py, gcol=gcol: e.tensor_scalar(
                            out=ACC2[ts2][:, :], in0=py, scalar1=gcol, scalar2=None, op0=ALU.mult),
                            reads=pyk + [gkey], writes=[f"f1_acc{ts2}"])
                    else:
                        p.op("vector", lambda e, ts2=ts2, py=py, gcol=gcol: e.scalar_tensor_tensor(
                            out=ACC2[ts2][:, :], in0=py, scalar=gcol, in1=ACC2[ts2][:, :],
                            op0=ALU.mult, op1=ALU.add),
                            reads=pyk + [gkey, f"f1_acc{ts2}"], writes=[f"f1_acc{ts2}"])
            for ts2 in range(2):
                p.op("scalar", lambda e, ts2=ts2: e.activation(out=GBUF[ts2][:, :], in_=ACC2[ts2][:, :],
                                                               func=AF.Silu),
                     reads=[f"f1_acc{ts2}"], writes=[f"f1_gbuf{ts2}"])
                ctx.store(qs, gout[b, t0 + ts2 * 128:t0 + (ts2 + 1) * 128, :],
                          GBUF[ts2][:, :], [f"f1_gbuf{ts2}"])



N_CORES = 8


def _new_ctx():
    nc = bass.Bass("TRN2", target_bir_lowering=False, debug=False)
    ctx = Ctx(nc)
    return ctx


def build_ln():
    ctx = _new_ctx()
    nc = ctx.nc
    xin = nc.dram_tensor("xin", [NB, T, D], F32, kind="ExternalInput").ap()
    xout = nc.dram_tensor("xout", [NB, T, D], BF16, kind="ExternalOutput").ap()
    xout32 = nc.dram_tensor("xout32", [NB, T, D], F32, kind="ExternalOutput").ap()
    ctx.shared()
    emit_ln(ctx, xin, xout, xout32)
    ctx.p.build_block()
    return ctx.nc


def build_ffn0():
    ctx = _new_ctx()
    nc = ctx.nc
    xn = nc.dram_tensor("xn", [NB, T, D], BF16, kind="ExternalInput").ap()
    xn32 = nc.dram_tensor("xn32", [NB, T, D], F32, kind="ExternalInput").ap()
    Wg = nc.dram_tensor("Wg", [D, 8], F32, kind="ExternalInput").ap()
    bg = nc.dram_tensor("bg", [8], F32, kind="ExternalInput").ap()
    W1 = nc.dram_tensor("W1", [8, D, 2048], F32, kind="ExternalInput").ap()
    b1 = nc.dram_tensor("b1", [8, 2048], F32, kind="ExternalInput").ap()
    W2 = nc.dram_tensor("W2", [8, 2048, D], F32, kind="ExternalInput").ap()
    b2 = nc.dram_tensor("b2", [8, D], F32, kind="ExternalInput").ap()
    g = nc.dram_tensor("g", [NB, T, D], F32, kind="ExternalOutput").ap()
    ctx.shared()
    emit_ffn0(ctx, xn, xn32, Wg, bg, W1, b1, W2, b2, g)
    ctx.p.build_block()
    return ctx.nc


def build_attn0():
    ctx = _new_ctx()
    nc = ctx.nc
    xn = nc.dram_tensor("xn", [NB, T, D], BF16, kind="ExternalInput").ap()
    wq = nc.dram_tensor("wq", [D, D], F32, kind="ExternalInput").ap()
    wk = nc.dram_tensor("wk", [D, D], F32, kind="ExternalInput").ap()
    wv = nc.dram_tensor("wv", [D, D], F32, kind="ExternalInput").ap()
    wo = nc.dram_tensor("wo", [D, D], F32, kind="ExternalInput").ap()
    wqb = nc.dram_tensor("wqb", [D], F32, kind="ExternalInput").ap()
    wkb = nc.dram_tensor("wkb", [D], F32, kind="ExternalInput").ap()
    wvb = nc.dram_tensor("wvb", [D], F32, kind="ExternalInput").ap()
    wob = nc.dram_tensor("wob", [D], F32, kind="ExternalInput").ap()
    thr = nc.dram_tensor("thr", [16], F32, kind="ExternalInput").ap()
    sharp = nc.dram_tensor("sharp", [16], F32, kind="ExternalInput").ap()
    s = nc.dram_tensor("s", [NB, T, D], F32, kind="ExternalOutput").ap()
    ctx.shared()
    emit_attn0(ctx, xn, wq, wk, wv, wo, wqb, wkb, wvb, wob, thr, sharp, s)
    ctx.p.build_block()
    return ctx.nc


def build_gla(debug=False):
    ctx = _new_ctx()
    nc = ctx.nc
    xn = nc.dram_tensor("xn", [NB, T, D], BF16, kind="ExternalInput").ap()
    Wq = nc.dram_tensor("Wq", [D, 256], F32, kind="ExternalInput").ap()
    bq = nc.dram_tensor("bq", [256], F32, kind="ExternalInput").ap()
    Wk = nc.dram_tensor("Wk", [D, 256], F32, kind="ExternalInput").ap()
    bk = nc.dram_tensor("bk", [256], F32, kind="ExternalInput").ap()
    Wv = nc.dram_tensor("Wv", [D, 256], F32, kind="ExternalInput").ap()
    bv = nc.dram_tensor("bv", [256], F32, kind="ExternalInput").ap()
    Wg = nc.dram_tensor("Wg", [D, 256], F32, kind="ExternalInput").ap()
    bg = nc.dram_tensor("bg", [256], F32, kind="ExternalInput").ap()
    Wo = nc.dram_tensor("Wo", [256, D], F32, kind="ExternalInput").ap()
    bo = nc.dram_tensor("bo", [D], F32, kind="ExternalInput").ap()
    s = nc.dram_tensor("s", [NB, T, D], F32, kind="ExternalOutput").ap()
    dbg = None
    if debug:
        dbg = {
            "LG": nc.dram_tensor("dbg_LG", [NB, 2, 128, T], F32, kind="ExternalOutput").ap(),
            "OF": nc.dram_tensor("dbg_OF", [NB, 2, 128, T], BF16, kind="ExternalOutput").ap(),
            "S": nc.dram_tensor("dbg_S", [4, 2, 128, 256], F32, kind="ExternalOutput").ap(),
            "QT": nc.dram_tensor("dbg_QT", [4, 2, 128, 64], BF16, kind="ExternalOutput").ap(),
            "F2": nc.dram_tensor("dbg_F2", [4, 2, 128, 64], BF16, kind="ExternalOutput").ap(),
            "KTq": nc.dram_tensor("dbg_KT", [4, 2, 128, 64], BF16, kind="ExternalOutput").ap(),
            "COLS": nc.dram_tensor("dbg_COLS", [4, 128, 8], F32, kind="ExternalOutput").ap(),
            "AT": nc.dram_tensor("dbg_AT", [4, 64, 64], BF16, kind="ExternalOutput").ap(),
        }
    ctx.shared()
    emit_gla(ctx, xn, Wq, bq, Wk, bk, Wv, bv, Wg, bg, Wo, bo, s, debug=dbg)
    ctx.p.build_block()
    return ctx.nc


def build_ffn1():
    ctx = _new_ctx()
    nc = ctx.nc
    xn = nc.dram_tensor("xn", [NB, T, D], BF16, kind="ExternalInput").ap()
    xn32 = nc.dram_tensor("xn32", [NB, T, D], F32, kind="ExternalInput").ap()
    wi = nc.dram_tensor("wi", [4096, D], F32, kind="ExternalInput").ap()
    bi = nc.dram_tensor("bi", [4096], F32, kind="ExternalInput").ap()
    wo = nc.dram_tensor("wo", [D, 4096], F32, kind="ExternalInput").ap()
    bo = nc.dram_tensor("bo", [D], F32, kind="ExternalInput").ap()
    lora_A = nc.dram_tensor("lora_A", [8, 16, D], F32, kind="ExternalInput").ap()
    lora_B = nc.dram_tensor("lora_B", [8, 4096, 16], F32, kind="ExternalInput").ap()
    gate_w = nc.dram_tensor("gate_w", [8, D], F32, kind="ExternalInput").ap()
    gate_b = nc.dram_tensor("gate_b", [8], F32, kind="ExternalInput").ap()
    g = nc.dram_tensor("g", [NB, T, D], F32, kind="ExternalOutput").ap()
    base_sp = nc.dram_tensor("base_sp", [NB, 32, 128, T], BF16, kind="Internal").ap()
    u_sp = nc.dram_tensor("u_sp", [NB, 8, 16, T], BF16, kind="Internal").ap()
    loraBT = nc.dram_tensor("loraBT", [8, 16, 4096], BF16, kind="Internal").ap()
    ctx.shared()
    emit_ffn1(ctx, xn, xn32, wi, bi, wo, bo, lora_A, lora_B, gate_w, gate_b,
              base_sp, u_sp, loraBT, g)
    ctx.p.build_block()
    return ctx.nc


def build_comb():
    ctx = _new_ctx()
    nc = ctx.nc
    xin = nc.dram_tensor("xin", [NB, T, D], F32, kind="ExternalInput").ap()
    sin = nc.dram_tensor("sin", [NB, T, D], F32, kind="ExternalInput").ap()
    gin = nc.dram_tensor("gin", [NB, T, D], F32, kind="ExternalInput").ap()
    xout = nc.dram_tensor("xout", [NB, T, D], F32, kind="ExternalOutput").ap()
    ctx.shared()
    emit_comb(ctx, xin, sin, gin, xout)
    ctx.p.build_block()
    return ctx.nc


def build_fused():
    ctx = _new_ctx()
    nc = ctx.nc

    def inp(name, shape, dt=F32):
        return nc.dram_tensor(name, shape, dt, kind="ExternalInput").ap()

    X = inp("L0_attn_x", [NB, T, D])
    wq = inp("L0_attn_wq_w", [D, D]); wqb = inp("L0_attn_wq_b", [D])
    wk = inp("L0_attn_wk_w", [D, D]); wkb = inp("L0_attn_wk_b", [D])
    wv = inp("L0_attn_wv_w", [D, D]); wvb = inp("L0_attn_wv_b", [D])
    wo = inp("L0_attn_wo_w", [D, D]); wob = inp("L0_attn_wo_b", [D])
    thr = inp("L0_attn_learned_threshold", [16])
    sharp = inp("L0_attn_gate_sharpness", [16])
    Wg0 = inp("L0_ffn_Wg", [D, 8]); bg0 = inp("L0_ffn_bg", [8])
    W10 = inp("L0_ffn_W1", [8, D, 2048]); b10 = inp("L0_ffn_b1", [8, 2048])
    W20 = inp("L0_ffn_W2", [8, 2048, D]); b20 = inp("L0_ffn_b2", [8, D])
    Wq1 = inp("L1_attn_Wq", [D, 256]); bq1 = inp("L1_attn_bq", [256])
    Wk1 = inp("L1_attn_Wk", [D, 256]); bk1 = inp("L1_attn_bk", [256])
    Wv1 = inp("L1_attn_Wv", [D, 256]); bv1 = inp("L1_attn_bv", [256])
    Wg1 = inp("L1_attn_Wg", [D, 256]); bg1 = inp("L1_attn_bg", [256])
    Wo1 = inp("L1_attn_Wo", [256, D]); bo1 = inp("L1_attn_bo", [D])
    wi1 = inp("L1_ffn_wi", [4096, D]); bi1 = inp("L1_ffn_bi", [4096])
    wo1 = inp("L1_ffn_wo", [D, 4096]); boo1 = inp("L1_ffn_bo", [D])
    lA1 = inp("L1_ffn_lora_A", [8, 16, D]); lB1 = inp("L1_ffn_lora_B", [8, 4096, 16])
    gw1 = inp("L1_ffn_gate_w", [8, D]); gb1 = inp("L1_ffn_gate_b", [8])
    OUT = nc.dram_tensor("out", [NB, T, D], F32, kind="ExternalOutput").ap()

    scr = lambda name, shape, dt: nc.dram_tensor(name, shape, dt, kind="Internal").ap()
    xn0 = scr("xn0", [NB, T, D], BF16)
    xn0_32 = scr("xn0_32", [NB, T, D], F32)
    g_sp = scr("g_sp", [NB, T, D], F32)
    s_sp = scr("s_sp", [NB, T, D], F32)
    x1_sp = scr("x1_sp", [NB, T, D], F32)
    xn1 = scr("xn1", [NB, T, D], BF16)
    xn1_32 = scr("xn1_32", [NB, T, D], F32)
    base_sp = scr("base_sp", [NB, 32, 128, T], BF16)
    u_sp = scr("u_sp", [NB, 8, 16, T], BF16)
    loraBT = scr("loraBT", [8, 16, 4096], BF16)

    ctx.shared()

    def stage(fn, *args, **kw):
        ctx.push_scope()
        fn(ctx, *args, **kw)
        ctx.pop_scope()

    stage(emit_ln, X, xn0, xn0_32)
    stage(emit_ffn0, xn0, xn0_32, Wg0, bg0, W10, b10, W20, b20, g_sp)
    stage(emit_attn0, xn0, wq, wk, wv, wo, wqb, wkb, wvb, wob, thr, sharp, s_sp)
    stage(emit_comb, X, s_sp, g_sp, x1_sp)
    stage(emit_ln, x1_sp, xn1, xn1_32)
    stage(emit_gla, xn1, Wq1, bq1, Wk1, bk1, Wv1, bv1, Wg1, bg1, Wo1, bo1, s_sp)
    stage(emit_ffn1, xn1, xn1_32, wi1, bi1, wo1, boo1, lA1, lB1, gw1, gb1,
          base_sp, u_sp, loraBT, g_sp)
    stage(emit_comb, x1_sp, s_sp, g_sp, OUT)
    ctx.p.build_block()
    return ctx.nc


def kernel_fused(**inputs):
    x = np.ascontiguousarray(inputs["L0_attn_x"])
    nc = _get_nc("fused", build_fused)
    wnames = [n for n in inputs if n != "L0_attn_x"]
    in_maps = []
    for c in range(N_CORES):
        m = {n: inputs[n] for n in wnames}
        m["L0_attn_x"] = x[2 * c:2 * c + 2]
        in_maps.append(m)
    res = run_bass_kernel_spmd(nc, in_maps, core_ids=list(range(N_CORES)))
    return np.concatenate([np.asarray(r["out"]) for r in res.results], axis=0)


_NC_CACHE = {}


def _get_nc(name, builder):
    if name not in _NC_CACHE:
        _NC_CACHE[name] = builder()
    return _NC_CACHE[name]


def kernel(**inputs):
    return kernel_fused(**inputs)


def kernel_unfused(**inputs):
    ins = inputs
    x = np.ascontiguousarray(ins["L0_attn_x"])
    xs = [x[2 * c:2 * c + 2] for c in range(N_CORES)]
    cores = list(range(N_CORES))

    def launch(name, builder, per_core, shared):
        nc = _get_nc(name, builder)
        in_maps = [dict(per_core[c], **shared) for c in range(N_CORES)]
        res = run_bass_kernel_spmd(nc, in_maps, core_ids=cores)
        return res.results

    r = launch("ln", build_ln, [{"xin": xs[c]} for c in range(N_CORES)], {})
    xn0 = [r[c]["xout"] for c in range(N_CORES)]
    xn0_32 = [r[c]["xout32"] for c in range(N_CORES)]

    r = launch("ffn0", build_ffn0,
               [{"xn": xn0[c], "xn32": xn0_32[c]} for c in range(N_CORES)],
               {"Wg": ins["L0_ffn_Wg"], "bg": ins["L0_ffn_bg"],
                "W1": ins["L0_ffn_W1"], "b1": ins["L0_ffn_b1"],
                "W2": ins["L0_ffn_W2"], "b2": ins["L0_ffn_b2"]})
    g0 = [r[c]["g"] for c in range(N_CORES)]

    r = launch("attn0", build_attn0,
               [{"xn": xn0[c]} for c in range(N_CORES)],
               {"wq": ins["L0_attn_wq_w"], "wk": ins["L0_attn_wk_w"],
                "wv": ins["L0_attn_wv_w"], "wo": ins["L0_attn_wo_w"],
                "wqb": ins["L0_attn_wq_b"], "wkb": ins["L0_attn_wk_b"],
                "wvb": ins["L0_attn_wv_b"], "wob": ins["L0_attn_wo_b"],
                "thr": ins["L0_attn_learned_threshold"],
                "sharp": ins["L0_attn_gate_sharpness"]})
    s0 = [r[c]["s"] for c in range(N_CORES)]

    r = launch("comb", build_comb,
               [{"xin": xs[c], "sin": s0[c], "gin": g0[c]} for c in range(N_CORES)], {})
    x1 = [r[c]["xout"] for c in range(N_CORES)]

    r = launch("ln", build_ln, [{"xin": x1[c]} for c in range(N_CORES)], {})
    xn1 = [r[c]["xout"] for c in range(N_CORES)]
    xn1_32 = [r[c]["xout32"] for c in range(N_CORES)]

    r = launch("gla", build_gla,
               [{"xn": xn1[c]} for c in range(N_CORES)],
               {"Wq": ins["L1_attn_Wq"], "bq": ins["L1_attn_bq"],
                "Wk": ins["L1_attn_Wk"], "bk": ins["L1_attn_bk"],
                "Wv": ins["L1_attn_Wv"], "bv": ins["L1_attn_bv"],
                "Wg": ins["L1_attn_Wg"], "bg": ins["L1_attn_bg"],
                "Wo": ins["L1_attn_Wo"], "bo": ins["L1_attn_bo"]})
    s1 = [r[c]["s"] for c in range(N_CORES)]

    r = launch("ffn1", build_ffn1,
               [{"xn": xn1[c], "xn32": xn1_32[c]} for c in range(N_CORES)],
               {"wi": ins["L1_ffn_wi"], "bi": ins["L1_ffn_bi"],
                "wo": ins["L1_ffn_wo"], "bo": ins["L1_ffn_bo"],
                "lora_A": ins["L1_ffn_lora_A"], "lora_B": ins["L1_ffn_lora_B"],
                "gate_w": ins["L1_ffn_gate_w"], "gate_b": ins["L1_ffn_gate_b"]})
    g1 = [r[c]["g"] for c in range(N_CORES)]

    r = launch("comb", build_comb,
               [{"xin": x1[c], "sin": s1[c], "gin": g1[c]} for c in range(N_CORES)], {})
    out = np.concatenate([np.asarray(r[c]["xout"]) for c in range(N_CORES)], axis=0)
    return out


# version 13
# speedup vs baseline: 1.0294x; 1.0294x over previous
import numpy as np
import ml_dtypes
from contextlib import ExitStack

import concourse.bass as bass
import concourse.mybir as mybir
from concourse.bass_utils import run_bass_kernel_spmd

F32 = mybir.dt.float32
BF16 = mybir.dt.bfloat16
AF = mybir.ActivationFunctionType
ALU = mybir.AluOpType
AX = mybir.AxisListType

ENGINES = ("tensor", "scalar", "vector", "gpsimd", "sync")
SAME_ENGINE_SYNC = True
NOSYNC_ENGINES = ("tensor",)

NB = 2
T = 2048
D = 1024
NT = T // 128
NC = D // 128



class DmaQueue:
    def __init__(self, prog, name, issue_engine, nlanes=8):
        self.prog = prog
        self.name = name
        self.eng = issue_engine
        self.nlanes = nlanes
        self.sems = [prog.nc.alloc_semaphore(f"dq_{name}_{i}") for i in range(nlanes)]
        self.lane_cnt = [0] * nlanes
        self.n = 0

    def key(self, lane):
        return f"__dmaq_{self.name}_{lane}"


class Prog:
    def __init__(self, nc, same_engine_sync=True):
        self.nc = nc
        self.sems = {e: nc.alloc_semaphore(f"S_{e}") for e in ENGINES}
        self.cnt = {e: 0 for e in ENGINES}
        self.items = {e: [] for e in ENGINES}
        self.writer = {}
        self.readers = {}
        self.frontier = {e: {} for e in ENGINES}
        self.same_sync = same_engine_sync
        self.queues = {}
        self.nops = 0

    def queue(self, name, issue_engine, nlanes=8):
        q = DmaQueue(self, name, issue_engine, nlanes)
        self.queues[name] = q
        return q

    def _src_id(self, src):
        return ("eng", src[1]) if src[0] == "eng" else ("dma", src[1], src[2])

    def _sem_of(self, sid):
        return self.sems[sid[1]] if sid[0] == "eng" else self.queues[sid[1]].sems[sid[2]]

    def _collect_waits(self, eng, reads, writes):
        waits = {}

        def need(src):
            cnt = src[-1]
            sid = self._src_id(src)
            if sid == ("eng", eng) and (not self.same_sync or eng in NOSYNC_ENGINES):
                return
            if self.frontier[eng].get(sid, 0) >= cnt:
                return
            old = waits.get(sid)
            if old is None or old < cnt:
                waits[sid] = cnt

        for k in reads:
            w = self.writer.get(k)
            if w is not None:
                need(w)
        for k in writes:
            w = self.writer.get(k)
            if w is not None:
                need(w)
            for sid, cnt in self.readers.get(k, {}).items():
                need((sid[0],) + sid[1:] + (cnt,))
        out = []
        for sid, val in waits.items():
            self.frontier[eng][sid] = val
            out.append((sid, val))
        return out

    def op(self, eng, fn, reads=(), writes=()):
        waits = self._collect_waits(eng, reads, writes)
        self.cnt[eng] += 1
        my = self.cnt[eng]
        self.items[eng].append((waits, fn, ("eng", eng), 1))
        me = ("eng", eng)
        for k in reads:
            self.readers.setdefault(k, {})[me] = my
        for k in writes:
            self.writer[k] = ("eng", eng, my)
            self.readers[k] = {}
        self.nops += 1
        return ("eng", eng, my)

    def dma(self, q, fn, reads=(), writes=()):
        eng = q.eng
        lane = q.n % q.nlanes
        reads = list(reads) + [q.key(lane)]
        waits = self._collect_waits(eng, reads, writes)
        q.n += 1
        q.lane_cnt[lane] += 16
        my = q.lane_cnt[lane]
        self.items[eng].append((waits, fn, ("dma", q.name, lane), 16))
        src = ("dma", q.name, lane, my)
        me = self._src_id(src)
        for k in reads:
            self.readers.setdefault(k, {})[me] = my
        self.writer[q.key(lane)] = src
        self.readers[q.key(lane)] = {}
        for k in writes:
            self.writer[k] = src
            self.readers[k] = {}
        self.nops += 1
        return src

    def barrier(self):
        targets = [(("eng", e), self.sems[e], self.cnt[e]) for e in ENGINES if self.cnt[e] > 0]
        for q in self.queues.values():
            for lane in range(q.nlanes):
                if q.lane_cnt[lane] > 0:
                    targets.append((("dma", q.name, lane), q.sems[lane], q.lane_cnt[lane]))
        for e in ENGINES:
            waits = []
            for sid, sem, val in targets:
                if sid == ("eng", e):
                    continue
                if self.frontier[e].get(sid, 0) >= val:
                    continue
                self.frontier[e][sid] = val
                waits.append((sid, val))
            if waits:
                self.items[e].append((waits, None, None, 0))

    def finish(self):
        eng = "gpsimd"
        for q in self.queues.values():
            for lane in range(q.nlanes):
                if q.lane_cnt[lane] > 0:
                    sid = ("dma", q.name, lane)
                    if self.frontier[eng].get(sid, 0) >= q.lane_cnt[lane]:
                        continue
                    self.items[eng].append(
                        ([(sid, q.lane_cnt[lane])], None, None, 0))
                    self.frontier[eng][sid] = q.lane_cnt[lane]

    def _prepare(self):
        pass

    def _run(self, eng_obj, eng):
        for waits, fn, dst, inc in self.items[eng]:
            for sid, wval in waits:
                if sid[0] == "eng":
                    eng_obj.wait_ge(self.sems[sid[1]], wval)
                else:
                    eng_obj.wait_ge(self.queues[sid[1]].sems[sid[2]], wval)
            if fn is not None:
                inst = fn(eng_obj)
                if dst[0] == "eng":
                    inst.then_inc(self.sems[eng], 1)
                else:
                    inst.then_inc(self.queues[dst[1]].sems[dst[2]], 16)

    def build_block(self):
        self.finish()
        self._prepare()
        nc = self.nc
        with nc.Block() as block:
            @block.sync
            def _(e):
                self._run(e, "sync")

            @block.tensor
            def _(e):
                self._run(e, "tensor")

            @block.scalar
            def _(e):
                self._run(e, "scalar")

            @block.vector
            def _(e):
                self._run(e, "vector")

            @block.gpsimd
            def _(e):
                self._run(e, "gpsimd")


class Ctx:

    def __init__(self, nc):
        self.nc = nc
        self.p = Prog(nc, same_engine_sync=SAME_ENGINE_SYNC)
        self.qs = self.p.queue("sp", "sync", nlanes=8)
        self.qw = self.p.queue("gw", "gpsimd", nlanes=8)
        self.st = ExitStack()
        self._scopes = []
        self._uid = 0

    def sb(self, name, shape, dtype):
        return self.st.enter_context(self.nc.sbuf_tensor(f"{name}_{self.uid()}", shape, dtype))

    def push_scope(self):
        self._scopes.append(self.st)
        self.st = ExitStack()

    def pop_scope(self):
        self.st.close()
        self.st = self._scopes.pop()
        self.p.barrier()

    def ps(self, name, shape, dtype=F32):
        return self.st.enter_context(self.nc.psum_tensor(name, shape, dtype))

    def uid(self):
        self._uid += 1
        return self._uid

    def shared(self):
        nc = self.nc
        self.ident_bf = self.sb("ident_bf", [128, 128], BF16)
        self.ident_f32 = self.sb("ident_f32", [128, 128], F32)
        self.mask64 = self.sb("mask64", [64, 64], F32)
        self.ones_col_bf = self.sb("ones_col_bf", [128, 1], BF16)
        self.ones_row_bf = self.sb("ones_row_bf", [1, 128], BF16)
        self.ones_row_f32 = self.sb("ones_row_f32", [1, 128], F32)
        self.zero_col = self.sb("zero_col", [128, 1], F32)
        ident_np = np.eye(128, dtype=ml_dtypes.bfloat16)
        identf_np = np.eye(128, dtype=np.float32)
        mask_np = np.triu(np.ones((64, 64), dtype=np.float32))
        d_ident = nc.inline_tensor(ident_np, "c_ident").ap()
        d_identf = nc.inline_tensor(identf_np, "c_identf").ap()
        d_mask = nc.inline_tensor(mask_np, "c_mask").ap()
        p, qs = self.p, self.qs
        self.dma(qs, d_ident, self.ident_bf[:, :], "ident_bf")
        self.dma(qs, d_identf, self.ident_f32[:, :], "ident_f32")
        self.dma(qs, d_mask, self.mask64[:, :], "mask64")
        p.op("vector", lambda e: e.memset(self.ones_col_bf[:, :], 1.0), writes=["ones_col_bf"])
        p.op("vector", lambda e: e.memset(self.ones_row_bf[:, :], 1.0), writes=["ones_row_bf"])
        p.op("vector", lambda e: e.memset(self.ones_row_f32[:, :], 1.0), writes=["ones_row_f32"])
        p.op("vector", lambda e: e.memset(self.zero_col[:, :], 0.0), writes=["zero_col"])
        self.cconst = self.sb("cconst", [128, 4], F32)
        self._cmap = {}
        for j, v in enumerate([1e-5, -1.0, 0.125, 0.5]):
            p.op("vector", lambda e, j=j, v=v: e.memset(self.cconst[:, j:j + 1], v),
                 writes=[f"cconst{j}"])
            self._cmap[v] = j
        self.PS = [self.ps(f"PSAR{i}", [128, 1024], F32) for i in range(4)]

    def cc(self, val, parts=128):
        j = self._cmap[val]
        return self.cconst[0:parts, j:j + 1], f"cconst{j}"

    def psk(self, i, half):
        return self.PS[i][:, half * 512:(half + 1) * 512], f"PS{i}_{half}"

    def psk8(self, idx):
        return self.psk(idx % 4, (idx // 4) % 2)

    def dma(self, q, src_ap, dst_ap, key, reads=(), transpose=False):
        if transpose:
            return self.p.dma(q, lambda e, o=dst_ap, i=src_ap: e.dma_start(out=o, in_=i, transpose=True),
                              reads=reads, writes=[key])
        return self.p.dma(q, lambda e, o=dst_ap, i=src_ap: e.dma_start(out=o, in_=i),
                          reads=reads, writes=[key])

    def store(self, q, dst_ap, src_ap, reads, writes=()):
        return self.p.dma(q, lambda e, o=dst_ap, i=src_ap: e.dma_start(out=o, in_=i),
                          reads=reads, writes=writes)



class XnLoader:

    def __init__(self, ctx, xn_dram, name):
        self.ctx = ctx
        self.dram = xn_dram
        self.t = [ctx.sb(f"{name}{c}", [128, T], BF16) for c in range(NC)]
        self.k = [f"{name}{c}" for c in range(NC)]

    def load(self, b):
        for c in range(NC):
            src = self.dram[b, :, c * 128:(c + 1) * 128]
            self.ctx.dma(self.ctx.qs, src, self.t[c][:, :], self.k[c], transpose=True)


def bias_cols_prep(ctx, dram_vec, n, name):
    stage = ctx.sb(f"bcs_{name}", [n, 128], F32)
    out = ctx.sb(f"bc_{name}", [128, n], F32)
    kst, kout = f"bcs_{name}", f"bc_{name}"
    ctx.dma(ctx.qs, dram_vec, stage[:, :], kst)
    psum = ctx.PS[3].bitcast(F32)[:, 0:128]
    ctx.p.op("tensor", lambda e: e.transpose(psum[0:128, 0:n], stage[:, :], ctx.ident_f32[0:n, 0:n]),
             reads=[kst, "ident_f32"], writes=["PS3_0"])
    ctx.p.op("vector", lambda e: e.tensor_copy(out=out[:, :], in_=psum[0:128, 0:n]),
             reads=["PS3_0"], writes=[kout])
    return out, kout


def bias_row(ctx, dram_vec, n, name, dtype=BF16):
    t = ctx.sb(f"br_{name}", [1, n], dtype)
    ctx.dma(ctx.qw if dtype == BF16 else ctx.qs, dram_vec, t[:, :], f"br_{name}")
    return t, f"br_{name}"



def emit_ln(ctx, xin, xout, xout32=None):
    p, qs = ctx.p, ctx.qs
    XT = [ctx.sb(f"ln_x{i}", [128, D], F32) for i in range(3)]
    XN = [ctx.sb(f"ln_o{i}", [128, D], BF16) for i in range(2)]
    XN32 = [ctx.sb(f"ln_p{i}", [128, D], F32) for i in range(2)] if xout32 is not None else None
    SQ = ctx.sb("ln_sq", [128, D], F32)
    C = ctx.sb("ln_c", [128, 8], F32)
    s1, s2, m, e2, nv, sd, r, nb = [C[:, j:j + 1] for j in range(8)]
    for b in range(NB):
        for i in range(NT):
            xk = f"ln_x{(b * NT + i) % 3}"
            ok = f"ln_o{(b * NT + i) % 2}"
            xt = XT[(b * NT + i) % 3]
            xo = XN[(b * NT + i) % 2]
            ctx.dma(qs, xin[b, i * 128:(i + 1) * 128, :], xt[:, :], xk)
            p.op("scalar", lambda e, xt=xt: e.activation(out=SQ[:, :], in_=xt[:, :], func=AF.Square,
                                                         accum_out=s2),
                 reads=[xk], writes=["ln_sq", "ln_s2"])
            p.op("vector", lambda e, xt=xt: e.reduce_sum(s1, xt[:, :], axis=AX.X),
                 reads=[xk], writes=["ln_s1"])
            p.op("vector", lambda e: e.tensor_scalar(out=m, in0=s1, scalar1=1.0 / D, scalar2=None, op0=ALU.mult),
                 reads=["ln_s1"], writes=["ln_m"])
            p.op("vector", lambda e: e.tensor_scalar(out=e2, in0=s2, scalar1=1.0 / D, scalar2=None, op0=ALU.mult),
                 reads=["ln_s2"], writes=["ln_e2"])
            p.op("vector", lambda e: e.scalar_tensor_tensor(out=nv, in0=m, scalar=m, in1=e2,
                                                            op0=ALU.mult, op1=ALU.subtract),
                 reads=["ln_m", "ln_e2"], writes=["ln_nv"])
            eps_ap, eps_k = ctx.cc(1e-5)
            neg1_ap, neg1_k = ctx.cc(-1.0)
            p.op("scalar", lambda e: e.activation(out=sd, in_=nv, func=AF.Sqrt, bias=eps_ap, scale=neg1_ap),
                 reads=["ln_nv", eps_k, neg1_k], writes=["ln_sd"])
            p.op("vector", lambda e: e.reciprocal(r, sd), reads=["ln_sd"], writes=["ln_r"])
            p.op("vector", lambda e: e.scalar_tensor_tensor(out=nb, in0=m, scalar=-1.0, in1=r,
                                                            op0=ALU.mult, op1=ALU.mult),
                 reads=["ln_m", "ln_r"], writes=["ln_nb"])
            p.op("scalar", lambda e, xt=xt, xo=xo: e.activation(out=xo[:, :], in_=xt[:, :], func=AF.Identity,
                                                                bias=nb, scale=r),
                 reads=[xk, "ln_nb", "ln_r"], writes=[ok])
            ctx.store(qs, xout[b, i * 128:(i + 1) * 128, :], xo[:, :], [ok])
            if xout32 is not None:
                ok32 = f"ln_p{(b * NT + i) % 2}"
                xo32 = XN32[(b * NT + i) % 2]
                p.op("vector", lambda e, xt=xt, xo32=xo32: e.tensor_scalar(
                    out=xo32[:, :], in0=xt[:, :], scalar1=r, scalar2=nb, op0=ALU.mult, op1=ALU.add),
                    reads=[xk, "ln_nb", "ln_r"], writes=[ok32])
                ctx.store(qs, xout32[b, i * 128:(i + 1) * 128, :], xo32[:, :], [ok32])



def emit_comb(ctx, xin, sin, gin, xout):
    p, qs = ctx.p, ctx.qs
    BX = [ctx.sb(f"cb_x{i}", [128, D], F32) for i in range(2)]
    BS = [ctx.sb(f"cb_s{i}", [128, D], F32) for i in range(2)]
    BG = [ctx.sb(f"cb_g{i}", [128, D], F32) for i in range(2)]
    BO = [ctx.sb(f"cb_o{i}", [128, D], F32) for i in range(2)]
    for b in range(NB):
        for i in range(NT):
            j = (b * NT + i) % 2
            kx, ks, kg, ko = f"cb_x{j}", f"cb_s{j}", f"cb_g{j}", f"cb_o{j}"
            sl = slice(i * 128, (i + 1) * 128)
            ctx.dma(qs, xin[b, sl, :], BX[j][:, :], kx)
            ctx.dma(qs, sin[b, sl, :], BS[j][:, :], ks)
            ctx.dma(qs, gin[b, sl, :], BG[j][:, :], kg)
            p.op("vector", lambda e, j=j: e.tensor_mul(BO[j][:, :], BS[j][:, :], BG[j][:, :]),
                 reads=[ks, kg], writes=[ko])
            p.op("vector", lambda e, j=j: e.tensor_add(BO[j][:, :], BO[j][:, :], BX[j][:, :]),
                 reads=[ko, kx], writes=[ko])
            ctx.store(qs, xout[b, sl, :], BO[j][:, :], [ko])


def emit_comb_ln(ctx, xin, sin, gin, x1out, xnout, xnout32):
    p, qs = ctx.p, ctx.qs
    BX = [ctx.sb(f"cl_x{i}", [128, D], F32) for i in range(2)]
    BS = [ctx.sb(f"cl_s{i}", [128, D], F32) for i in range(2)]
    BG = [ctx.sb(f"cl_g{i}", [128, D], F32) for i in range(2)]
    BO = [ctx.sb(f"cl_o{i}", [128, D], F32) for i in range(2)]
    XN = [ctx.sb(f"cl_n{i}", [128, D], BF16) for i in range(2)]
    XN32 = [ctx.sb(f"cl_p{i}", [128, D], F32) for i in range(2)]
    SQ = ctx.sb("cl_sq", [128, D], F32)
    C = ctx.sb("cl_c", [128, 8], F32)
    s1, s2, m, e2, nv, sd, r, nb = [C[:, j:j + 1] for j in range(8)]
    eps_ap, eps_k = ctx.cc(1e-5)
    neg1_ap, neg1_k = ctx.cc(-1.0)
    for b in range(NB):
        for i in range(NT):
            j = (b * NT + i) % 2
            kx, ks, kg, ko = f"cl_x{j}", f"cl_s{j}", f"cl_g{j}", f"cl_o{j}"
            kn, kp = f"cl_n{j}", f"cl_p{j}"
            sl = slice(i * 128, (i + 1) * 128)
            ctx.dma(qs, xin[b, sl, :], BX[j][:, :], kx)
            ctx.dma(qs, sin[b, sl, :], BS[j][:, :], ks)
            ctx.dma(qs, gin[b, sl, :], BG[j][:, :], kg)
            p.op("vector", lambda e, j=j: e.tensor_mul(BO[j][:, :], BS[j][:, :], BG[j][:, :]),
                 reads=[ks, kg], writes=[ko])
            p.op("vector", lambda e, j=j: e.tensor_add(BO[j][:, :], BO[j][:, :], BX[j][:, :]),
                 reads=[ko, kx], writes=[ko])
            ctx.store(qs, x1out[b, sl, :], BO[j][:, :], [ko])
            p.op("scalar", lambda e, j=j: e.activation(out=SQ[:, :], in_=BO[j][:, :], func=AF.Square,
                                                       accum_out=s2),
                 reads=[ko], writes=["cl_sq", "cl_s2"])
            p.op("vector", lambda e, j=j: e.reduce_sum(s1, BO[j][:, :], axis=AX.X),
                 reads=[ko], writes=["cl_s1"])
            p.op("vector", lambda e: e.tensor_scalar(out=m, in0=s1, scalar1=1.0 / D, scalar2=None, op0=ALU.mult),
                 reads=["cl_s1"], writes=["cl_m"])
            p.op("vector", lambda e: e.tensor_scalar(out=e2, in0=s2, scalar1=1.0 / D, scalar2=None, op0=ALU.mult),
                 reads=["cl_s2"], writes=["cl_e2"])
            p.op("vector", lambda e: e.scalar_tensor_tensor(out=nv, in0=m, scalar=m, in1=e2,
                                                            op0=ALU.mult, op1=ALU.subtract),
                 reads=["cl_m", "cl_e2"], writes=["cl_nv"])
            p.op("scalar", lambda e: e.activation(out=sd, in_=nv, func=AF.Sqrt, bias=eps_ap, scale=neg1_ap),
                 reads=["cl_nv", eps_k, neg1_k], writes=["cl_sd"])
            p.op("vector", lambda e: e.reciprocal(r, sd), reads=["cl_sd"], writes=["cl_r"])
            p.op("vector", lambda e: e.scalar_tensor_tensor(out=nb, in0=m, scalar=-1.0, in1=r,
                                                            op0=ALU.mult, op1=ALU.mult),
                 reads=["cl_m", "cl_r"], writes=["cl_nb"])
            p.op("scalar", lambda e, j=j: e.activation(out=XN[j][:, :], in_=BO[j][:, :], func=AF.Identity,
                                                       bias=nb, scale=r),
                 reads=[ko, "cl_nb", "cl_r"], writes=[kn])
            ctx.store(qs, xnout[b, sl, :], XN[j][:, :], [kn])
            p.op("vector", lambda e, j=j: e.tensor_scalar(
                out=XN32[j][:, :], in0=BO[j][:, :], scalar1=r, scalar2=nb, op0=ALU.mult, op1=ALU.add),
                reads=[ko, "cl_nb", "cl_r"], writes=[kp])
            ctx.store(qs, xnout32[b, sl, :], XN32[j][:, :], [kp])



def emit_ffn0(ctx, xn, xn32, Wg, bg, W1, b1, W2, b2, gout):
    p, qs, qw = ctx.p, ctx.qs, ctx.qw
    nH = 16
    WGS = [ctx.sb(f"f0_wg{c}", [128, 8], F32) for c in range(NC)]
    for c in range(NC):
        ctx.dma(qs, Wg[c * 128:(c + 1) * 128, :], WGS[c][:, :], f"f0_wg{c}")
    bg_row, bg_k = bias_row(ctx, bg.rearrange("(o e) -> o e", o=1), 8, "f0bg", dtype=F32)
    XT32 = [ctx.sb(f"f0_xt{i}", [128, D], F32) for i in range(2)]
    XF32 = [ctx.sb(f"f0_xf{c}", [128, 128], F32) for c in range(NC)]
    b1T, b1T_k = bias_cols_prep(ctx, b1.rearrange("e (a p) -> (e a) p", p=128), 128, "f0b1")
    W1S = [ctx.sb(f"f0_w1_{c}", [128, 2048], BF16) for c in range(NC)]
    W2S = [ctx.sb(f"f0_w2_{h}", [128, 1024], BF16) for h in range(nH)]
    B2R = ctx.sb("f0_b2r", [1, 1024], BF16)
    H = [ctx.sb(f"f0_h{h}", [128, 512], BF16) for h in range(nH)]
    ACC = [ctx.sb(f"f0_acc{i}", [128, 1024], F32) for i in range(NT)]
    COMB = ctx.sb("f0_comb", [128, 8 * NT], F32)
    SC = ctx.sb("f0_sc", [128, 8], F32)
    em1, em2, den, rden, nm1, ex1 = [SC[:, j:j + 1] for j in range(6)]
    ET = ctx.sb("f0_et", [128, 8], F32)
    EQ = ctx.sb("f0_eq", [128, 8], F32)
    EMK = ctx.sb("f0_emk", [128, 8], F32)
    KP = ctx.sb("f0_kp", [128, 8], F32)
    GS = [ctx.sb(f"f0_gs{i}", [128, D], F32) for i in range(2)]

    xnl = XnLoader(ctx, xn, "f0xn")
    XN, xnk = xnl.t, xnl.k
    for b in range(NB):
        xnl.load(b)
        for i in range(NT):
            xti = i % 2
            xtk = f"f0_xt{xti}"
            ctx.dma(qs, xn32[b, i * 128:(i + 1) * 128, :], XT32[xti][:, :], xtk)
            for c in range(NC):
                pt = ctx.PS[3][:, 512 + (c % 4) * 128:512 + (c % 4) * 128 + 128]
                ptk = "PS3_1"
                p.op("tensor", lambda e, c=c, xti=xti, pt=pt: e.transpose(
                    pt, XT32[xti][:, c * 128:(c + 1) * 128], ctx.ident_f32[:, :]),
                    reads=[xtk, "ident_f32"], writes=[ptk])
                p.op("vector", lambda e, c=c, pt=pt: e.tensor_copy(out=XF32[c][:, :], in_=pt),
                     reads=[ptk], writes=[f"f0_xf{c}"])
            pg, pgk = ctx.psk(3, 0)
            pg = pg[:, 0:8]
            for c in range(NC):
                p.op("tensor", lambda e, c=c, pg=pg: e.matmul(
                    pg, XF32[c][:, :], WGS[c][:, :],
                    start=(c == 0), stop=False),
                    reads=[f"f0_xf{c}", f"f0_wg{c}"], writes=[pgk])
            p.op("tensor", lambda e, pg=pg: e.matmul(pg, ctx.ones_row_f32[0:1, :], bg_row[0:1, :],
                                                     start=False, stop=True),
                 reads=["ones_row_f32", bg_k], writes=[pgk])
            p.op("vector", lambda e, pg=pg: e.reduce_max(em1, pg, axis=AX.X),
                 reads=[pgk], writes=["f0_em1"])
            p.op("vector", lambda e: e.tensor_scalar(out=nm1, in0=em1, scalar1=-1.0, scalar2=None, op0=ALU.mult),
                 reads=["f0_em1"], writes=["f0_nm1"])
            p.op("scalar", lambda e, pg=pg: e.activation(out=ET[:, :], in_=pg, func=AF.Exp, bias=nm1),
                 reads=[pgk, "f0_nm1"], writes=["f0_et"])
            p.op("vector", lambda e: e.reduce_max(ex1, ET[:, :], axis=AX.X),
                 reads=["f0_et"], writes=["f0_ex1"])
            p.op("vector", lambda e: e.tensor_scalar(out=EQ[:, :], in0=ET[:, :], scalar1=ex1, scalar2=None,
                                                     op0=ALU.is_ge),
                 reads=["f0_et", "f0_ex1"], writes=["f0_eq"])
            p.op("vector", lambda e: e.scalar_tensor_tensor(out=EMK[:, :], in0=EQ[:, :], scalar=-2.0, in1=ET[:, :],
                                                            op0=ALU.mult, op1=ALU.add),
                 reads=["f0_eq", "f0_et"], writes=["f0_emk"])
            p.op("vector", lambda e: e.reduce_max(em2, EMK[:, :], axis=AX.X),
                 reads=["f0_emk"], writes=["f0_em2"])
            p.op("vector", lambda e: e.tensor_tensor(out=den, in0=ex1, in1=em2, op=ALU.add),
                 reads=["f0_ex1", "f0_em2"], writes=["f0_den"])
            p.op("vector", lambda e: e.reciprocal(rden, den), reads=["f0_den"], writes=["f0_rden"])
            p.op("vector", lambda e: e.tensor_scalar(out=KP[:, :], in0=ET[:, :], scalar1=em2, scalar2=None,
                                                     op0=ALU.is_ge),
                 reads=["f0_et", "f0_em2"], writes=["f0_kp"])
            p.op("vector", lambda e: e.tensor_mul(KP[:, :], KP[:, :], ET[:, :]),
                 reads=["f0_kp", "f0_et"], writes=["f0_kp"])
            p.op("vector", lambda e, i=i: e.tensor_scalar(out=COMB[:, i * 8:(i + 1) * 8], in0=KP[:, :],
                                                          scalar1=rden, scalar2=None, op0=ALU.mult),
                 reads=["f0_kp", "f0_rden"], writes=[f"f0_comb{i}"])
        for ex in range(8):
            for c in range(NC):
                ctx.dma(qw, W1[ex, c * 128:(c + 1) * 128, :], W1S[c][:, :], f"f0_w1_{c}")
            for h in range(nH):
                ctx.dma(qw, W2[ex, h * 128:(h + 1) * 128, :], W2S[h][:, :], f"f0_w2_{h}")
            ctx.dma(qw, b2[ex:ex + 1, :], B2R[:, :], "f0_b2r")
            for sl in range(4):
                tsl = slice(sl * 512, (sl + 1) * 512)
                for h in range(nH):
                    ph, phk = ctx.psk(h % 2, (h // 2) % 2)
                    for c in range(NC):
                        p.op("tensor", lambda e, c=c, h=h, ph=ph, tsl=tsl: e.matmul(
                            ph, W1S[c][:, h * 128:(h + 1) * 128], XN[c][:, tsl],
                            start=(c == 0), stop=(c == NC - 1)),
                            reads=[f"f0_w1_{c}", xnk[c]], writes=[phk])
                    p.op("scalar", lambda e, h=h, ph=ph, ex=ex: e.activation(
                        out=H[h][:, :], in_=ph, func=AF.Relu, bias=b1T[:, ex * 16 + h:ex * 16 + h + 1]),
                        reads=[phk, b1T_k], writes=[f"f0_h{h}"])
                for m in range(4):
                    i = sl * 4 + m
                    py = ctx.PS[2 + m % 2][:, :]
                    pyk = [f"PS{2 + m % 2}_0", f"PS{2 + m % 2}_1"]
                    for h in range(nH):
                        for nhalf in range(2):
                            p.op("tensor", lambda e, h=h, m=m, nhalf=nhalf, py=py: e.matmul(
                                py[:, nhalf * 512:(nhalf + 1) * 512],
                                H[h][:, m * 128:(m + 1) * 128],
                                W2S[h][:, nhalf * 512:(nhalf + 1) * 512],
                                start=(h == 0), stop=False),
                                reads=[f"f0_h{h}", f"f0_w2_{h}"], writes=[pyk[nhalf]])
                    for nhalf in range(2):
                        p.op("tensor", lambda e, nhalf=nhalf, py=py: e.matmul(
                            py[:, nhalf * 512:(nhalf + 1) * 512], ctx.ones_row_bf[0:1, :],
                            B2R[0:1, nhalf * 512:(nhalf + 1) * 512], start=False, stop=True),
                            reads=["ones_row_bf", "f0_b2r"], writes=[pyk[nhalf]])
                    cc = COMB[:, i * 8 + ex:i * 8 + ex + 1]
                    if ex == 0:
                        p.op("vector", lambda e, i=i, py=py, cc=cc: e.tensor_scalar(
                            out=ACC[i][:, :], in0=py, scalar1=cc, scalar2=None, op0=ALU.mult),
                            reads=pyk + [f"f0_comb{i}"], writes=[f"f0_acc{i}"])
                    else:
                        p.op("vector", lambda e, i=i, py=py, cc=cc: e.scalar_tensor_tensor(
                            out=ACC[i][:, :], in0=py, scalar=cc, in1=ACC[i][:, :],
                            op0=ALU.mult, op1=ALU.add),
                            reads=pyk + [f"f0_comb{i}", f"f0_acc{i}"], writes=[f"f0_acc{i}"])
        for i in range(NT):
            j = i % 2
            p.op("scalar", lambda e, i=i, j=j: e.activation(out=GS[j][:, :], in_=ACC[i][:, :], func=AF.Silu),
                 reads=[f"f0_acc{i}"], writes=[f"f0_gs{j}"])
            ctx.store(qs, gout[b, i * 128:(i + 1) * 128, :], GS[j][:, :], [f"f0_gs{j}"])



def emit_attn0(ctx, xn, wq, wk, wv, wo, wqb, wkb, wvb, wob, thr, sharp, sout):
    p, qs, qw = ctx.p, ctx.qs, ctx.qw
    TR = ctx.sb("a0_tr", [1, 16], F32)
    SR = ctx.sb("a0_sr", [1, 16], F32)
    TV = ctx.sb("a0_tv", [1, 16], F32)
    NSR = ctx.sb("a0_nsr", [1, 16], F32)
    SHARPC = ctx.sb("a0_shc", [128, 16], F32)
    NSTC = ctx.sb("a0_nst", [128, 16], F32)
    ctx.dma(qs, thr.rearrange("(o e) -> o e", o=1), TR[:, :], "a0_tr")
    ctx.dma(qs, sharp.rearrange("(o e) -> o e", o=1), SR[:, :], "a0_sr")
    p.op("scalar", lambda e: e.activation(out=TV[:, :], in_=TR[:, :], func=AF.Sigmoid),
         reads=["a0_tr"], writes=["a0_tv"])
    p.op("vector", lambda e: e.tensor_mul(NSR[:, :], SR[:, :], TV[:, :]),
         reads=["a0_sr", "a0_tv"], writes=["a0_nsr"])
    p.op("vector", lambda e: e.tensor_scalar(out=NSR[:, :], in0=NSR[:, :], scalar1=-1.0, scalar2=None,
                                             op0=ALU.mult),
         reads=["a0_nsr"], writes=["a0_nsr"])
    pb = ctx.PS[3][:, 0:16]
    p.op("tensor", lambda e: e.matmul(pb[:, 0:16], ctx.ones_row_f32[0:1, :], SR[0:1, :],
                                      start=True, stop=True),
         reads=["ones_row_f32", "a0_sr"], writes=["PS3_0"])
    p.op("vector", lambda e: e.tensor_copy(out=SHARPC[:, :], in_=pb[:, 0:16]),
         reads=["PS3_0"], writes=["a0_shc"])
    p.op("tensor", lambda e: e.matmul(pb[:, 0:16], ctx.ones_row_f32[0:1, :], NSR[0:1, :],
                                      start=True, stop=True),
         reads=["ones_row_f32", "a0_nsr"], writes=["PS3_0"])
    p.op("vector", lambda e: e.tensor_copy(out=NSTC[:, :], in_=pb[:, 0:16]),
         reads=["PS3_0"], writes=["a0_nst"])
    qbc, qbc_k = bias_cols_prep(ctx, wqb.rearrange("(n p) -> n p", p=128), 8, "a0qb")
    kbc, kbc_k = bias_cols_prep(ctx, wkb.rearrange("(n p) -> n p", p=128), 8, "a0kb")
    vbr, vbr_k = bias_row(ctx, wvb.rearrange("(o e) -> o e", o=1), D, "a0vb")
    obr, obr_k = bias_row(ctx, wob.rearrange("(o e) -> o e", o=1), D, "a0ob")
    WB = [ctx.sb(f"a0_wb{c}", [128, D], BF16) for c in range(NC)]
    QF = [ctx.sb(f"a0_qf{c}", [128, T], BF16) for c in range(NC)]
    KF = [ctx.sb(f"a0_kf{c}", [128, T], BF16) for c in range(NC)]
    VT = [ctx.sb(f"a0_vt{i}", [128, D], BF16) for i in range(NT)]
    AOF = [ctx.sb(f"a0_ao{c}", [128, T], BF16) for c in range(NC)]
    RZ = ctx.sb("a0_rz", [1, 1024], F32)
    RZB = ctx.sb("a0_rzb", [1, 1024], BF16)
    RBS = ctx.sb("a0_rbs", [128, 1024], BF16)
    GB = [ctx.sb(f"a0_gb{i}", [128, 1024], BF16) for i in range(4)]
    SO = [ctx.sb(f"a0_so{i}", [128, D], F32) for i in range(2)]

    xnl = XnLoader(ctx, xn, "a0xn")
    XN, xnk = xnl.t, xnl.k
    PT = [XN[kt // 2][:, (kt % 2) * 1024:(kt % 2) * 1024 + 1024] for kt in range(NT)]
    PTK = [xnk[kt // 2] for kt in range(NT)]

    def load_w(wdram, tag):
        for c in range(NC):
            ctx.dma(qw, wdram[c * 128:(c + 1) * 128, :], WB[c][:, :], f"a0_wb{c}")

    sc125, sc125k = ctx.cc(0.125)
    for b in range(NB):
        xnl.load(b)
        for (dst, dk, bias_cols, biask, wdram) in ((QF, "a0_qf", qbc, qbc_k, wq),
                                                   (KF, "a0_kf", kbc, kbc_k, wk)):
            load_w(wdram, dk)
            for dc in range(NC):
                for sl in range(4):
                    pp, ppk = ctx.psk8(dc * 4 + sl)
                    tsl = slice(sl * 512, (sl + 1) * 512)
                    for c in range(NC):
                        p.op("tensor", lambda e, c=c, dc=dc, pp=pp, tsl=tsl: e.matmul(
                            pp, WB[c][:, dc * 128:(dc + 1) * 128], XN[c][:, tsl],
                            start=(c == 0), stop=(c == NC - 1)),
                            reads=[f"a0_wb{c}", xnk[c]], writes=[ppk])
                    p.op("scalar", lambda e, dst=dst, dc=dc, pp=pp, tsl=tsl, bias_cols=bias_cols: e.activation(
                        out=dst[dc][:, tsl], in_=pp, func=AF.Identity,
                        bias=bias_cols[:, dc:dc + 1]),
                        reads=[ppk, biask], writes=[f"{dk}{dc}"])
        load_w(wv, "a0_vt")
        for i in range(NT):
            pp = ctx.PS[i % 4][:, :]
            ppk = [f"PS{i % 4}_0", f"PS{i % 4}_1"]
            for nh in range(2):
                nsl = slice(nh * 512, (nh + 1) * 512)
                for c in range(NC):
                    p.op("tensor", lambda e, c=c, i=i, pp=pp, nsl=nsl: e.matmul(
                        pp[:, nsl], XN[c][:, i * 128:(i + 1) * 128], WB[c][:, nsl],
                        start=(c == 0), stop=False),
                        reads=[xnk[c], f"a0_wb{c}"], writes=[ppk[nh]])
                p.op("tensor", lambda e, nh=nh, pp=pp, nsl=nsl: e.matmul(
                    pp[:, nsl], ctx.ones_row_bf[0:1, :],
                    vbr[0:1, nsl], start=False, stop=True),
                    reads=["ones_row_bf", vbr_k], writes=[ppk[nh]])
            p.op("vector", lambda e, i=i, pp=pp: e.tensor_copy(out=VT[i][:, :], in_=pp),
                 reads=ppk, writes=[f"a0_vt{i}"])
        def emit_st_exp(h, qh, kt):
            hc, hp = h // 2, (h % 2) * 64
            pp = ctx.PS[kt % 2][:, :]
            ppk = [f"PS{kt % 2}_0", f"PS{kt % 2}_1"]
            for nh in range(2):
                p.op("tensor", lambda e, kt=kt, nh=nh, pp=pp, hc=hc, hp=hp, qh=qh: e.matmul(
                    pp[:, nh * 512:(nh + 1) * 512],
                    KF[hc][hp:hp + 64, kt * 128:(kt + 1) * 128],
                    QF[hc][hp:hp + 64, qh * 1024 + nh * 512:qh * 1024 + (nh + 1) * 512],
                    start=True, stop=True),
                    reads=[f"a0_kf{hc}", f"a0_qf{hc}"], writes=[ppk[nh]])
            p.op("scalar", lambda e, kt=kt, pp=pp: e.activation(
                out=PT[kt], in_=pp, func=AF.Exp, scale=sc125),
                reads=ppk + [sc125k], writes=[PTK[kt]])

        def emit_z(kt):
            for nh in range(2):
                p.op("tensor", lambda e, kt=kt, nh=nh: e.matmul(
                    ctx.PS[2][0:1, nh * 512:(nh + 1) * 512],
                    ctx.ones_col_bf[:, 0:1], PT[kt][:, nh * 512:(nh + 1) * 512],
                    start=(kt == 0), stop=(kt == NT - 1)),
                    reads=["ones_col_bf", PTK[kt]], writes=[f"PS2_{nh}"])

        def emit_ph2():
            p.op("vector", lambda e: e.reciprocal(RZ[0:1, :], ctx.PS[2][0:1, :]),
                 reads=["PS2_0", "PS2_1"], writes=["a0_rz"])
            p.op("vector", lambda e: e.tensor_copy(out=RZB[0:1, :], in_=RZ[0:1, :]),
                 reads=["a0_rz"], writes=["a0_rzb"])
            for nh in range(2):
                p.op("tensor", lambda e, nh=nh: e.matmul(
                    ctx.PS[2][:, nh * 512:(nh + 1) * 512], ctx.ones_row_bf[0:1, :],
                    RZB[0:1, nh * 512:(nh + 1) * 512], start=True, stop=True),
                    reads=["ones_row_bf", "a0_rzb"], writes=[f"PS2_{nh}"])
            p.op("scalar", lambda e: e.activation(out=RBS[:, :], in_=ctx.PS[2][:, :], func=AF.Copy),
                 reads=["PS2_0", "PS2_1"], writes=["a0_rbs"])

        def emit_multA(kt):
            p.op("vector", lambda e, kt=kt: e.tensor_mul(PT[kt], PT[kt], RBS[:, :]),
                 reads=[PTK[kt], "a0_rbs"], writes=[PTK[kt]])

        def emit_ph3_step(h, kt):
            jg = kt % 4
            p.op("scalar", lambda e, kt=kt, jg=jg, h=h: e.activation(
                out=GB[jg][:, :], in_=PT[kt], func=AF.Sigmoid,
                bias=NSTC[:, h:h + 1], scale=SHARPC[:, h:h + 1]),
                reads=[PTK[kt], "a0_nst", "a0_shc"], writes=[f"a0_gb{jg}"])
            if kt + 3 < NT:
                emit_multA(kt + 3)
            p.op("vector", lambda e, kt=kt, jg=jg: e.tensor_mul(PT[kt], PT[kt], GB[jg][:, :]),
                 reads=[PTK[kt], f"a0_gb{jg}"], writes=[PTK[kt]])
            for nh in range(2):
                p.op("tensor", lambda e, kt=kt, nh=nh, h=h: e.matmul(
                    ctx.PS[3][0:64, nh * 512:(nh + 1) * 512],
                    VT[kt][:, h * 64:(h + 1) * 64],
                    PT[kt][:, nh * 512:(nh + 1) * 512],
                    start=(kt == 0), stop=(kt == NT - 1)),
                    reads=[PTK[kt], f"a0_vt{kt}"], writes=[f"PS3_{nh}"])

        iters = [(h, qh) for h in range(16) for qh in range(2)]
        for idx, (h, qh) in enumerate(iters):
            hc, hp = h // 2, (h % 2) * 64
            qsl = slice(qh * 1024, (qh + 1) * 1024)
            for kt in range(NT):
                emit_st_exp(h, qh, kt)
                if kt > 0:
                    emit_z(kt - 1)
            emit_z(NT - 1)
            emit_ph2()
            emit_multA(0)
            emit_multA(1)
            emit_multA(2)
            for kt in range(NT):
                emit_ph3_step(h, kt)
            p.op("scalar", lambda e, hc=hc, hp=hp, qsl=qsl: e.activation(
                out=AOF[hc][hp:hp + 64, qsl], in_=ctx.PS[3][0:64, :], func=AF.Copy),
                reads=["PS3_0", "PS3_1"], writes=[f"a0_ao{hc}"])
        load_w(wo, "a0_wo")
        for i in range(NT):
            pp = ctx.PS[i % 4][:, :]
            ppk = [f"PS{i % 4}_0", f"PS{i % 4}_1"]
            for nh in range(2):
                nsl = slice(nh * 512, (nh + 1) * 512)
                for c in range(NC):
                    p.op("tensor", lambda e, c=c, i=i, pp=pp, nsl=nsl: e.matmul(
                        pp[:, nsl], AOF[c][:, i * 128:(i + 1) * 128], WB[c][:, nsl],
                        start=(c == 0), stop=False),
                        reads=[f"a0_ao{c}", f"a0_wb{c}"], writes=[ppk[nh]])
                p.op("tensor", lambda e, pp=pp, nsl=nsl: e.matmul(
                    pp[:, nsl], ctx.ones_row_bf[0:1, :],
                    obr[0:1, nsl], start=False, stop=True),
                    reads=["ones_row_bf", obr_k], writes=[ppk[nh]])
            j = i % 2
            p.op("vector", lambda e, j=j, pp=pp: e.tensor_copy(out=SO[j][:, :], in_=pp),
                 reads=ppk, writes=[f"a0_so{j}"])
            ctx.store(qs, sout[b, i * 128:(i + 1) * 128, :], SO[j][:, :], [f"a0_so{j}"])



def emit_gla(ctx, xn, Wq, bq, Wk, bk, Wv, bv, Wg, bg, Wo, bo, sout, debug=None):
    p, qs, qw = ctx.p, ctx.qs, ctx.qw
    CH = 64
    NCH = T // CH
    K2 = 2
    WQS = [ctx.sb(f"g1_wq{c}", [128, 256], BF16) for c in range(NC)]
    WKS = [ctx.sb(f"g1_wk{c}", [128, 256], BF16) for c in range(NC)]
    WGS = [ctx.sb(f"g1_wg{c}", [128, 256], BF16) for c in range(NC)]
    WVS = [ctx.sb(f"g1_wv{c}", [128, 256], BF16) for c in range(NC)]
    for c in range(NC):
        sl = slice(c * 128, (c + 1) * 128)
        ctx.dma(qw, Wq[sl, :], WQS[c][:, :], f"g1_wq{c}")
        ctx.dma(qw, Wk[sl, :], WKS[c][:, :], f"g1_wk{c}")
        ctx.dma(qw, Wg[sl, :], WGS[c][:, :], f"g1_wg{c}")
        ctx.dma(qw, Wv[sl, :], WVS[c][:, :], f"g1_wv{c}")
    WOS = [ctx.sb(f"g1_wo{v}", [128, 1024], BF16) for v in range(K2)]
    for v in range(K2):
        ctx.dma(qw, Wo[v * 128:(v + 1) * 128, :], WOS[v][:, :], f"g1_wo{v}")
    bqc, bqc_k = bias_cols_prep(ctx, bq.rearrange("(n p) -> n p", p=128), 2, "g1bq")
    bkc, bkc_k = bias_cols_prep(ctx, bk.rearrange("(n p) -> n p", p=128), 2, "g1bk")
    bgc, bgc_k = bias_cols_prep(ctx, bg.rearrange("(n p) -> n p", p=128), 2, "g1bg")
    bvr, bvr_k = bias_row(ctx, bv.rearrange("(o e) -> o e", o=1), 256, "g1bv")
    bor, bor_k = bias_row(ctx, bo.rearrange("(o e) -> o e", o=1), D, "g1bo")
    QFm = [ctx.sb(f"g1_q{kc}", [128, T], BF16) for kc in range(K2)]
    KS = [ctx.sb(f"g1_ks{kc}", [128, T], BF16) for kc in range(K2)]
    GSIG = [ctx.sb(f"g1_gs{kc}", [128, T], F32) for kc in range(K2)]
    LG = [ctx.sb(f"g1_lg{kc}", [128, T], F32) for kc in range(K2)]
    VT = [ctx.sb(f"g1_vt{i}", [64, 256], BF16) for i in range(2 * NT)]
    OF = [ctx.sb(f"g1_of{vc}", [128, T], BF16) for vc in range(K2)]
    S = [ctx.sb(f"g1_s{kc}", [128, 256], F32) for kc in range(K2)]
    SB = [ctx.sb(f"g1_sb{kc}", [128, 256], BF16) for kc in range(K2)]
    Z512 = ctx.sb("g1_z512", [128, 512], F32)
    QT = [ctx.sb(f"g1_qt{kc}", [128, CH], BF16) for kc in range(2 * K2)]
    F2 = [ctx.sb(f"g1_f2{kc}", [128, CH], BF16) for kc in range(2 * K2)]
    KT = [ctx.sb(f"g1_kt{kc}", [128, CH], BF16) for kc in range(2 * K2)]
    KTT = [ctx.sb(f"g1_ktt{i}", [64, 256], BF16) for i in range(2)]
    ATm = ctx.sb("g1_at", [64, 64], BF16)
    EPOS = [ctx.sb(f"g1_ep{kc}", [128, CH], F32) for kc in range(K2)]
    ENEG = [ctx.sb(f"g1_en{kc}", [128, CH], F32) for kc in range(K2)]
    COLS = ctx.sb("g1_cols", [128, 16], F32)
    SO = [ctx.sb(f"g1_so{i}", [128, D], F32) for i in range(2)]
    p.op("vector", lambda e: e.memset(Z512[:, :], 0.0), writes=["g1_z512"])

    xnl = XnLoader(ctx, xn, "g1xn")
    XN, xnk = xnl.t, xnl.k
    neg1, neg1k = ctx.cc(-1.0)

    for b in range(NB):
        xnl.load(b)
        for kc in range(K2):
            p.op("vector", lambda e, kc=kc: e.memset(S[kc][:, :], 0.0), writes=[f"g1_s{kc}"])
        pidx = 0
        for kc in range(K2):
            for sl4 in range(4):
                tsl = slice(sl4 * 512, (sl4 + 1) * 512)
                for (WS, wsk, dstf, dstk, func, bc, bck) in (
                        (WQS, "g1_wq", QFm, "g1_q", AF.Identity, bqc, bqc_k),
                        (WKS, "g1_wk", KS, "g1_ks", AF.Sigmoid, bkc, bkc_k),
                        (WGS, "g1_wg", GSIG, "g1_gs", AF.Sigmoid, bgc, bgc_k)):
                    pp, ppk = ctx.psk8(pidx); pidx += 1
                    for c in range(NC):
                        p.op("tensor", lambda e, c=c, kc=kc, pp=pp, tsl=tsl, WS=WS, wsk=wsk: e.matmul(
                            pp, WS[c][:, kc * 128:(kc + 1) * 128], XN[c][:, tsl],
                            start=(c == 0), stop=(c == NC - 1)),
                            reads=[f"{wsk}{c}", xnk[c]], writes=[ppk])
                    p.op("scalar", lambda e, kc=kc, pp=pp, tsl=tsl, dstf=dstf, func=func, bc=bc: e.activation(
                        out=dstf[kc][:, tsl], in_=pp, func=func, bias=bc[:, kc:kc + 1]),
                        reads=[ppk, bck], writes=[f"{dstk}{kc}"])
        for kc in range(K2):
            p.op("scalar", lambda e, kc=kc: e.activation(out=GSIG[kc][:, :], in_=GSIG[kc][:, :], func=AF.Ln),
                 reads=[f"g1_gs{kc}"], writes=[f"g1_gs{kc}"])
        for i in range(2 * NT):
            pp, ppk = ctx.psk8(i)
            pv = pp[0:64, 0:256]
            for c in range(NC):
                p.op("tensor", lambda e, c=c, i=i, pv=pv: e.matmul(
                    pv, XN[c][:, i * 64:(i + 1) * 64], WVS[c][:, :],
                    start=(c == 0), stop=False),
                    reads=[xnk[c], f"g1_wv{c}"], writes=[ppk])
            p.op("tensor", lambda e, pv=pv: e.matmul(pv, ctx.ones_row_bf[0:1, 0:64], bvr[0:1, :],
                                                     start=False, stop=True),
                 reads=["ones_row_bf", bvr_k], writes=[ppk])
            p.op("vector", lambda e, i=i, pv=pv: e.tensor_copy(out=VT[i][:, :], in_=pv),
                 reads=[ppk], writes=[f"g1_vt{i}"])
        for kc in range(K2):
            for blk in range(4):
                tsl = slice(blk * 512, (blk + 1) * 512)
                init = 0.0 if blk == 0 else LG[kc][:, blk * 512 - 1:blk * 512]
                p.op("vector", lambda e, kc=kc, tsl=tsl, init=init: e.tensor_tensor_scan(
                    out=LG[kc][:, tsl], data0=GSIG[kc][:, tsl], data1=Z512[:, :],
                    initial=init, op0=ALU.add, op1=ALU.add),
                    reads=[f"g1_gs{kc}", "g1_z512", f"g1_lg{kc}"], writes=[f"g1_lg{kc}"])
        def emit_elem(c):
            par = c % 2
            t0 = c * CH
            cols = slice(t0, t0 + CH)
            for kc in range(K2):
                s2 = 2 * par + kc
                lg0 = ctx.zero_col[:, 0:1] if c == 0 else LG[kc][:, t0 - 1:t0]
                lg0k = "zero_col" if c == 0 else f"g1_lg{kc}"
                nl0 = COLS[:, 8 * par + kc:8 * par + kc + 1]
                ecol = COLS[:, 8 * par + 2 + kc:8 * par + 3 + kc]
                dcol = COLS[:, 8 * par + 4 + kc:8 * par + 5 + kc]
                p.op("vector", lambda e, nl0=nl0, lg0=lg0: e.tensor_scalar(
                    out=nl0, in0=lg0, scalar1=-1.0, scalar2=None, op0=ALU.mult),
                    reads=[lg0k], writes=[f"g1_nl0{par}{kc}"])
                p.op("scalar", lambda e, kc=kc, cols=cols, nl0=nl0: e.activation(
                    out=EPOS[kc][:, :], in_=LG[kc][:, cols], func=AF.Exp, bias=nl0),
                    reads=[f"g1_lg{kc}", f"g1_nl0{par}{kc}"], writes=[f"g1_ep{kc}"])
                p.op("scalar", lambda e, kc=kc, cols=cols, lg0=lg0: e.activation(
                    out=ENEG[kc][:, :], in_=LG[kc][:, cols], func=AF.Exp, bias=lg0, scale=neg1),
                    reads=[f"g1_lg{kc}", lg0k, neg1k], writes=[f"g1_en{kc}"])
                p.op("vector", lambda e, kc=kc, t0=t0, dcol=dcol, lg0=lg0: e.tensor_tensor(
                    out=dcol, in0=LG[kc][:, t0 + CH - 1:t0 + CH], in1=lg0, op=ALU.subtract),
                    reads=[f"g1_lg{kc}", lg0k], writes=[f"g1_dc{par}{kc}"])
                p.op("scalar", lambda e, ecol=ecol, dcol=dcol: e.activation(
                    out=ecol, in_=dcol, func=AF.Exp),
                    reads=[f"g1_dc{par}{kc}"], writes=[f"g1_ec{par}{kc}"])
                p.op("vector", lambda e, kc=kc, s2=s2, cols=cols: e.tensor_mul(
                    QT[s2][:, :], QFm[kc][:, cols], EPOS[kc][:, :]),
                    reads=[f"g1_q{kc}", f"g1_ep{kc}"], writes=[f"g1_qt{s2}"])
                p.op("vector", lambda e, kc=kc, s2=s2, cols=cols: e.tensor_mul(
                    F2[s2][:, :], KS[kc][:, cols], ENEG[kc][:, :]),
                    reads=[f"g1_ks{kc}", f"g1_en{kc}"], writes=[f"g1_f2{s2}"])
                p.op("vector", lambda e, s2=s2, ecol=ecol: e.tensor_scalar(
                    out=KT[s2][:, :], in0=F2[s2][:, :], scalar1=ecol, scalar2=None, op0=ALU.mult),
                    reads=[f"g1_f2{s2}", f"g1_ec{par}{kc}"], writes=[f"g1_kt{s2}"])

        def emit_heavy(c):
            par = c % 2
            t0 = c * CH
            cols = slice(t0, t0 + CH)
            vt_t = VT[c]
            vkey = f"g1_vt{c}"
            for kc in range(K2):
                s2 = 2 * par + kc
                ptr = ctx.PS[1].bitcast(BF16)[0:64, kc * 1024:kc * 1024 + 128]
                ptk = f"PS1_{kc}"
                p.op("tensor", lambda e, s2=s2, ptr=ptr: e.transpose(
                    ptr, KT[s2][:, :], ctx.ident_bf[:, :]),
                    reads=[f"g1_kt{s2}", "ident_bf"], writes=[ptk])
                p.op("vector", lambda e, kc=kc, par=par, ptr=ptr: e.tensor_copy(
                    out=KTT[par][0:64, kc * 128:(kc + 1) * 128], in_=ptr),
                    reads=[ptk], writes=[f"g1_ktt{par}"])
            pa = ctx.PS[3][0:64, 0:64]
            for kc in range(K2):
                p.op("tensor", lambda e, s2=2 * par + kc, kc=kc, pa=pa: e.matmul(
                    pa, F2[s2][:, :], QT[s2][:, :], start=(kc == 0), stop=(kc == K2 - 1)),
                    reads=[f"g1_f2{2 * par + kc}", f"g1_qt{2 * par + kc}"], writes=["PS3_0"])
            p.op("vector", lambda e, pa=pa: e.tensor_mul(ATm[0:64, :], pa, ctx.mask64[:, :]),
                 reads=["PS3_0", "mask64"], writes=["g1_at"])
            for kc in range(K2):
                pd = ctx.PS[0][:, kc * 512:kc * 512 + 256]
                pdk = f"PS0_{kc}"
                p.op("tensor", lambda e, kc=kc, par=par, pd=pd, vt_t=vt_t: e.matmul(
                    pd, KTT[par][0:64, kc * 128:(kc + 1) * 128], vt_t[0:64, :],
                    start=True, stop=True),
                    reads=[f"g1_ktt{par}", vkey], writes=[pdk])
            for vc in range(K2):
                po = ctx.PS[2][:, vc * 512:vc * 512 + 64]
                pok = f"PS2_{vc}"
                p.op("tensor", lambda e, vc=vc, po=po, vt_t=vt_t, c=c: e.matmul(
                    po, vt_t[0:64, vc * 128:(vc + 1) * 128], ATm[0:64, :],
                    start=True, stop=(c == 0)),
                    reads=[vkey, "g1_at"], writes=[pok])
                if c > 0:
                    for kc in range(K2):
                        p.op("tensor", lambda e, vc=vc, kc=kc, s2=2 * par + kc, po=po: e.matmul(
                            po, SB[kc][:, vc * 128:(vc + 1) * 128], QT[s2][:, :],
                            start=False, stop=(kc == K2 - 1)),
                            reads=[f"g1_sb{kc}", f"g1_qt{2 * par + kc}"], writes=[pok])
                p.op("scalar", lambda e, vc=vc, po=po, cols=cols: e.activation(
                    out=OF[vc][:, cols], in_=po, func=AF.Copy),
                    reads=[pok], writes=[f"g1_of{vc}"])
            for kc in range(K2):
                pd = ctx.PS[0][:, kc * 512:kc * 512 + 256]
                pdk = f"PS0_{kc}"
                ecol = COLS[:, 8 * par + 2 + kc:8 * par + 3 + kc]
                p.op("vector", lambda e, kc=kc, pd=pd, ecol=ecol: e.scalar_tensor_tensor(
                    out=S[kc][:, :], in0=S[kc][:, :], scalar=ecol, in1=pd,
                    op0=ALU.mult, op1=ALU.add),
                    reads=[f"g1_s{kc}", f"g1_ec{par}{kc}", pdk], writes=[f"g1_s{kc}"])
                p.op("vector", lambda e, kc=kc: e.tensor_copy(out=SB[kc][:, :], in_=S[kc][:, :]),
                     reads=[f"g1_s{kc}"], writes=[f"g1_sb{kc}"])

        emit_elem(0)
        for c in range(NCH):
            if c + 1 < NCH:
                emit_elem(c + 1)
            emit_heavy(c)
            if debug is not None and b == 0 and c < 4:
                for kc in range(K2):
                    ctx.store(qs, debug["S"][c, kc], S[kc][:, :], [f"g1_s{kc}"])
                    ctx.store(qs, debug["QT"][c, kc], QT[kc][:, :], [f"g1_qt{kc}"])
                    ctx.store(qs, debug["F2"][c, kc], F2[kc][:, :], [f"g1_f2{kc}"])
                    ctx.store(qs, debug["KTq"][c, kc], KT[kc][:, :], [f"g1_kt{kc}"])
                allcols = [f"g1_{pfx}{kc}" for pfx in ("nl0", "ec", "dc") for kc in range(K2)]
                ctx.store(qs, debug["COLS"][c], COLS[:, :], allcols)
                ctx.store(qs, debug["AT"][c], ATm[0:64, :], ["g1_at"])
        for i in range(NT):
            pp = ctx.PS[1][:, :]
            ppk = ["PS1_0", "PS1_1"]
            for nh in range(2):
                nsl = slice(nh * 512, (nh + 1) * 512)
                for vc in range(K2):
                    p.op("tensor", lambda e, vc=vc, i=i, pp=pp, nsl=nsl: e.matmul(
                        pp[:, nsl], OF[vc][:, i * 128:(i + 1) * 128], WOS[vc][:, nsl],
                        start=(vc == 0), stop=False),
                        reads=[f"g1_of{vc}", f"g1_wo{vc}"], writes=[ppk[nh]])
                p.op("tensor", lambda e, pp=pp, nsl=nsl: e.matmul(
                    pp[:, nsl], ctx.ones_row_bf[0:1, :], bor[0:1, nsl], start=False, stop=True),
                    reads=["ones_row_bf", bor_k], writes=[ppk[nh]])
            j = i % 2
            p.op("vector", lambda e, j=j, pp=pp: e.tensor_copy(out=SO[j][:, :], in_=pp),
                 reads=ppk, writes=[f"g1_so{j}"])
            ctx.store(qs, sout[b, i * 128:(i + 1) * 128, :], SO[j][:, :], [f"g1_so{j}"])
        if debug is not None:
            for kc in range(K2):
                ctx.store(qs, debug["LG"][b, kc], LG[kc][:, :], [f"g1_lg{kc}"])
            for vc in range(K2):
                ctx.store(qs, debug["OF"][b, vc], OF[vc][:, :], [f"g1_of{vc}"])



def emit_ffn1(ctx, xn, xn32, wi, bi, wo, bo, lora_A, lora_B, gate_w, gate_b,
              base_sp, u_sp, loraBT, gout):
    p, qs, qw = ctx.p, ctx.qs, ctx.qw
    NF = 32
    W64 = [ctx.sb(f"f1_w{dc}", [128, 4096], BF16) for dc in range(NC)]
    wk64 = [f"f1_w{dc}" for dc in range(NC)]
    WSTG = [ctx.sb(f"f1_stg{i}", [128, 1024], BF16) for i in range(2)]
    ATall = ctx.sb("f1_at", [128, 1024], BF16)
    A16 = ctx.sb("f1_a16", [16, 1024], BF16)
    B128 = ctx.sb("f1_b128", [128, 64], BF16)
    BTW = ctx.sb("f1_btw", [16, 4096], BF16)
    GWT = ctx.sb("f1_gwt", [128, 64], F32)
    GW8 = ctx.sb("f1_gw8", [8, 1024], F32)
    gbr, gbr_k = bias_row(ctx, gate_b.rearrange("(o e) -> o e", o=1), 8, "f1gb", dtype=F32)
    biT, biT_k = bias_cols_prep(ctx, bi.rearrange("(n p) -> n p", p=128), 32, "f1bi")
    bor, bor_k = bias_row(ctx, bo.rearrange("(o e) -> o e", o=1), D, "f1bo")
    XT32 = [ctx.sb(f"f1_xt{i}", [128, D], F32) for i in range(2)]
    XF32 = [ctx.sb(f"f1_xf{c}", [128, 128], F32) for c in range(NC)]
    GATES = ctx.sb("f1_gates", [128, 8 * NT * NB], F32)
    SC = ctx.sb("f1_sc", [128, 8], F32)
    m1c, em2, den, rden, nm1, ex1, zc, gv = [SC[:, j:j + 1] for j in range(8)]
    ET = ctx.sb("f1_et", [128, 8], F32)
    EQ = ctx.sb("f1_eq", [128, 8], F32)
    EMK = ctx.sb("f1_emk", [128, 8], F32)
    KP = ctx.sb("f1_kp", [128, 8], F32)
    BB = ctx.sb("f1_bb", [128, T], BF16)
    UTB = ctx.sb("f1_utb", [16, T], BF16)
    BSL = ctx.sb("f1_bsl", [128, 8192], BF16)
    UTe = ctx.sb("f1_ute", [16, 256], BF16)
    HF = [ctx.sb(f"f1_hf{i}", [128, 256], F32) for i in range(2)]
    HB = [ctx.sb(f"f1_hb{i}", [128, 256], BF16) for i in range(4)]
    ACC2 = [ctx.sb(f"f1_acc{i}", [128, D], F32) for i in range(2)]
    GBUF = [ctx.sb(f"f1_gb{i}", [128, D], F32) for i in range(2)]

    xnl = XnLoader(ctx, xn, "f1xn")
    XN, xnk = xnl.t, xnl.k

    def transpose_128(src_ap, src_key, dst_ap, dst_key, pslot):
        bank = pslot % 2
        pt = ctx.PS[3].bitcast(BF16)[:, bank * 1024:bank * 1024 + 128]
        ptk = f"PS3_{bank}"
        p.op("tensor", lambda e: e.transpose(pt, src_ap, ctx.ident_bf[:, :]),
             reads=[src_key, "ident_bf"], writes=[ptk])
        p.op("vector", lambda e: e.tensor_copy(out=dst_ap, in_=pt),
             reads=[ptk], writes=[dst_key])

    for fc in range(NF):
        j = fc % 2
        ctx.dma(qw, wi[fc * 128:(fc + 1) * 128, :], WSTG[j][:, :], f"f1_stg{j}")
        for dc in range(NC):
            transpose_128(WSTG[j][:, dc * 128:(dc + 1) * 128], f"f1_stg{j}",
                          W64[dc][:, fc * 128:(fc + 1) * 128], wk64[dc], fc % 8)
    for ex in range(8):
        ctx.dma(qw, lora_A[ex], A16[:, :], "f1_a16")
        for dc in range(NC):
            bank = dc % 2
            pt = ctx.PS[3].bitcast(BF16)[:, bank * 1024:bank * 1024 + 16]
            ptk = f"PS3_{bank}"
            p.op("tensor", lambda e, dc=dc, pt=pt: e.transpose(
                pt[0:128, 0:16], A16[0:16, dc * 128:(dc + 1) * 128], ctx.ident_bf[0:16, 0:16]),
                reads=["f1_a16", "ident_bf"], writes=[ptk])
            p.op("vector", lambda e, ex=ex, dc=dc, pt=pt: e.tensor_copy(
                out=ATall[:, (ex * 8 + dc) * 16:(ex * 8 + dc) * 16 + 16], in_=pt[0:128, 0:16]),
                reads=[ptk], writes=["f1_at"])
    for ex in range(8):
        for half in range(8):
            ctx.dma(qw, lora_B[ex, half * 512:(half + 1) * 512, :].rearrange("(a p) r -> p a r", p=128),
                    B128.rearrange("p (a r) -> p a r", r=16), "f1_b128")
            for a in range(4):
                bank = a % 2
                pt = ctx.PS[3].bitcast(BF16)[0:16, bank * 1024:bank * 1024 + 128]
                ptk = f"PS3_{bank}"
                p.op("tensor", lambda e, a=a, pt=pt: e.transpose(
                    pt, B128[:, a * 16:(a + 1) * 16], ctx.ident_bf[:, :]),
                    reads=["f1_b128", "ident_bf"], writes=[ptk])
                p.op("vector", lambda e, a=a, pt=pt: e.tensor_copy(
                    out=BTW[0:16, a * 128:(a + 1) * 128], in_=pt),
                    reads=[ptk], writes=["f1_btw"])
            ctx.store(qs, loraBT[ex, :, half * 512:(half + 1) * 512],
                      BTW[0:16, 0:512], ["f1_btw"], writes=[f"f1bt{ex}_{half}"])
    ctx.dma(qs, gate_w, GW8[:, :], "f1_gw8")
    for dc in range(NC):
        pt = ctx.PS[3][:, 512:512 + 8]
        p.op("tensor", lambda e, dc=dc, pt=pt: e.transpose(
            pt[0:128, 0:8], GW8[0:8, dc * 128:(dc + 1) * 128], ctx.ident_f32[0:8, 0:8]),
            reads=["f1_gw8", "ident_f32"], writes=["PS3_1"])
        p.op("vector", lambda e, dc=dc, pt=pt: e.tensor_copy(
            out=GWT[:, dc * 8:dc * 8 + 8], in_=pt[0:128, 0:8]),
            reads=["PS3_1"], writes=["f1_gwt"])

    for b in range(NB):
        xnl.load(b)
        for i in range(NT):
            xti = i % 2
            xtk = f"f1_xt{xti}"
            ctx.dma(qs, xn32[b, i * 128:(i + 1) * 128, :], XT32[xti][:, :], xtk)
            for c in range(NC):
                pt = ctx.PS[3][:, 512 + (c % 4) * 128:512 + (c % 4) * 128 + 128]
                p.op("tensor", lambda e, c=c, xti=xti, pt=pt: e.transpose(
                    pt, XT32[xti][:, c * 128:(c + 1) * 128], ctx.ident_f32[:, :]),
                    reads=[xtk, "ident_f32"], writes=["PS3_1"])
                p.op("vector", lambda e, c=c, pt=pt: e.tensor_copy(out=XF32[c][:, :], in_=pt),
                     reads=["PS3_1"], writes=[f"f1_xf{c}"])
            pg = ctx.PS[3][:, 0:8]
            pgk = "PS3_0"
            for c in range(NC):
                p.op("tensor", lambda e, c=c, pg=pg: e.matmul(
                    pg, XF32[c][:, :], GWT[:, c * 8:c * 8 + 8],
                    start=(c == 0), stop=False),
                    reads=[f"f1_xf{c}", "f1_gwt"], writes=[pgk])
            p.op("tensor", lambda e, pg=pg: e.matmul(pg, ctx.ones_row_f32[0:1, :], gbr[0:1, :],
                                                     start=False, stop=True),
                 reads=["ones_row_f32", gbr_k], writes=[pgk])
            p.op("vector", lambda e, pg=pg: e.reduce_max(m1c, pg, axis=AX.X),
                 reads=[pgk], writes=["f1_m1"])
            p.op("vector", lambda e: e.tensor_scalar(out=nm1, in0=m1c, scalar1=-1.0, scalar2=None, op0=ALU.mult),
                 reads=["f1_m1"], writes=["f1_nm1"])
            p.op("scalar", lambda e, pg=pg: e.activation(out=ET[:, :], in_=pg, func=AF.Exp, bias=nm1,
                                                         accum_out=zc),
                 reads=[pgk, "f1_nm1"], writes=["f1_et", "f1_z"])
            p.op("vector", lambda e: e.reduce_max(ex1, ET[:, :], axis=AX.X),
                 reads=["f1_et"], writes=["f1_ex1"])
            p.op("vector", lambda e: e.tensor_scalar(out=EQ[:, :], in0=ET[:, :], scalar1=ex1, scalar2=None,
                                                     op0=ALU.is_ge),
                 reads=["f1_et", "f1_ex1"], writes=["f1_eq"])
            p.op("vector", lambda e: e.scalar_tensor_tensor(out=EMK[:, :], in0=EQ[:, :], scalar=-2.0, in1=ET[:, :],
                                                            op0=ALU.mult, op1=ALU.add),
                 reads=["f1_eq", "f1_et"], writes=["f1_emk"])
            p.op("vector", lambda e: e.reduce_max(em2, EMK[:, :], axis=AX.X),
                 reads=["f1_emk"], writes=["f1_em2"])
            p.op("vector", lambda e: e.tensor_tensor(out=den, in0=ex1, in1=em2, op=ALU.add),
                 reads=["f1_ex1", "f1_em2"], writes=["f1_den"])
            p.op("vector", lambda e: e.reciprocal(rden, zc), reads=["f1_z"], writes=["f1_rz"])
            p.op("vector", lambda e: e.tensor_mul(gv, den, rden),
                 reads=["f1_den", "f1_rz"], writes=["f1_gv"])
            p.op("vector", lambda e: e.tensor_scalar(out=KP[:, :], in0=ET[:, :], scalar1=em2, scalar2=None,
                                                     op0=ALU.is_ge),
                 reads=["f1_et", "f1_em2"], writes=["f1_kp"])
            p.op("vector", lambda e, b=b, i=i: e.tensor_scalar(
                out=GATES[:, (b * NT + i) * 8:(b * NT + i) * 8 + 8], in0=KP[:, :],
                scalar1=gv, scalar2=None, op0=ALU.mult),
                reads=["f1_kp", "f1_gv"], writes=[f"f1_gate{b}_{i}"])
        for fc in range(NF):
            for sl4 in range(4):
                tsl = slice(sl4 * 512, (sl4 + 1) * 512)
                pp, ppk = ctx.psk8(fc * 4 + sl4)
                for dc in range(NC):
                    p.op("tensor", lambda e, dc=dc, fc=fc, pp=pp, tsl=tsl: e.matmul(
                        pp, W64[dc][:, fc * 128:(fc + 1) * 128], XN[dc][:, tsl],
                        start=(dc == 0), stop=(dc == NC - 1)),
                        reads=[wk64[dc], xnk[dc]], writes=[ppk])
                p.op("scalar", lambda e, fc=fc, pp=pp, tsl=tsl: e.activation(
                    out=BB[:, tsl], in_=pp, func=AF.Identity, bias=biT[:, fc:fc + 1]),
                    reads=[ppk, biT_k], writes=["f1_bb"])
            ctx.store(qs, base_sp[b, fc], BB[:, :], ["f1_bb"], writes=[f"f1base{b}_{fc}"])
        for ex in range(8):
            for sl4 in range(4):
                tsl = slice(sl4 * 512, (sl4 + 1) * 512)
                pu = ctx.PS[2][0:16, 0:512]
                for dc in range(NC):
                    p.op("tensor", lambda e, dc=dc, ex=ex, pu=pu, tsl=tsl: e.matmul(
                        pu, ATall[:, (ex * 8 + dc) * 16:(ex * 8 + dc) * 16 + 16], XN[dc][:, tsl],
                        start=(dc == 0), stop=(dc == NC - 1)),
                        reads=["f1_at", xnk[dc]], writes=["PS2_0"])
                p.op("vector", lambda e, pu=pu, tsl=tsl: e.tensor_copy(out=UTB[0:16, tsl], in_=pu),
                     reads=["PS2_0"], writes=["f1_utb"])
            ctx.store(qs, u_sp[b, ex], UTB[0:16, :], ["f1_utb"], writes=[f"f1u{b}_{ex}"])

    WOT = [(W64[fc // 4][:, (fc % 4) * 1024:(fc % 4) * 1024 + 1024], wk64[fc // 4]) for fc in range(NF)]
    for dc in range(NC):
        j = dc % 2
        for q4 in range(4):
            ctx.dma(qw, wo[dc * 128:(dc + 1) * 128, q4 * 1024:(q4 + 1) * 1024], WSTG[j][:, :], f"f1_stg{j}")
            for fq in range(8):
                fc = q4 * 8 + fq
                dst, dstk = WOT[fc]
                transpose_128(WSTG[j][:, fq * 128:(fq + 1) * 128], f"f1_stg{j}",
                              dst[:, dc * 128:(dc + 1) * 128], dstk, fq)
    for b in range(NB):
        for sl8 in range(8):
            t0 = sl8 * 256
            p.dma(qs, lambda e, b=b, t0=t0: e.dma_start(
                out=BSL.rearrange("p (f t) -> p f t", t=256),
                in_=base_sp[b, :, :, t0:t0 + 256].rearrange("f p t -> p f t")),
                reads=[f"f1base{b}_{fc}" for fc in range(NF)], writes=["f1_bsl"])
            for ex in range(8):
                p.dma(qw, lambda e, ex=ex: e.dma_start(out=BTW[0:16, :], in_=loraBT[ex]),
                      reads=[f"f1bt{ex}_{h2}" for h2 in range(8)], writes=["f1_btw"])
                p.dma(qs, lambda e, b=b, ex=ex, t0=t0: e.dma_start(
                    out=UTe[0:16, :], in_=u_sp[b, ex, :, t0:t0 + 256]),
                    reads=[f"f1u{b}_{ex}"], writes=["f1_ute"])

                def emit_lora(fc):
                    sl4 = fc % 4
                    pl = ctx.PS[2 + sl4 // 2][:, (fc % 2) * 512:(fc % 2) * 512 + 256]
                    plk = f"PS{2 + sl4 // 2}_{fc % 2}"
                    p.op("tensor", lambda e, fc=fc, pl=pl: e.matmul(
                        pl, BTW[0:16, fc * 128:(fc + 1) * 128], UTe[0:16, :],
                        start=True, stop=True),
                        reads=["f1_btw", "f1_ute"], writes=[plk])
                    jf = fc % 2
                    p.op("vector", lambda e, fc=fc, pl=pl, jf=jf: e.tensor_tensor(
                        out=HF[jf][:, :], in0=BSL[:, fc * 256:(fc + 1) * 256], in1=pl, op=ALU.add),
                        reads=["f1_bsl", plk], writes=[f"f1_hf{jf}"])
                    p.op("scalar", lambda e, j=fc % 4, jf=jf: e.activation(
                        out=HB[j][:, :], in_=HF[jf][:, :], func=AF.Relu),
                        reads=[f"f1_hf{jf}"], writes=[f"f1_hb{fc % 4}"])

                emit_lora(0)
                emit_lora(1)
                for fc in range(NF):
                    if fc + 2 < NF:
                        emit_lora(fc + 2)
                    for ts2 in range(2):
                        py = ctx.PS[ts2][:, :]
                        pyk = [f"PS{ts2}_0", f"PS{ts2}_1"]
                        for nh2 in range(2):
                            p.op("tensor", lambda e, fc=fc, j=fc % 4, ts2=ts2, nh2=nh2, py=py: e.matmul(
                                py[:, nh2 * 512:(nh2 + 1) * 512],
                                HB[j][:, ts2 * 128:(ts2 + 1) * 128],
                                WOT[fc][0][:, nh2 * 512:(nh2 + 1) * 512],
                                start=(fc == 0), stop=False),
                                reads=[f"f1_hb{fc % 4}", WOT[fc][1]], writes=[pyk[nh2]])
                for ts2 in range(2):
                    py = ctx.PS[ts2][:, :]
                    pyk = [f"PS{ts2}_0", f"PS{ts2}_1"]
                    for nh2 in range(2):
                        p.op("tensor", lambda e, ts2=ts2, nh2=nh2, py=py: e.matmul(
                            py[:, nh2 * 512:(nh2 + 1) * 512], ctx.ones_row_bf[0:1, :],
                            bor[0:1, nh2 * 512:(nh2 + 1) * 512], start=False, stop=True),
                            reads=["ones_row_bf", bor_k], writes=[pyk[nh2]])
                    i2 = sl8 * 2 + ts2
                    gcol = GATES[:, (b * NT + i2) * 8 + ex:(b * NT + i2) * 8 + ex + 1]
                    gkey = f"f1_gate{b}_{i2}"
                    if ex == 0:
                        p.op("vector", lambda e, ts2=ts2, py=py, gcol=gcol: e.tensor_scalar(
                            out=ACC2[ts2][:, :], in0=py, scalar1=gcol, scalar2=None, op0=ALU.mult),
                            reads=pyk + [gkey], writes=[f"f1_acc{ts2}"])
                    else:
                        p.op("vector", lambda e, ts2=ts2, py=py, gcol=gcol: e.scalar_tensor_tensor(
                            out=ACC2[ts2][:, :], in0=py, scalar=gcol, in1=ACC2[ts2][:, :],
                            op0=ALU.mult, op1=ALU.add),
                            reads=pyk + [gkey, f"f1_acc{ts2}"], writes=[f"f1_acc{ts2}"])
            for ts2 in range(2):
                p.op("scalar", lambda e, ts2=ts2: e.activation(out=GBUF[ts2][:, :], in_=ACC2[ts2][:, :],
                                                               func=AF.Silu),
                     reads=[f"f1_acc{ts2}"], writes=[f"f1_gbuf{ts2}"])
                ctx.store(qs, gout[b, t0 + ts2 * 128:t0 + (ts2 + 1) * 128, :],
                          GBUF[ts2][:, :], [f"f1_gbuf{ts2}"])



N_CORES = 8


def _new_ctx():
    nc = bass.Bass("TRN2", target_bir_lowering=False, debug=False)
    ctx = Ctx(nc)
    return ctx


def build_ln():
    ctx = _new_ctx()
    nc = ctx.nc
    xin = nc.dram_tensor("xin", [NB, T, D], F32, kind="ExternalInput").ap()
    xout = nc.dram_tensor("xout", [NB, T, D], BF16, kind="ExternalOutput").ap()
    xout32 = nc.dram_tensor("xout32", [NB, T, D], F32, kind="ExternalOutput").ap()
    ctx.shared()
    emit_ln(ctx, xin, xout, xout32)
    ctx.p.build_block()
    return ctx.nc


def build_ffn0():
    ctx = _new_ctx()
    nc = ctx.nc
    xn = nc.dram_tensor("xn", [NB, T, D], BF16, kind="ExternalInput").ap()
    xn32 = nc.dram_tensor("xn32", [NB, T, D], F32, kind="ExternalInput").ap()
    Wg = nc.dram_tensor("Wg", [D, 8], F32, kind="ExternalInput").ap()
    bg = nc.dram_tensor("bg", [8], F32, kind="ExternalInput").ap()
    W1 = nc.dram_tensor("W1", [8, D, 2048], F32, kind="ExternalInput").ap()
    b1 = nc.dram_tensor("b1", [8, 2048], F32, kind="ExternalInput").ap()
    W2 = nc.dram_tensor("W2", [8, 2048, D], F32, kind="ExternalInput").ap()
    b2 = nc.dram_tensor("b2", [8, D], F32, kind="ExternalInput").ap()
    g = nc.dram_tensor("g", [NB, T, D], F32, kind="ExternalOutput").ap()
    ctx.shared()
    emit_ffn0(ctx, xn, xn32, Wg, bg, W1, b1, W2, b2, g)
    ctx.p.build_block()
    return ctx.nc


def build_attn0():
    ctx = _new_ctx()
    nc = ctx.nc
    xn = nc.dram_tensor("xn", [NB, T, D], BF16, kind="ExternalInput").ap()
    wq = nc.dram_tensor("wq", [D, D], F32, kind="ExternalInput").ap()
    wk = nc.dram_tensor("wk", [D, D], F32, kind="ExternalInput").ap()
    wv = nc.dram_tensor("wv", [D, D], F32, kind="ExternalInput").ap()
    wo = nc.dram_tensor("wo", [D, D], F32, kind="ExternalInput").ap()
    wqb = nc.dram_tensor("wqb", [D], F32, kind="ExternalInput").ap()
    wkb = nc.dram_tensor("wkb", [D], F32, kind="ExternalInput").ap()
    wvb = nc.dram_tensor("wvb", [D], F32, kind="ExternalInput").ap()
    wob = nc.dram_tensor("wob", [D], F32, kind="ExternalInput").ap()
    thr = nc.dram_tensor("thr", [16], F32, kind="ExternalInput").ap()
    sharp = nc.dram_tensor("sharp", [16], F32, kind="ExternalInput").ap()
    s = nc.dram_tensor("s", [NB, T, D], F32, kind="ExternalOutput").ap()
    ctx.shared()
    emit_attn0(ctx, xn, wq, wk, wv, wo, wqb, wkb, wvb, wob, thr, sharp, s)
    ctx.p.build_block()
    return ctx.nc


def build_gla(debug=False):
    ctx = _new_ctx()
    nc = ctx.nc
    xn = nc.dram_tensor("xn", [NB, T, D], BF16, kind="ExternalInput").ap()
    Wq = nc.dram_tensor("Wq", [D, 256], F32, kind="ExternalInput").ap()
    bq = nc.dram_tensor("bq", [256], F32, kind="ExternalInput").ap()
    Wk = nc.dram_tensor("Wk", [D, 256], F32, kind="ExternalInput").ap()
    bk = nc.dram_tensor("bk", [256], F32, kind="ExternalInput").ap()
    Wv = nc.dram_tensor("Wv", [D, 256], F32, kind="ExternalInput").ap()
    bv = nc.dram_tensor("bv", [256], F32, kind="ExternalInput").ap()
    Wg = nc.dram_tensor("Wg", [D, 256], F32, kind="ExternalInput").ap()
    bg = nc.dram_tensor("bg", [256], F32, kind="ExternalInput").ap()
    Wo = nc.dram_tensor("Wo", [256, D], F32, kind="ExternalInput").ap()
    bo = nc.dram_tensor("bo", [D], F32, kind="ExternalInput").ap()
    s = nc.dram_tensor("s", [NB, T, D], F32, kind="ExternalOutput").ap()
    dbg = None
    if debug:
        dbg = {
            "LG": nc.dram_tensor("dbg_LG", [NB, 2, 128, T], F32, kind="ExternalOutput").ap(),
            "OF": nc.dram_tensor("dbg_OF", [NB, 2, 128, T], BF16, kind="ExternalOutput").ap(),
            "S": nc.dram_tensor("dbg_S", [4, 2, 128, 256], F32, kind="ExternalOutput").ap(),
            "QT": nc.dram_tensor("dbg_QT", [4, 2, 128, 64], BF16, kind="ExternalOutput").ap(),
            "F2": nc.dram_tensor("dbg_F2", [4, 2, 128, 64], BF16, kind="ExternalOutput").ap(),
            "KTq": nc.dram_tensor("dbg_KT", [4, 2, 128, 64], BF16, kind="ExternalOutput").ap(),
            "COLS": nc.dram_tensor("dbg_COLS", [4, 128, 8], F32, kind="ExternalOutput").ap(),
            "AT": nc.dram_tensor("dbg_AT", [4, 64, 64], BF16, kind="ExternalOutput").ap(),
        }
    ctx.shared()
    emit_gla(ctx, xn, Wq, bq, Wk, bk, Wv, bv, Wg, bg, Wo, bo, s, debug=dbg)
    ctx.p.build_block()
    return ctx.nc


def build_ffn1():
    ctx = _new_ctx()
    nc = ctx.nc
    xn = nc.dram_tensor("xn", [NB, T, D], BF16, kind="ExternalInput").ap()
    xn32 = nc.dram_tensor("xn32", [NB, T, D], F32, kind="ExternalInput").ap()
    wi = nc.dram_tensor("wi", [4096, D], F32, kind="ExternalInput").ap()
    bi = nc.dram_tensor("bi", [4096], F32, kind="ExternalInput").ap()
    wo = nc.dram_tensor("wo", [D, 4096], F32, kind="ExternalInput").ap()
    bo = nc.dram_tensor("bo", [D], F32, kind="ExternalInput").ap()
    lora_A = nc.dram_tensor("lora_A", [8, 16, D], F32, kind="ExternalInput").ap()
    lora_B = nc.dram_tensor("lora_B", [8, 4096, 16], F32, kind="ExternalInput").ap()
    gate_w = nc.dram_tensor("gate_w", [8, D], F32, kind="ExternalInput").ap()
    gate_b = nc.dram_tensor("gate_b", [8], F32, kind="ExternalInput").ap()
    g = nc.dram_tensor("g", [NB, T, D], F32, kind="ExternalOutput").ap()
    base_sp = nc.dram_tensor("base_sp", [NB, 32, 128, T], BF16, kind="Internal").ap()
    u_sp = nc.dram_tensor("u_sp", [NB, 8, 16, T], BF16, kind="Internal").ap()
    loraBT = nc.dram_tensor("loraBT", [8, 16, 4096], BF16, kind="Internal").ap()
    ctx.shared()
    emit_ffn1(ctx, xn, xn32, wi, bi, wo, bo, lora_A, lora_B, gate_w, gate_b,
              base_sp, u_sp, loraBT, g)
    ctx.p.build_block()
    return ctx.nc


def build_comb():
    ctx = _new_ctx()
    nc = ctx.nc
    xin = nc.dram_tensor("xin", [NB, T, D], F32, kind="ExternalInput").ap()
    sin = nc.dram_tensor("sin", [NB, T, D], F32, kind="ExternalInput").ap()
    gin = nc.dram_tensor("gin", [NB, T, D], F32, kind="ExternalInput").ap()
    xout = nc.dram_tensor("xout", [NB, T, D], F32, kind="ExternalOutput").ap()
    ctx.shared()
    emit_comb(ctx, xin, sin, gin, xout)
    ctx.p.build_block()
    return ctx.nc


def build_fused():
    ctx = _new_ctx()
    nc = ctx.nc

    def inp(name, shape, dt=F32):
        return nc.dram_tensor(name, shape, dt, kind="ExternalInput").ap()

    X = inp("L0_attn_x", [NB, T, D])
    wq = inp("L0_attn_wq_w", [D, D]); wqb = inp("L0_attn_wq_b", [D])
    wk = inp("L0_attn_wk_w", [D, D]); wkb = inp("L0_attn_wk_b", [D])
    wv = inp("L0_attn_wv_w", [D, D]); wvb = inp("L0_attn_wv_b", [D])
    wo = inp("L0_attn_wo_w", [D, D]); wob = inp("L0_attn_wo_b", [D])
    thr = inp("L0_attn_learned_threshold", [16])
    sharp = inp("L0_attn_gate_sharpness", [16])
    Wg0 = inp("L0_ffn_Wg", [D, 8]); bg0 = inp("L0_ffn_bg", [8])
    W10 = inp("L0_ffn_W1", [8, D, 2048]); b10 = inp("L0_ffn_b1", [8, 2048])
    W20 = inp("L0_ffn_W2", [8, 2048, D]); b20 = inp("L0_ffn_b2", [8, D])
    Wq1 = inp("L1_attn_Wq", [D, 256]); bq1 = inp("L1_attn_bq", [256])
    Wk1 = inp("L1_attn_Wk", [D, 256]); bk1 = inp("L1_attn_bk", [256])
    Wv1 = inp("L1_attn_Wv", [D, 256]); bv1 = inp("L1_attn_bv", [256])
    Wg1 = inp("L1_attn_Wg", [D, 256]); bg1 = inp("L1_attn_bg", [256])
    Wo1 = inp("L1_attn_Wo", [256, D]); bo1 = inp("L1_attn_bo", [D])
    wi1 = inp("L1_ffn_wi", [4096, D]); bi1 = inp("L1_ffn_bi", [4096])
    wo1 = inp("L1_ffn_wo", [D, 4096]); boo1 = inp("L1_ffn_bo", [D])
    lA1 = inp("L1_ffn_lora_A", [8, 16, D]); lB1 = inp("L1_ffn_lora_B", [8, 4096, 16])
    gw1 = inp("L1_ffn_gate_w", [8, D]); gb1 = inp("L1_ffn_gate_b", [8])
    OUT = nc.dram_tensor("out", [NB, T, D], F32, kind="ExternalOutput").ap()

    scr = lambda name, shape, dt: nc.dram_tensor(name, shape, dt, kind="Internal").ap()
    xn0 = scr("xn0", [NB, T, D], BF16)
    xn0_32 = scr("xn0_32", [NB, T, D], F32)
    g_sp = scr("g_sp", [NB, T, D], F32)
    s_sp = scr("s_sp", [NB, T, D], F32)
    x1_sp = scr("x1_sp", [NB, T, D], F32)
    xn1 = scr("xn1", [NB, T, D], BF16)
    xn1_32 = scr("xn1_32", [NB, T, D], F32)
    base_sp = scr("base_sp", [NB, 32, 128, T], BF16)
    u_sp = scr("u_sp", [NB, 8, 16, T], BF16)
    loraBT = scr("loraBT", [8, 16, 4096], BF16)

    ctx.shared()

    def stage(fn, *args, **kw):
        ctx.push_scope()
        fn(ctx, *args, **kw)
        ctx.pop_scope()

    stage(emit_ln, X, xn0, xn0_32)
    stage(emit_ffn0, xn0, xn0_32, Wg0, bg0, W10, b10, W20, b20, g_sp)
    stage(emit_attn0, xn0, wq, wk, wv, wo, wqb, wkb, wvb, wob, thr, sharp, s_sp)
    stage(emit_comb_ln, X, s_sp, g_sp, x1_sp, xn1, xn1_32)
    stage(emit_gla, xn1, Wq1, bq1, Wk1, bk1, Wv1, bv1, Wg1, bg1, Wo1, bo1, s_sp)
    stage(emit_ffn1, xn1, xn1_32, wi1, bi1, wo1, boo1, lA1, lB1, gw1, gb1,
          base_sp, u_sp, loraBT, g_sp)
    stage(emit_comb, x1_sp, s_sp, g_sp, OUT)
    ctx.p.build_block()
    return ctx.nc


def kernel_fused(**inputs):
    x = np.ascontiguousarray(inputs["L0_attn_x"])
    nc = _get_nc("fused", build_fused)
    wnames = [n for n in inputs if n != "L0_attn_x"]
    in_maps = []
    for c in range(N_CORES):
        m = {n: inputs[n] for n in wnames}
        m["L0_attn_x"] = x[2 * c:2 * c + 2]
        in_maps.append(m)
    res = run_bass_kernel_spmd(nc, in_maps, core_ids=list(range(N_CORES)))
    return np.concatenate([np.asarray(r["out"]) for r in res.results], axis=0)


_NC_CACHE = {}


def _get_nc(name, builder):
    if name not in _NC_CACHE:
        _NC_CACHE[name] = builder()
    return _NC_CACHE[name]


def kernel(**inputs):
    return kernel_fused(**inputs)


def kernel_unfused(**inputs):
    ins = inputs
    x = np.ascontiguousarray(ins["L0_attn_x"])
    xs = [x[2 * c:2 * c + 2] for c in range(N_CORES)]
    cores = list(range(N_CORES))

    def launch(name, builder, per_core, shared):
        nc = _get_nc(name, builder)
        in_maps = [dict(per_core[c], **shared) for c in range(N_CORES)]
        res = run_bass_kernel_spmd(nc, in_maps, core_ids=cores)
        return res.results

    r = launch("ln", build_ln, [{"xin": xs[c]} for c in range(N_CORES)], {})
    xn0 = [r[c]["xout"] for c in range(N_CORES)]
    xn0_32 = [r[c]["xout32"] for c in range(N_CORES)]

    r = launch("ffn0", build_ffn0,
               [{"xn": xn0[c], "xn32": xn0_32[c]} for c in range(N_CORES)],
               {"Wg": ins["L0_ffn_Wg"], "bg": ins["L0_ffn_bg"],
                "W1": ins["L0_ffn_W1"], "b1": ins["L0_ffn_b1"],
                "W2": ins["L0_ffn_W2"], "b2": ins["L0_ffn_b2"]})
    g0 = [r[c]["g"] for c in range(N_CORES)]

    r = launch("attn0", build_attn0,
               [{"xn": xn0[c]} for c in range(N_CORES)],
               {"wq": ins["L0_attn_wq_w"], "wk": ins["L0_attn_wk_w"],
                "wv": ins["L0_attn_wv_w"], "wo": ins["L0_attn_wo_w"],
                "wqb": ins["L0_attn_wq_b"], "wkb": ins["L0_attn_wk_b"],
                "wvb": ins["L0_attn_wv_b"], "wob": ins["L0_attn_wo_b"],
                "thr": ins["L0_attn_learned_threshold"],
                "sharp": ins["L0_attn_gate_sharpness"]})
    s0 = [r[c]["s"] for c in range(N_CORES)]

    r = launch("comb", build_comb,
               [{"xin": xs[c], "sin": s0[c], "gin": g0[c]} for c in range(N_CORES)], {})
    x1 = [r[c]["xout"] for c in range(N_CORES)]

    r = launch("ln", build_ln, [{"xin": x1[c]} for c in range(N_CORES)], {})
    xn1 = [r[c]["xout"] for c in range(N_CORES)]
    xn1_32 = [r[c]["xout32"] for c in range(N_CORES)]

    r = launch("gla", build_gla,
               [{"xn": xn1[c]} for c in range(N_CORES)],
               {"Wq": ins["L1_attn_Wq"], "bq": ins["L1_attn_bq"],
                "Wk": ins["L1_attn_Wk"], "bk": ins["L1_attn_bk"],
                "Wv": ins["L1_attn_Wv"], "bv": ins["L1_attn_bv"],
                "Wg": ins["L1_attn_Wg"], "bg": ins["L1_attn_bg"],
                "Wo": ins["L1_attn_Wo"], "bo": ins["L1_attn_bo"]})
    s1 = [r[c]["s"] for c in range(N_CORES)]

    r = launch("ffn1", build_ffn1,
               [{"xn": xn1[c], "xn32": xn1_32[c]} for c in range(N_CORES)],
               {"wi": ins["L1_ffn_wi"], "bi": ins["L1_ffn_bi"],
                "wo": ins["L1_ffn_wo"], "bo": ins["L1_ffn_bo"],
                "lora_A": ins["L1_ffn_lora_A"], "lora_B": ins["L1_ffn_lora_B"],
                "gate_w": ins["L1_ffn_gate_w"], "gate_b": ins["L1_ffn_gate_b"]})
    g1 = [r[c]["g"] for c in range(N_CORES)]

    r = launch("comb", build_comb,
               [{"xin": x1[c], "sin": s1[c], "gin": g1[c]} for c in range(N_CORES)], {})
    out = np.concatenate([np.asarray(r[c]["xout"]) for c in range(N_CORES)], axis=0)
    return out


# version 14 (final)
# speedup vs baseline: 1.0420x; 1.0123x over previous
import numpy as np
import ml_dtypes
from contextlib import ExitStack

import concourse.bass as bass
import concourse.mybir as mybir
from concourse.bass_utils import run_bass_kernel_spmd

F32 = mybir.dt.float32
BF16 = mybir.dt.bfloat16
AF = mybir.ActivationFunctionType
ALU = mybir.AluOpType
AX = mybir.AxisListType

ENGINES = ("tensor", "scalar", "vector", "gpsimd", "sync")
SAME_ENGINE_SYNC = True
NOSYNC_ENGINES = ("tensor",)

NB = 2
T = 2048
D = 1024
NT = T // 128
NC = D // 128



class DmaQueue:
    def __init__(self, prog, name, issue_engine, nlanes=8):
        self.prog = prog
        self.name = name
        self.eng = issue_engine
        self.nlanes = nlanes
        self.sems = [prog.nc.alloc_semaphore(f"dq_{name}_{i}") for i in range(nlanes)]
        self.lane_cnt = [0] * nlanes
        self.n = 0

    def key(self, lane):
        return f"__dmaq_{self.name}_{lane}"


class Prog:
    def __init__(self, nc, same_engine_sync=True):
        self.nc = nc
        self.sems = {e: nc.alloc_semaphore(f"S_{e}") for e in ENGINES}
        self.cnt = {e: 0 for e in ENGINES}
        self.items = {e: [] for e in ENGINES}
        self.writer = {}
        self.readers = {}
        self.frontier = {e: {} for e in ENGINES}
        self.same_sync = same_engine_sync
        self.queues = {}
        self.nops = 0

    def queue(self, name, issue_engine, nlanes=8):
        q = DmaQueue(self, name, issue_engine, nlanes)
        self.queues[name] = q
        return q

    def _src_id(self, src):
        return ("eng", src[1]) if src[0] == "eng" else ("dma", src[1], src[2])

    def _sem_of(self, sid):
        return self.sems[sid[1]] if sid[0] == "eng" else self.queues[sid[1]].sems[sid[2]]

    def _collect_waits(self, eng, reads, writes):
        waits = {}

        def need(src):
            cnt = src[-1]
            sid = self._src_id(src)
            if sid == ("eng", eng) and (not self.same_sync or eng in NOSYNC_ENGINES):
                return
            if self.frontier[eng].get(sid, 0) >= cnt:
                return
            old = waits.get(sid)
            if old is None or old < cnt:
                waits[sid] = cnt

        for k in reads:
            w = self.writer.get(k)
            if w is not None:
                need(w)
        for k in writes:
            w = self.writer.get(k)
            if w is not None:
                need(w)
            for sid, cnt in self.readers.get(k, {}).items():
                need((sid[0],) + sid[1:] + (cnt,))
        out = []
        for sid, val in waits.items():
            self.frontier[eng][sid] = val
            out.append((sid, val))
        return out

    def op(self, eng, fn, reads=(), writes=()):
        waits = self._collect_waits(eng, reads, writes)
        self.cnt[eng] += 1
        my = self.cnt[eng]
        self.items[eng].append((waits, fn, ("eng", eng), 1))
        me = ("eng", eng)
        for k in reads:
            self.readers.setdefault(k, {})[me] = my
        for k in writes:
            self.writer[k] = ("eng", eng, my)
            self.readers[k] = {}
        self.nops += 1
        return ("eng", eng, my)

    def dma(self, q, fn, reads=(), writes=()):
        eng = q.eng
        lane = q.n % q.nlanes
        reads = list(reads) + [q.key(lane)]
        waits = self._collect_waits(eng, reads, writes)
        q.n += 1
        q.lane_cnt[lane] += 16
        my = q.lane_cnt[lane]
        self.items[eng].append((waits, fn, ("dma", q.name, lane), 16))
        src = ("dma", q.name, lane, my)
        me = self._src_id(src)
        for k in reads:
            self.readers.setdefault(k, {})[me] = my
        self.writer[q.key(lane)] = src
        self.readers[q.key(lane)] = {}
        for k in writes:
            self.writer[k] = src
            self.readers[k] = {}
        self.nops += 1
        return src

    def barrier(self):
        targets = [(("eng", e), self.sems[e], self.cnt[e]) for e in ENGINES if self.cnt[e] > 0]
        for q in self.queues.values():
            for lane in range(q.nlanes):
                if q.lane_cnt[lane] > 0:
                    targets.append((("dma", q.name, lane), q.sems[lane], q.lane_cnt[lane]))
        for e in ENGINES:
            waits = []
            for sid, sem, val in targets:
                if sid == ("eng", e):
                    continue
                if self.frontier[e].get(sid, 0) >= val:
                    continue
                self.frontier[e][sid] = val
                waits.append((sid, val))
            if waits:
                self.items[e].append((waits, None, None, 0))

    def finish(self):
        eng = "gpsimd"
        for q in self.queues.values():
            for lane in range(q.nlanes):
                if q.lane_cnt[lane] > 0:
                    sid = ("dma", q.name, lane)
                    if self.frontier[eng].get(sid, 0) >= q.lane_cnt[lane]:
                        continue
                    self.items[eng].append(
                        ([(sid, q.lane_cnt[lane])], None, None, 0))
                    self.frontier[eng][sid] = q.lane_cnt[lane]

    def _prepare(self):
        pass

    def _run(self, eng_obj, eng):
        for waits, fn, dst, inc in self.items[eng]:
            for sid, wval in waits:
                if sid[0] == "eng":
                    eng_obj.wait_ge(self.sems[sid[1]], wval)
                else:
                    eng_obj.wait_ge(self.queues[sid[1]].sems[sid[2]], wval)
            if fn is not None:
                inst = fn(eng_obj)
                if dst[0] == "eng":
                    inst.then_inc(self.sems[eng], 1)
                else:
                    inst.then_inc(self.queues[dst[1]].sems[dst[2]], 16)

    def build_block(self):
        self.finish()
        self._prepare()
        nc = self.nc
        with nc.Block() as block:
            @block.sync
            def _(e):
                self._run(e, "sync")

            @block.tensor
            def _(e):
                self._run(e, "tensor")

            @block.scalar
            def _(e):
                self._run(e, "scalar")

            @block.vector
            def _(e):
                self._run(e, "vector")

            @block.gpsimd
            def _(e):
                self._run(e, "gpsimd")


class Ctx:

    def __init__(self, nc):
        self.nc = nc
        self.p = Prog(nc, same_engine_sync=SAME_ENGINE_SYNC)
        self.qs = self.p.queue("sp", "sync", nlanes=8)
        self.qw = self.p.queue("gw", "gpsimd", nlanes=8)
        self.st = ExitStack()
        self._scopes = []
        self._uid = 0

    def sb(self, name, shape, dtype):
        return self.st.enter_context(self.nc.sbuf_tensor(f"{name}_{self.uid()}", shape, dtype))

    def push_scope(self):
        self._scopes.append(self.st)
        self.st = ExitStack()

    def pop_scope(self):
        self.st.close()
        self.st = self._scopes.pop()
        self.p.barrier()

    def ps(self, name, shape, dtype=F32):
        return self.st.enter_context(self.nc.psum_tensor(name, shape, dtype))

    def uid(self):
        self._uid += 1
        return self._uid

    def shared(self):
        nc = self.nc
        self.ident_bf = self.sb("ident_bf", [128, 128], BF16)
        self.ident_f32 = self.sb("ident_f32", [128, 128], F32)
        self.mask64 = self.sb("mask64", [64, 64], F32)
        self.ones_col_bf = self.sb("ones_col_bf", [128, 1], BF16)
        self.ones_row_bf = self.sb("ones_row_bf", [1, 128], BF16)
        self.ones_row_f32 = self.sb("ones_row_f32", [1, 128], F32)
        self.zero_col = self.sb("zero_col", [128, 1], F32)
        ident_np = np.eye(128, dtype=ml_dtypes.bfloat16)
        identf_np = np.eye(128, dtype=np.float32)
        mask_np = np.triu(np.ones((64, 64), dtype=np.float32))
        d_ident = nc.inline_tensor(ident_np, "c_ident").ap()
        d_identf = nc.inline_tensor(identf_np, "c_identf").ap()
        d_mask = nc.inline_tensor(mask_np, "c_mask").ap()
        p, qs = self.p, self.qs
        self.dma(qs, d_ident, self.ident_bf[:, :], "ident_bf")
        self.dma(qs, d_identf, self.ident_f32[:, :], "ident_f32")
        self.dma(qs, d_mask, self.mask64[:, :], "mask64")
        p.op("vector", lambda e: e.memset(self.ones_col_bf[:, :], 1.0), writes=["ones_col_bf"])
        p.op("vector", lambda e: e.memset(self.ones_row_bf[:, :], 1.0), writes=["ones_row_bf"])
        p.op("vector", lambda e: e.memset(self.ones_row_f32[:, :], 1.0), writes=["ones_row_f32"])
        p.op("vector", lambda e: e.memset(self.zero_col[:, :], 0.0), writes=["zero_col"])
        self.cconst = self.sb("cconst", [128, 4], F32)
        self._cmap = {}
        for j, v in enumerate([1e-5, -1.0, 0.125, 0.5]):
            p.op("vector", lambda e, j=j, v=v: e.memset(self.cconst[:, j:j + 1], v),
                 writes=[f"cconst{j}"])
            self._cmap[v] = j
        self.PS = [self.ps(f"PSAR{i}", [128, 1024], F32) for i in range(4)]

    def cc(self, val, parts=128):
        j = self._cmap[val]
        return self.cconst[0:parts, j:j + 1], f"cconst{j}"

    def psk(self, i, half):
        return self.PS[i][:, half * 512:(half + 1) * 512], f"PS{i}_{half}"

    def psk8(self, idx):
        return self.psk(idx % 4, (idx // 4) % 2)

    def dma(self, q, src_ap, dst_ap, key, reads=(), transpose=False):
        if transpose:
            return self.p.dma(q, lambda e, o=dst_ap, i=src_ap: e.dma_start(out=o, in_=i, transpose=True),
                              reads=reads, writes=[key])
        return self.p.dma(q, lambda e, o=dst_ap, i=src_ap: e.dma_start(out=o, in_=i),
                          reads=reads, writes=[key])

    def store(self, q, dst_ap, src_ap, reads, writes=()):
        return self.p.dma(q, lambda e, o=dst_ap, i=src_ap: e.dma_start(out=o, in_=i),
                          reads=reads, writes=writes)



class XnLoader:

    def __init__(self, ctx, xn_dram, name):
        self.ctx = ctx
        self.dram = xn_dram
        self.t = [ctx.sb(f"{name}{c}", [128, T], BF16) for c in range(NC)]
        self.k = [f"{name}{c}" for c in range(NC)]

    def load(self, b):
        for c in range(NC):
            src = self.dram[b, :, c * 128:(c + 1) * 128]
            self.ctx.dma(self.ctx.qs, src, self.t[c][:, :], self.k[c], transpose=True)


def bias_cols_prep(ctx, dram_vec, n, name):
    stage = ctx.sb(f"bcs_{name}", [n, 128], F32)
    out = ctx.sb(f"bc_{name}", [128, n], F32)
    kst, kout = f"bcs_{name}", f"bc_{name}"
    ctx.dma(ctx.qs, dram_vec, stage[:, :], kst)
    psum = ctx.PS[3].bitcast(F32)[:, 0:128]
    ctx.p.op("tensor", lambda e: e.transpose(psum[0:128, 0:n], stage[:, :], ctx.ident_f32[0:n, 0:n]),
             reads=[kst, "ident_f32"], writes=["PS3_0"])
    ctx.p.op("vector", lambda e: e.tensor_copy(out=out[:, :], in_=psum[0:128, 0:n]),
             reads=["PS3_0"], writes=[kout])
    return out, kout


def bias_row(ctx, dram_vec, n, name, dtype=BF16):
    t = ctx.sb(f"br_{name}", [1, n], dtype)
    ctx.dma(ctx.qw if dtype == BF16 else ctx.qs, dram_vec, t[:, :], f"br_{name}")
    return t, f"br_{name}"



def emit_ln(ctx, xin, xout, xout32=None):
    p, qs = ctx.p, ctx.qs
    XT = [ctx.sb(f"ln_x{i}", [128, D], F32) for i in range(3)]
    XN = [ctx.sb(f"ln_o{i}", [128, D], BF16) for i in range(2)]
    XN32 = [ctx.sb(f"ln_p{i}", [128, D], F32) for i in range(2)] if xout32 is not None else None
    SQ = ctx.sb("ln_sq", [128, D], F32)
    C = ctx.sb("ln_c", [128, 8], F32)
    s1, s2, m, e2, nv, sd, r, nb = [C[:, j:j + 1] for j in range(8)]
    for b in range(NB):
        for i in range(NT):
            xk = f"ln_x{(b * NT + i) % 3}"
            ok = f"ln_o{(b * NT + i) % 2}"
            xt = XT[(b * NT + i) % 3]
            xo = XN[(b * NT + i) % 2]
            ctx.dma(qs, xin[b, i * 128:(i + 1) * 128, :], xt[:, :], xk)
            p.op("scalar", lambda e, xt=xt: e.activation(out=SQ[:, :], in_=xt[:, :], func=AF.Square,
                                                         accum_out=s2),
                 reads=[xk], writes=["ln_sq", "ln_s2"])
            p.op("vector", lambda e, xt=xt: e.reduce_sum(s1, xt[:, :], axis=AX.X),
                 reads=[xk], writes=["ln_s1"])
            p.op("vector", lambda e: e.tensor_scalar(out=m, in0=s1, scalar1=1.0 / D, scalar2=None, op0=ALU.mult),
                 reads=["ln_s1"], writes=["ln_m"])
            p.op("vector", lambda e: e.tensor_scalar(out=e2, in0=s2, scalar1=1.0 / D, scalar2=None, op0=ALU.mult),
                 reads=["ln_s2"], writes=["ln_e2"])
            p.op("vector", lambda e: e.scalar_tensor_tensor(out=nv, in0=m, scalar=m, in1=e2,
                                                            op0=ALU.mult, op1=ALU.subtract),
                 reads=["ln_m", "ln_e2"], writes=["ln_nv"])
            eps_ap, eps_k = ctx.cc(1e-5)
            neg1_ap, neg1_k = ctx.cc(-1.0)
            p.op("scalar", lambda e: e.activation(out=sd, in_=nv, func=AF.Sqrt, bias=eps_ap, scale=neg1_ap),
                 reads=["ln_nv", eps_k, neg1_k], writes=["ln_sd"])
            p.op("vector", lambda e: e.reciprocal(r, sd), reads=["ln_sd"], writes=["ln_r"])
            p.op("vector", lambda e: e.scalar_tensor_tensor(out=nb, in0=m, scalar=-1.0, in1=r,
                                                            op0=ALU.mult, op1=ALU.mult),
                 reads=["ln_m", "ln_r"], writes=["ln_nb"])
            p.op("scalar", lambda e, xt=xt, xo=xo: e.activation(out=xo[:, :], in_=xt[:, :], func=AF.Identity,
                                                                bias=nb, scale=r),
                 reads=[xk, "ln_nb", "ln_r"], writes=[ok])
            ctx.store(qs, xout[b, i * 128:(i + 1) * 128, :], xo[:, :], [ok])
            if xout32 is not None:
                ok32 = f"ln_p{(b * NT + i) % 2}"
                xo32 = XN32[(b * NT + i) % 2]
                p.op("vector", lambda e, xt=xt, xo32=xo32: e.tensor_scalar(
                    out=xo32[:, :], in0=xt[:, :], scalar1=r, scalar2=nb, op0=ALU.mult, op1=ALU.add),
                    reads=[xk, "ln_nb", "ln_r"], writes=[ok32])
                ctx.store(qs, xout32[b, i * 128:(i + 1) * 128, :], xo32[:, :], [ok32])



def emit_comb(ctx, xin, sin, gin, xout):
    p, qs = ctx.p, ctx.qs
    BX = [ctx.sb(f"cb_x{i}", [128, D], F32) for i in range(2)]
    BS = [ctx.sb(f"cb_s{i}", [128, D], F32) for i in range(2)]
    BG = [ctx.sb(f"cb_g{i}", [128, D], F32) for i in range(2)]
    BO = [ctx.sb(f"cb_o{i}", [128, D], F32) for i in range(2)]
    for b in range(NB):
        for i in range(NT):
            j = (b * NT + i) % 2
            kx, ks, kg, ko = f"cb_x{j}", f"cb_s{j}", f"cb_g{j}", f"cb_o{j}"
            sl = slice(i * 128, (i + 1) * 128)
            ctx.dma(qs, xin[b, sl, :], BX[j][:, :], kx)
            ctx.dma(qs, sin[b, sl, :], BS[j][:, :], ks)
            ctx.dma(qs, gin[b, sl, :], BG[j][:, :], kg)
            p.op("vector", lambda e, j=j: e.tensor_mul(BO[j][:, :], BS[j][:, :], BG[j][:, :]),
                 reads=[ks, kg], writes=[ko])
            p.op("vector", lambda e, j=j: e.tensor_add(BO[j][:, :], BO[j][:, :], BX[j][:, :]),
                 reads=[ko, kx], writes=[ko])
            ctx.store(qs, xout[b, sl, :], BO[j][:, :], [ko])


def emit_comb_ln(ctx, xin, sin, gin, x1out, xnout, xnout32):
    p, qs = ctx.p, ctx.qs
    BX = [ctx.sb(f"cl_x{i}", [128, D], F32) for i in range(2)]
    BS = [ctx.sb(f"cl_s{i}", [128, D], F32) for i in range(2)]
    BG = [ctx.sb(f"cl_g{i}", [128, D], F32) for i in range(2)]
    BO = [ctx.sb(f"cl_o{i}", [128, D], F32) for i in range(2)]
    XN = [ctx.sb(f"cl_n{i}", [128, D], BF16) for i in range(2)]
    XN32 = [ctx.sb(f"cl_p{i}", [128, D], F32) for i in range(2)]
    SQ = ctx.sb("cl_sq", [128, D], F32)
    C = ctx.sb("cl_c", [128, 8], F32)
    s1, s2, m, e2, nv, sd, r, nb = [C[:, j:j + 1] for j in range(8)]
    eps_ap, eps_k = ctx.cc(1e-5)
    neg1_ap, neg1_k = ctx.cc(-1.0)
    for b in range(NB):
        for i in range(NT):
            j = (b * NT + i) % 2
            kx, ks, kg, ko = f"cl_x{j}", f"cl_s{j}", f"cl_g{j}", f"cl_o{j}"
            kn, kp = f"cl_n{j}", f"cl_p{j}"
            sl = slice(i * 128, (i + 1) * 128)
            ctx.dma(qs, xin[b, sl, :], BX[j][:, :], kx)
            ctx.dma(qs, sin[b, sl, :], BS[j][:, :], ks)
            ctx.dma(qs, gin[b, sl, :], BG[j][:, :], kg)
            p.op("vector", lambda e, j=j: e.tensor_mul(BO[j][:, :], BS[j][:, :], BG[j][:, :]),
                 reads=[ks, kg], writes=[ko])
            p.op("vector", lambda e, j=j: e.tensor_add(BO[j][:, :], BO[j][:, :], BX[j][:, :]),
                 reads=[ko, kx], writes=[ko])
            ctx.store(qs, x1out[b, sl, :], BO[j][:, :], [ko])
            p.op("scalar", lambda e, j=j: e.activation(out=SQ[:, :], in_=BO[j][:, :], func=AF.Square,
                                                       accum_out=s2),
                 reads=[ko], writes=["cl_sq", "cl_s2"])
            p.op("vector", lambda e, j=j: e.reduce_sum(s1, BO[j][:, :], axis=AX.X),
                 reads=[ko], writes=["cl_s1"])
            p.op("vector", lambda e: e.tensor_scalar(out=m, in0=s1, scalar1=1.0 / D, scalar2=None, op0=ALU.mult),
                 reads=["cl_s1"], writes=["cl_m"])
            p.op("vector", lambda e: e.tensor_scalar(out=e2, in0=s2, scalar1=1.0 / D, scalar2=None, op0=ALU.mult),
                 reads=["cl_s2"], writes=["cl_e2"])
            p.op("vector", lambda e: e.scalar_tensor_tensor(out=nv, in0=m, scalar=m, in1=e2,
                                                            op0=ALU.mult, op1=ALU.subtract),
                 reads=["cl_m", "cl_e2"], writes=["cl_nv"])
            p.op("scalar", lambda e: e.activation(out=sd, in_=nv, func=AF.Sqrt, bias=eps_ap, scale=neg1_ap),
                 reads=["cl_nv", eps_k, neg1_k], writes=["cl_sd"])
            p.op("vector", lambda e: e.reciprocal(r, sd), reads=["cl_sd"], writes=["cl_r"])
            p.op("vector", lambda e: e.scalar_tensor_tensor(out=nb, in0=m, scalar=-1.0, in1=r,
                                                            op0=ALU.mult, op1=ALU.mult),
                 reads=["cl_m", "cl_r"], writes=["cl_nb"])
            p.op("scalar", lambda e, j=j: e.activation(out=XN[j][:, :], in_=BO[j][:, :], func=AF.Identity,
                                                       bias=nb, scale=r),
                 reads=[ko, "cl_nb", "cl_r"], writes=[kn])
            ctx.store(qs, xnout[b, sl, :], XN[j][:, :], [kn])
            p.op("vector", lambda e, j=j: e.tensor_scalar(
                out=XN32[j][:, :], in0=BO[j][:, :], scalar1=r, scalar2=nb, op0=ALU.mult, op1=ALU.add),
                reads=[ko, "cl_nb", "cl_r"], writes=[kp])
            ctx.store(qs, xnout32[b, sl, :], XN32[j][:, :], [kp])



def emit_ffn0(ctx, xn, xn32, Wg, bg, W1, b1, W2, b2, gout):
    p, qs, qw = ctx.p, ctx.qs, ctx.qw
    nH = 16
    WGS = [ctx.sb(f"f0_wg{c}", [128, 8], F32) for c in range(NC)]
    for c in range(NC):
        ctx.dma(qs, Wg[c * 128:(c + 1) * 128, :], WGS[c][:, :], f"f0_wg{c}")
    bg_row, bg_k = bias_row(ctx, bg.rearrange("(o e) -> o e", o=1), 8, "f0bg", dtype=F32)
    XT32 = [ctx.sb(f"f0_xt{i}", [128, D], F32) for i in range(2)]
    XF32 = [ctx.sb(f"f0_xf{c}", [128, 128], F32) for c in range(NC)]
    b1T, b1T_k = bias_cols_prep(ctx, b1.rearrange("e (a p) -> (e a) p", p=128), 128, "f0b1")
    W1S = [ctx.sb(f"f0_w1_{c}", [128, 2048], BF16) for c in range(NC)]
    W2S = [ctx.sb(f"f0_w2_{h}", [128, 1024], BF16) for h in range(nH)]
    B2R = ctx.sb("f0_b2r", [1, 1024], BF16)
    H = [ctx.sb(f"f0_h{h}", [128, 512], BF16) for h in range(nH)]
    ACC = [ctx.sb(f"f0_acc{i}", [128, 1024], F32) for i in range(NT)]
    COMB = ctx.sb("f0_comb", [128, 8 * NT], F32)
    SC = ctx.sb("f0_sc", [128, 8], F32)
    em1, em2, den, rden, nm1, ex1 = [SC[:, j:j + 1] for j in range(6)]
    ET = ctx.sb("f0_et", [128, 8], F32)
    EQ = ctx.sb("f0_eq", [128, 8], F32)
    EMK = ctx.sb("f0_emk", [128, 8], F32)
    KP = ctx.sb("f0_kp", [128, 8], F32)
    GS = [ctx.sb(f"f0_gs{i}", [128, D], F32) for i in range(2)]

    xnl = XnLoader(ctx, xn, "f0xn")
    XN, xnk = xnl.t, xnl.k
    for b in range(NB):
        xnl.load(b)
        for i in range(NT):
            xti = i % 2
            xtk = f"f0_xt{xti}"
            ctx.dma(qs, xn32[b, i * 128:(i + 1) * 128, :], XT32[xti][:, :], xtk)
            for c in range(NC):
                pt = ctx.PS[3][:, 512 + (c % 4) * 128:512 + (c % 4) * 128 + 128]
                ptk = "PS3_1"
                p.op("tensor", lambda e, c=c, xti=xti, pt=pt: e.transpose(
                    pt, XT32[xti][:, c * 128:(c + 1) * 128], ctx.ident_f32[:, :]),
                    reads=[xtk, "ident_f32"], writes=[ptk])
                p.op("vector", lambda e, c=c, pt=pt: e.tensor_copy(out=XF32[c][:, :], in_=pt),
                     reads=[ptk], writes=[f"f0_xf{c}"])
            pg, pgk = ctx.psk(3, 0)
            pg = pg[:, 0:8]
            for c in range(NC):
                p.op("tensor", lambda e, c=c, pg=pg: e.matmul(
                    pg, XF32[c][:, :], WGS[c][:, :],
                    start=(c == 0), stop=False),
                    reads=[f"f0_xf{c}", f"f0_wg{c}"], writes=[pgk])
            p.op("tensor", lambda e, pg=pg: e.matmul(pg, ctx.ones_row_f32[0:1, :], bg_row[0:1, :],
                                                     start=False, stop=True),
                 reads=["ones_row_f32", bg_k], writes=[pgk])
            p.op("vector", lambda e, pg=pg: e.reduce_max(em1, pg, axis=AX.X),
                 reads=[pgk], writes=["f0_em1"])
            p.op("vector", lambda e: e.tensor_scalar(out=nm1, in0=em1, scalar1=-1.0, scalar2=None, op0=ALU.mult),
                 reads=["f0_em1"], writes=["f0_nm1"])
            p.op("scalar", lambda e, pg=pg: e.activation(out=ET[:, :], in_=pg, func=AF.Exp, bias=nm1),
                 reads=[pgk, "f0_nm1"], writes=["f0_et"])
            p.op("vector", lambda e: e.reduce_max(ex1, ET[:, :], axis=AX.X),
                 reads=["f0_et"], writes=["f0_ex1"])
            p.op("vector", lambda e: e.tensor_scalar(out=EQ[:, :], in0=ET[:, :], scalar1=ex1, scalar2=None,
                                                     op0=ALU.is_ge),
                 reads=["f0_et", "f0_ex1"], writes=["f0_eq"])
            p.op("vector", lambda e: e.scalar_tensor_tensor(out=EMK[:, :], in0=EQ[:, :], scalar=-2.0, in1=ET[:, :],
                                                            op0=ALU.mult, op1=ALU.add),
                 reads=["f0_eq", "f0_et"], writes=["f0_emk"])
            p.op("vector", lambda e: e.reduce_max(em2, EMK[:, :], axis=AX.X),
                 reads=["f0_emk"], writes=["f0_em2"])
            p.op("vector", lambda e: e.tensor_tensor(out=den, in0=ex1, in1=em2, op=ALU.add),
                 reads=["f0_ex1", "f0_em2"], writes=["f0_den"])
            p.op("vector", lambda e: e.reciprocal(rden, den), reads=["f0_den"], writes=["f0_rden"])
            p.op("vector", lambda e: e.tensor_scalar(out=KP[:, :], in0=ET[:, :], scalar1=em2, scalar2=None,
                                                     op0=ALU.is_ge),
                 reads=["f0_et", "f0_em2"], writes=["f0_kp"])
            p.op("vector", lambda e: e.tensor_mul(KP[:, :], KP[:, :], ET[:, :]),
                 reads=["f0_kp", "f0_et"], writes=["f0_kp"])
            p.op("vector", lambda e, i=i: e.tensor_scalar(out=COMB[:, i * 8:(i + 1) * 8], in0=KP[:, :],
                                                          scalar1=rden, scalar2=None, op0=ALU.mult),
                 reads=["f0_kp", "f0_rden"], writes=[f"f0_comb{i}"])
        for ex in range(8):
            for c in range(NC):
                ctx.dma(qw, W1[ex, c * 128:(c + 1) * 128, :], W1S[c][:, :], f"f0_w1_{c}")
            for h in range(nH):
                ctx.dma(qw, W2[ex, h * 128:(h + 1) * 128, :], W2S[h][:, :], f"f0_w2_{h}")
            ctx.dma(qw, b2[ex:ex + 1, :], B2R[:, :], "f0_b2r")
            for sl in range(4):
                tsl = slice(sl * 512, (sl + 1) * 512)
                for h in range(nH):
                    ph, phk = ctx.psk(h % 2, (h // 2) % 2)
                    for c in range(NC):
                        p.op("tensor", lambda e, c=c, h=h, ph=ph, tsl=tsl: e.matmul(
                            ph, W1S[c][:, h * 128:(h + 1) * 128], XN[c][:, tsl],
                            start=(c == 0), stop=(c == NC - 1)),
                            reads=[f"f0_w1_{c}", xnk[c]], writes=[phk])
                    p.op("scalar", lambda e, h=h, ph=ph, ex=ex: e.activation(
                        out=H[h][:, :], in_=ph, func=AF.Relu, bias=b1T[:, ex * 16 + h:ex * 16 + h + 1]),
                        reads=[phk, b1T_k], writes=[f"f0_h{h}"])
                for m in range(4):
                    i = sl * 4 + m
                    py = ctx.PS[2 + m % 2][:, :]
                    pyk = [f"PS{2 + m % 2}_0", f"PS{2 + m % 2}_1"]
                    for h in range(nH):
                        for nhalf in range(2):
                            p.op("tensor", lambda e, h=h, m=m, nhalf=nhalf, py=py: e.matmul(
                                py[:, nhalf * 512:(nhalf + 1) * 512],
                                H[h][:, m * 128:(m + 1) * 128],
                                W2S[h][:, nhalf * 512:(nhalf + 1) * 512],
                                start=(h == 0), stop=False),
                                reads=[f"f0_h{h}", f"f0_w2_{h}"], writes=[pyk[nhalf]])
                    for nhalf in range(2):
                        p.op("tensor", lambda e, nhalf=nhalf, py=py: e.matmul(
                            py[:, nhalf * 512:(nhalf + 1) * 512], ctx.ones_row_bf[0:1, :],
                            B2R[0:1, nhalf * 512:(nhalf + 1) * 512], start=False, stop=True),
                            reads=["ones_row_bf", "f0_b2r"], writes=[pyk[nhalf]])
                    cc = COMB[:, i * 8 + ex:i * 8 + ex + 1]
                    if ex == 0:
                        p.op("vector", lambda e, i=i, py=py, cc=cc: e.tensor_scalar(
                            out=ACC[i][:, :], in0=py, scalar1=cc, scalar2=None, op0=ALU.mult),
                            reads=pyk + [f"f0_comb{i}"], writes=[f"f0_acc{i}"])
                    else:
                        p.op("vector", lambda e, i=i, py=py, cc=cc: e.scalar_tensor_tensor(
                            out=ACC[i][:, :], in0=py, scalar=cc, in1=ACC[i][:, :],
                            op0=ALU.mult, op1=ALU.add),
                            reads=pyk + [f"f0_comb{i}", f"f0_acc{i}"], writes=[f"f0_acc{i}"])
        for i in range(NT):
            j = i % 2
            p.op("scalar", lambda e, i=i, j=j: e.activation(out=GS[j][:, :], in_=ACC[i][:, :], func=AF.Silu),
                 reads=[f"f0_acc{i}"], writes=[f"f0_gs{j}"])
            ctx.store(qs, gout[b, i * 128:(i + 1) * 128, :], GS[j][:, :], [f"f0_gs{j}"])



def emit_attn0(ctx, xn, wq, wk, wv, wo, wqb, wkb, wvb, wob, thr, sharp, sout):
    p, qs, qw = ctx.p, ctx.qs, ctx.qw
    TR = ctx.sb("a0_tr", [1, 16], F32)
    SR = ctx.sb("a0_sr", [1, 16], F32)
    TV = ctx.sb("a0_tv", [1, 16], F32)
    NSR = ctx.sb("a0_nsr", [1, 16], F32)
    SHARPC = ctx.sb("a0_shc", [128, 16], F32)
    NSTC = ctx.sb("a0_nst", [128, 16], F32)
    ctx.dma(qs, thr.rearrange("(o e) -> o e", o=1), TR[:, :], "a0_tr")
    ctx.dma(qs, sharp.rearrange("(o e) -> o e", o=1), SR[:, :], "a0_sr")
    p.op("scalar", lambda e: e.activation(out=TV[:, :], in_=TR[:, :], func=AF.Sigmoid),
         reads=["a0_tr"], writes=["a0_tv"])
    p.op("vector", lambda e: e.tensor_mul(NSR[:, :], SR[:, :], TV[:, :]),
         reads=["a0_sr", "a0_tv"], writes=["a0_nsr"])
    p.op("vector", lambda e: e.tensor_scalar(out=NSR[:, :], in0=NSR[:, :], scalar1=-1.0, scalar2=None,
                                             op0=ALU.mult),
         reads=["a0_nsr"], writes=["a0_nsr"])
    pb = ctx.PS[3][:, 0:16]
    p.op("tensor", lambda e: e.matmul(pb[:, 0:16], ctx.ones_row_f32[0:1, :], SR[0:1, :],
                                      start=True, stop=True),
         reads=["ones_row_f32", "a0_sr"], writes=["PS3_0"])
    p.op("vector", lambda e: e.tensor_copy(out=SHARPC[:, :], in_=pb[:, 0:16]),
         reads=["PS3_0"], writes=["a0_shc"])
    p.op("tensor", lambda e: e.matmul(pb[:, 0:16], ctx.ones_row_f32[0:1, :], NSR[0:1, :],
                                      start=True, stop=True),
         reads=["ones_row_f32", "a0_nsr"], writes=["PS3_0"])
    p.op("vector", lambda e: e.tensor_copy(out=NSTC[:, :], in_=pb[:, 0:16]),
         reads=["PS3_0"], writes=["a0_nst"])
    qbc, qbc_k = bias_cols_prep(ctx, wqb.rearrange("(n p) -> n p", p=128), 8, "a0qb")
    kbc, kbc_k = bias_cols_prep(ctx, wkb.rearrange("(n p) -> n p", p=128), 8, "a0kb")
    vbr, vbr_k = bias_row(ctx, wvb.rearrange("(o e) -> o e", o=1), D, "a0vb")
    obr, obr_k = bias_row(ctx, wob.rearrange("(o e) -> o e", o=1), D, "a0ob")
    WB = [ctx.sb(f"a0_wb{c}", [128, D], BF16) for c in range(NC)]
    QF = [ctx.sb(f"a0_qf{c}", [128, T], BF16) for c in range(NC)]
    KF = [ctx.sb(f"a0_kf{c}", [128, T], BF16) for c in range(NC)]
    VT = [ctx.sb(f"a0_vt{i}", [128, D], BF16) for i in range(NT)]
    AOF = [ctx.sb(f"a0_ao{c}", [128, T], BF16) for c in range(NC)]
    RZ = ctx.sb("a0_rz", [1, 1024], F32)
    RZB = ctx.sb("a0_rzb", [1, 1024], BF16)
    RBS = ctx.sb("a0_rbs", [128, 1024], BF16)
    GB = [ctx.sb(f"a0_gb{i}", [128, 1024], BF16) for i in range(4)]
    SO = [ctx.sb(f"a0_so{i}", [128, D], F32) for i in range(2)]

    xnl = XnLoader(ctx, xn, "a0xn")
    XN, xnk = xnl.t, xnl.k
    PT = [XN[kt // 2][:, (kt % 2) * 1024:(kt % 2) * 1024 + 1024] for kt in range(NT)]
    PTK = [xnk[kt // 2] for kt in range(NT)]

    def load_w(wdram, tag):
        for c in range(NC):
            ctx.dma(qw, wdram[c * 128:(c + 1) * 128, :], WB[c][:, :], f"a0_wb{c}")

    sc125, sc125k = ctx.cc(0.125)
    for b in range(NB):
        xnl.load(b)
        for (dst, dk, bias_cols, biask, wdram) in ((QF, "a0_qf", qbc, qbc_k, wq),
                                                   (KF, "a0_kf", kbc, kbc_k, wk)):
            load_w(wdram, dk)
            for dc in range(NC):
                for sl in range(4):
                    pp, ppk = ctx.psk8(dc * 4 + sl)
                    tsl = slice(sl * 512, (sl + 1) * 512)
                    for c in range(NC):
                        p.op("tensor", lambda e, c=c, dc=dc, pp=pp, tsl=tsl: e.matmul(
                            pp, WB[c][:, dc * 128:(dc + 1) * 128], XN[c][:, tsl],
                            start=(c == 0), stop=(c == NC - 1)),
                            reads=[f"a0_wb{c}", xnk[c]], writes=[ppk])
                    p.op("scalar", lambda e, dst=dst, dc=dc, pp=pp, tsl=tsl, bias_cols=bias_cols: e.activation(
                        out=dst[dc][:, tsl], in_=pp, func=AF.Identity,
                        bias=bias_cols[:, dc:dc + 1]),
                        reads=[ppk, biask], writes=[f"{dk}{dc}"])
        load_w(wv, "a0_vt")
        for i in range(NT):
            pp = ctx.PS[i % 4][:, :]
            ppk = [f"PS{i % 4}_0", f"PS{i % 4}_1"]
            for nh in range(2):
                nsl = slice(nh * 512, (nh + 1) * 512)
                for c in range(NC):
                    p.op("tensor", lambda e, c=c, i=i, pp=pp, nsl=nsl: e.matmul(
                        pp[:, nsl], XN[c][:, i * 128:(i + 1) * 128], WB[c][:, nsl],
                        start=(c == 0), stop=False),
                        reads=[xnk[c], f"a0_wb{c}"], writes=[ppk[nh]])
                p.op("tensor", lambda e, nh=nh, pp=pp, nsl=nsl: e.matmul(
                    pp[:, nsl], ctx.ones_row_bf[0:1, :],
                    vbr[0:1, nsl], start=False, stop=True),
                    reads=["ones_row_bf", vbr_k], writes=[ppk[nh]])
            p.op("vector", lambda e, i=i, pp=pp: e.tensor_copy(out=VT[i][:, :], in_=pp),
                 reads=ppk, writes=[f"a0_vt{i}"])
        def emit_st_exp(h, qh, kt):
            hc, hp = h // 2, (h % 2) * 64
            pp = ctx.PS[kt % 2][:, :]
            ppk = [f"PS{kt % 2}_0", f"PS{kt % 2}_1"]
            for nh in range(2):
                p.op("tensor", lambda e, kt=kt, nh=nh, pp=pp, hc=hc, hp=hp, qh=qh: e.matmul(
                    pp[:, nh * 512:(nh + 1) * 512],
                    KF[hc][hp:hp + 64, kt * 128:(kt + 1) * 128],
                    QF[hc][hp:hp + 64, qh * 1024 + nh * 512:qh * 1024 + (nh + 1) * 512],
                    start=True, stop=True),
                    reads=[f"a0_kf{hc}", f"a0_qf{hc}"], writes=[ppk[nh]])
            p.op("scalar", lambda e, kt=kt, pp=pp: e.activation(
                out=PT[kt], in_=pp, func=AF.Exp, scale=sc125),
                reads=ppk + [sc125k], writes=[PTK[kt]])

        def emit_z(kt):
            for nh in range(2):
                p.op("tensor", lambda e, kt=kt, nh=nh: e.matmul(
                    ctx.PS[2][0:1, nh * 512:(nh + 1) * 512],
                    ctx.ones_col_bf[:, 0:1], PT[kt][:, nh * 512:(nh + 1) * 512],
                    start=(kt == 0), stop=(kt == NT - 1)),
                    reads=["ones_col_bf", PTK[kt]], writes=[f"PS2_{nh}"])

        def emit_ph2():
            p.op("vector", lambda e: e.reciprocal(RZ[0:1, :], ctx.PS[2][0:1, :]),
                 reads=["PS2_0", "PS2_1"], writes=["a0_rz"])
            p.op("vector", lambda e: e.tensor_copy(out=RZB[0:1, :], in_=RZ[0:1, :]),
                 reads=["a0_rz"], writes=["a0_rzb"])
            for nh in range(2):
                p.op("tensor", lambda e, nh=nh: e.matmul(
                    ctx.PS[2][:, nh * 512:(nh + 1) * 512], ctx.ones_row_bf[0:1, :],
                    RZB[0:1, nh * 512:(nh + 1) * 512], start=True, stop=True),
                    reads=["ones_row_bf", "a0_rzb"], writes=[f"PS2_{nh}"])
            p.op("scalar", lambda e: e.activation(out=RBS[:, :], in_=ctx.PS[2][:, :], func=AF.Copy),
                 reads=["PS2_0", "PS2_1"], writes=["a0_rbs"])

        def emit_multA(kt):
            p.op("vector", lambda e, kt=kt: e.tensor_mul(PT[kt], PT[kt], RBS[:, :]),
                 reads=[PTK[kt], "a0_rbs"], writes=[PTK[kt]])

        def emit_ph3_step(h, kt):
            jg = kt % 4
            p.op("scalar", lambda e, kt=kt, jg=jg, h=h: e.activation(
                out=GB[jg][:, :], in_=PT[kt], func=AF.Sigmoid,
                bias=NSTC[:, h:h + 1], scale=SHARPC[:, h:h + 1]),
                reads=[PTK[kt], "a0_nst", "a0_shc"], writes=[f"a0_gb{jg}"])
            if kt + 3 < NT:
                emit_multA(kt + 3)
            p.op("vector", lambda e, kt=kt, jg=jg: e.tensor_mul(PT[kt], PT[kt], GB[jg][:, :]),
                 reads=[PTK[kt], f"a0_gb{jg}"], writes=[PTK[kt]])
            for nh in range(2):
                p.op("tensor", lambda e, kt=kt, nh=nh, h=h: e.matmul(
                    ctx.PS[3][0:64, nh * 512:(nh + 1) * 512],
                    VT[kt][:, h * 64:(h + 1) * 64],
                    PT[kt][:, nh * 512:(nh + 1) * 512],
                    start=(kt == 0), stop=(kt == NT - 1)),
                    reads=[PTK[kt], f"a0_vt{kt}"], writes=[f"PS3_{nh}"])

        iters = [(h, qh) for h in range(16) for qh in range(2)]
        for idx, (h, qh) in enumerate(iters):
            hc, hp = h // 2, (h % 2) * 64
            qsl = slice(qh * 1024, (qh + 1) * 1024)
            for kt in range(NT):
                emit_st_exp(h, qh, kt)
                if kt > 0:
                    emit_z(kt - 1)
            emit_z(NT - 1)
            emit_ph2()
            emit_multA(0)
            emit_multA(1)
            emit_multA(2)
            for kt in range(NT):
                emit_ph3_step(h, kt)
            p.op("scalar", lambda e, hc=hc, hp=hp, qsl=qsl: e.activation(
                out=AOF[hc][hp:hp + 64, qsl], in_=ctx.PS[3][0:64, :], func=AF.Copy),
                reads=["PS3_0", "PS3_1"], writes=[f"a0_ao{hc}"])
        load_w(wo, "a0_wo")
        for i in range(NT):
            pp = ctx.PS[i % 4][:, :]
            ppk = [f"PS{i % 4}_0", f"PS{i % 4}_1"]
            for nh in range(2):
                nsl = slice(nh * 512, (nh + 1) * 512)
                for c in range(NC):
                    p.op("tensor", lambda e, c=c, i=i, pp=pp, nsl=nsl: e.matmul(
                        pp[:, nsl], AOF[c][:, i * 128:(i + 1) * 128], WB[c][:, nsl],
                        start=(c == 0), stop=False),
                        reads=[f"a0_ao{c}", f"a0_wb{c}"], writes=[ppk[nh]])
                p.op("tensor", lambda e, pp=pp, nsl=nsl: e.matmul(
                    pp[:, nsl], ctx.ones_row_bf[0:1, :],
                    obr[0:1, nsl], start=False, stop=True),
                    reads=["ones_row_bf", obr_k], writes=[ppk[nh]])
            j = i % 2
            p.op("vector", lambda e, j=j, pp=pp: e.tensor_copy(out=SO[j][:, :], in_=pp),
                 reads=ppk, writes=[f"a0_so{j}"])
            ctx.store(qs, sout[b, i * 128:(i + 1) * 128, :], SO[j][:, :], [f"a0_so{j}"])



def emit_gla(ctx, xn, Wq, bq, Wk, bk, Wv, bv, Wg, bg, Wo, bo, sout, debug=None):
    p, qs, qw = ctx.p, ctx.qs, ctx.qw
    CH = 64
    NCH = T // CH
    K2 = 2
    WQS = [ctx.sb(f"g1_wq{c}", [128, 256], BF16) for c in range(NC)]
    WKS = [ctx.sb(f"g1_wk{c}", [128, 256], BF16) for c in range(NC)]
    WGS = [ctx.sb(f"g1_wg{c}", [128, 256], BF16) for c in range(NC)]
    WVS = [ctx.sb(f"g1_wv{c}", [128, 256], BF16) for c in range(NC)]
    for c in range(NC):
        sl = slice(c * 128, (c + 1) * 128)
        ctx.dma(qw, Wq[sl, :], WQS[c][:, :], f"g1_wq{c}")
        ctx.dma(qw, Wk[sl, :], WKS[c][:, :], f"g1_wk{c}")
        ctx.dma(qw, Wg[sl, :], WGS[c][:, :], f"g1_wg{c}")
        ctx.dma(qw, Wv[sl, :], WVS[c][:, :], f"g1_wv{c}")
    WOS = [ctx.sb(f"g1_wo{v}", [128, 1024], BF16) for v in range(K2)]
    for v in range(K2):
        ctx.dma(qw, Wo[v * 128:(v + 1) * 128, :], WOS[v][:, :], f"g1_wo{v}")
    bqc, bqc_k = bias_cols_prep(ctx, bq.rearrange("(n p) -> n p", p=128), 2, "g1bq")
    bkc, bkc_k = bias_cols_prep(ctx, bk.rearrange("(n p) -> n p", p=128), 2, "g1bk")
    bgc, bgc_k = bias_cols_prep(ctx, bg.rearrange("(n p) -> n p", p=128), 2, "g1bg")
    bvr, bvr_k = bias_row(ctx, bv.rearrange("(o e) -> o e", o=1), 256, "g1bv")
    bor, bor_k = bias_row(ctx, bo.rearrange("(o e) -> o e", o=1), D, "g1bo")
    QFm = [ctx.sb(f"g1_q{kc}", [128, T], BF16) for kc in range(K2)]
    KS = [ctx.sb(f"g1_ks{kc}", [128, T], BF16) for kc in range(K2)]
    GSIG = [ctx.sb(f"g1_gs{kc}", [128, T], F32) for kc in range(K2)]
    LG = [ctx.sb(f"g1_lg{kc}", [128, T], F32) for kc in range(K2)]
    VT = [ctx.sb(f"g1_vt{i}", [64, 256], BF16) for i in range(2 * NT)]
    OF = [ctx.sb(f"g1_of{vc}", [128, T], BF16) for vc in range(K2)]
    S = [ctx.sb(f"g1_s{kc}", [128, 256], F32) for kc in range(K2)]
    SB = [ctx.sb(f"g1_sb{kc}", [128, 256], BF16) for kc in range(K2)]
    Z512 = ctx.sb("g1_z512", [128, 512], F32)
    QT = [ctx.sb(f"g1_qt{kc}", [128, CH], BF16) for kc in range(2 * K2)]
    F2 = [ctx.sb(f"g1_f2{kc}", [128, CH], BF16) for kc in range(2 * K2)]
    KT = [ctx.sb(f"g1_kt{kc}", [128, CH], BF16) for kc in range(2 * K2)]
    KTT = [ctx.sb(f"g1_ktt{i}", [64, 256], BF16) for i in range(2)]
    ATm = ctx.sb("g1_at", [64, 64], BF16)
    EPOS = [ctx.sb(f"g1_ep{kc}", [128, CH], F32) for kc in range(K2)]
    ENEG = [ctx.sb(f"g1_en{kc}", [128, CH], F32) for kc in range(K2)]
    COLS = ctx.sb("g1_cols", [128, 16], F32)
    SO = [ctx.sb(f"g1_so{i}", [128, D], F32) for i in range(2)]
    p.op("vector", lambda e: e.memset(Z512[:, :], 0.0), writes=["g1_z512"])

    xnl = XnLoader(ctx, xn, "g1xn")
    XN, xnk = xnl.t, xnl.k
    neg1, neg1k = ctx.cc(-1.0)

    for b in range(NB):
        xnl.load(b)
        for kc in range(K2):
            p.op("vector", lambda e, kc=kc: e.memset(S[kc][:, :], 0.0), writes=[f"g1_s{kc}"])
        pidx = 0
        for kc in range(K2):
            for sl4 in range(4):
                tsl = slice(sl4 * 512, (sl4 + 1) * 512)
                for (WS, wsk, dstf, dstk, func, bc, bck) in (
                        (WQS, "g1_wq", QFm, "g1_q", AF.Identity, bqc, bqc_k),
                        (WKS, "g1_wk", KS, "g1_ks", AF.Sigmoid, bkc, bkc_k),
                        (WGS, "g1_wg", GSIG, "g1_gs", AF.Sigmoid, bgc, bgc_k)):
                    pp, ppk = ctx.psk8(pidx); pidx += 1
                    for c in range(NC):
                        p.op("tensor", lambda e, c=c, kc=kc, pp=pp, tsl=tsl, WS=WS, wsk=wsk: e.matmul(
                            pp, WS[c][:, kc * 128:(kc + 1) * 128], XN[c][:, tsl],
                            start=(c == 0), stop=(c == NC - 1)),
                            reads=[f"{wsk}{c}", xnk[c]], writes=[ppk])
                    p.op("scalar", lambda e, kc=kc, pp=pp, tsl=tsl, dstf=dstf, func=func, bc=bc: e.activation(
                        out=dstf[kc][:, tsl], in_=pp, func=func, bias=bc[:, kc:kc + 1]),
                        reads=[ppk, bck], writes=[f"{dstk}{kc}"])
        for kc in range(K2):
            p.op("scalar", lambda e, kc=kc: e.activation(out=GSIG[kc][:, :], in_=GSIG[kc][:, :], func=AF.Ln),
                 reads=[f"g1_gs{kc}"], writes=[f"g1_gs{kc}"])
        for i in range(2 * NT):
            pp, ppk = ctx.psk8(i)
            pv = pp[0:64, 0:256]
            for c in range(NC):
                p.op("tensor", lambda e, c=c, i=i, pv=pv: e.matmul(
                    pv, XN[c][:, i * 64:(i + 1) * 64], WVS[c][:, :],
                    start=(c == 0), stop=False),
                    reads=[xnk[c], f"g1_wv{c}"], writes=[ppk])
            p.op("tensor", lambda e, pv=pv: e.matmul(pv, ctx.ones_row_bf[0:1, 0:64], bvr[0:1, :],
                                                     start=False, stop=True),
                 reads=["ones_row_bf", bvr_k], writes=[ppk])
            p.op("vector", lambda e, i=i, pv=pv: e.tensor_copy(out=VT[i][:, :], in_=pv),
                 reads=[ppk], writes=[f"g1_vt{i}"])
        for kc in range(K2):
            for blk in range(4):
                tsl = slice(blk * 512, (blk + 1) * 512)
                init = 0.0 if blk == 0 else LG[kc][:, blk * 512 - 1:blk * 512]
                p.op("vector", lambda e, kc=kc, tsl=tsl, init=init: e.tensor_tensor_scan(
                    out=LG[kc][:, tsl], data0=GSIG[kc][:, tsl], data1=Z512[:, :],
                    initial=init, op0=ALU.add, op1=ALU.add),
                    reads=[f"g1_gs{kc}", "g1_z512", f"g1_lg{kc}"], writes=[f"g1_lg{kc}"])
        def emit_elem(c):
            par = c % 2
            t0 = c * CH
            cols = slice(t0, t0 + CH)
            for kc in range(K2):
                s2 = 2 * par + kc
                lg0 = ctx.zero_col[:, 0:1] if c == 0 else LG[kc][:, t0 - 1:t0]
                lg0k = "zero_col" if c == 0 else f"g1_lg{kc}"
                nl0 = COLS[:, 8 * par + kc:8 * par + kc + 1]
                ecol = COLS[:, 8 * par + 2 + kc:8 * par + 3 + kc]
                dcol = COLS[:, 8 * par + 4 + kc:8 * par + 5 + kc]
                p.op("vector", lambda e, nl0=nl0, lg0=lg0: e.tensor_scalar(
                    out=nl0, in0=lg0, scalar1=-1.0, scalar2=None, op0=ALU.mult),
                    reads=[lg0k], writes=[f"g1_nl0{par}{kc}"])
                p.op("scalar", lambda e, kc=kc, cols=cols, nl0=nl0: e.activation(
                    out=EPOS[kc][:, :], in_=LG[kc][:, cols], func=AF.Exp, bias=nl0),
                    reads=[f"g1_lg{kc}", f"g1_nl0{par}{kc}"], writes=[f"g1_ep{kc}"])
                p.op("scalar", lambda e, kc=kc, cols=cols, lg0=lg0: e.activation(
                    out=ENEG[kc][:, :], in_=LG[kc][:, cols], func=AF.Exp, bias=lg0, scale=neg1),
                    reads=[f"g1_lg{kc}", lg0k, neg1k], writes=[f"g1_en{kc}"])
                p.op("vector", lambda e, kc=kc, t0=t0, dcol=dcol, lg0=lg0: e.tensor_tensor(
                    out=dcol, in0=LG[kc][:, t0 + CH - 1:t0 + CH], in1=lg0, op=ALU.subtract),
                    reads=[f"g1_lg{kc}", lg0k], writes=[f"g1_dc{par}{kc}"])
                p.op("scalar", lambda e, ecol=ecol, dcol=dcol: e.activation(
                    out=ecol, in_=dcol, func=AF.Exp),
                    reads=[f"g1_dc{par}{kc}"], writes=[f"g1_ec{par}{kc}"])
                p.op("vector", lambda e, kc=kc, s2=s2, cols=cols: e.tensor_mul(
                    QT[s2][:, :], QFm[kc][:, cols], EPOS[kc][:, :]),
                    reads=[f"g1_q{kc}", f"g1_ep{kc}"], writes=[f"g1_qt{s2}"])
                p.op("vector", lambda e, kc=kc, s2=s2, cols=cols: e.tensor_mul(
                    F2[s2][:, :], KS[kc][:, cols], ENEG[kc][:, :]),
                    reads=[f"g1_ks{kc}", f"g1_en{kc}"], writes=[f"g1_f2{s2}"])
                p.op("vector", lambda e, s2=s2, ecol=ecol: e.tensor_scalar(
                    out=KT[s2][:, :], in0=F2[s2][:, :], scalar1=ecol, scalar2=None, op0=ALU.mult),
                    reads=[f"g1_f2{s2}", f"g1_ec{par}{kc}"], writes=[f"g1_kt{s2}"])

        def emit_heavy(c):
            par = c % 2
            t0 = c * CH
            cols = slice(t0, t0 + CH)
            vt_t = VT[c]
            vkey = f"g1_vt{c}"
            for kc in range(K2):
                s2 = 2 * par + kc
                ptr = ctx.PS[1].bitcast(BF16)[0:64, kc * 1024:kc * 1024 + 128]
                ptk = f"PS1_{kc}"
                p.op("tensor", lambda e, s2=s2, ptr=ptr: e.transpose(
                    ptr, KT[s2][:, :], ctx.ident_bf[:, :]),
                    reads=[f"g1_kt{s2}", "ident_bf"], writes=[ptk])
                p.op("vector", lambda e, kc=kc, par=par, ptr=ptr: e.tensor_copy(
                    out=KTT[par][0:64, kc * 128:(kc + 1) * 128], in_=ptr),
                    reads=[ptk], writes=[f"g1_ktt{par}"])
            pa = ctx.PS[3][0:64, 0:64]
            for kc in range(K2):
                p.op("tensor", lambda e, s2=2 * par + kc, kc=kc, pa=pa: e.matmul(
                    pa, F2[s2][:, :], QT[s2][:, :], start=(kc == 0), stop=(kc == K2 - 1)),
                    reads=[f"g1_f2{2 * par + kc}", f"g1_qt{2 * par + kc}"], writes=["PS3_0"])
            p.op("vector", lambda e, pa=pa: e.tensor_mul(ATm[0:64, :], pa, ctx.mask64[:, :]),
                 reads=["PS3_0", "mask64"], writes=["g1_at"])
            for kc in range(K2):
                pd = ctx.PS[0][:, kc * 512:kc * 512 + 256]
                pdk = f"PS0_{kc}"
                p.op("tensor", lambda e, kc=kc, par=par, pd=pd, vt_t=vt_t: e.matmul(
                    pd, KTT[par][0:64, kc * 128:(kc + 1) * 128], vt_t[0:64, :],
                    start=True, stop=True),
                    reads=[f"g1_ktt{par}", vkey], writes=[pdk])
            for vc in range(K2):
                po = ctx.PS[2][:, vc * 512:vc * 512 + 64]
                pok = f"PS2_{vc}"
                p.op("tensor", lambda e, vc=vc, po=po, vt_t=vt_t, c=c: e.matmul(
                    po, vt_t[0:64, vc * 128:(vc + 1) * 128], ATm[0:64, :],
                    start=True, stop=(c == 0)),
                    reads=[vkey, "g1_at"], writes=[pok])
                if c > 0:
                    for kc in range(K2):
                        p.op("tensor", lambda e, vc=vc, kc=kc, s2=2 * par + kc, po=po: e.matmul(
                            po, SB[kc][:, vc * 128:(vc + 1) * 128], QT[s2][:, :],
                            start=False, stop=(kc == K2 - 1)),
                            reads=[f"g1_sb{kc}", f"g1_qt{2 * par + kc}"], writes=[pok])
                p.op("scalar", lambda e, vc=vc, po=po, cols=cols: e.activation(
                    out=OF[vc][:, cols], in_=po, func=AF.Copy),
                    reads=[pok], writes=[f"g1_of{vc}"])
            for kc in range(K2):
                pd = ctx.PS[0][:, kc * 512:kc * 512 + 256]
                pdk = f"PS0_{kc}"
                ecol = COLS[:, 8 * par + 2 + kc:8 * par + 3 + kc]
                p.op("vector", lambda e, kc=kc, pd=pd, ecol=ecol: e.scalar_tensor_tensor(
                    out=S[kc][:, :], in0=S[kc][:, :], scalar=ecol, in1=pd,
                    op0=ALU.mult, op1=ALU.add),
                    reads=[f"g1_s{kc}", f"g1_ec{par}{kc}", pdk], writes=[f"g1_s{kc}"])
                p.op("vector", lambda e, kc=kc: e.tensor_copy(out=SB[kc][:, :], in_=S[kc][:, :]),
                     reads=[f"g1_s{kc}"], writes=[f"g1_sb{kc}"])

        emit_elem(0)
        for c in range(NCH):
            if c + 1 < NCH:
                emit_elem(c + 1)
            emit_heavy(c)
            if debug is not None and b == 0 and c < 4:
                for kc in range(K2):
                    ctx.store(qs, debug["S"][c, kc], S[kc][:, :], [f"g1_s{kc}"])
                    ctx.store(qs, debug["QT"][c, kc], QT[kc][:, :], [f"g1_qt{kc}"])
                    ctx.store(qs, debug["F2"][c, kc], F2[kc][:, :], [f"g1_f2{kc}"])
                    ctx.store(qs, debug["KTq"][c, kc], KT[kc][:, :], [f"g1_kt{kc}"])
                allcols = [f"g1_{pfx}{kc}" for pfx in ("nl0", "ec", "dc") for kc in range(K2)]
                ctx.store(qs, debug["COLS"][c], COLS[:, :], allcols)
                ctx.store(qs, debug["AT"][c], ATm[0:64, :], ["g1_at"])
        for i in range(NT):
            pp = ctx.PS[1][:, :]
            ppk = ["PS1_0", "PS1_1"]
            for nh in range(2):
                nsl = slice(nh * 512, (nh + 1) * 512)
                for vc in range(K2):
                    p.op("tensor", lambda e, vc=vc, i=i, pp=pp, nsl=nsl: e.matmul(
                        pp[:, nsl], OF[vc][:, i * 128:(i + 1) * 128], WOS[vc][:, nsl],
                        start=(vc == 0), stop=False),
                        reads=[f"g1_of{vc}", f"g1_wo{vc}"], writes=[ppk[nh]])
                p.op("tensor", lambda e, pp=pp, nsl=nsl: e.matmul(
                    pp[:, nsl], ctx.ones_row_bf[0:1, :], bor[0:1, nsl], start=False, stop=True),
                    reads=["ones_row_bf", bor_k], writes=[ppk[nh]])
            j = i % 2
            p.op("vector", lambda e, j=j, pp=pp: e.tensor_copy(out=SO[j][:, :], in_=pp),
                 reads=ppk, writes=[f"g1_so{j}"])
            ctx.store(qs, sout[b, i * 128:(i + 1) * 128, :], SO[j][:, :], [f"g1_so{j}"])
        if debug is not None:
            for kc in range(K2):
                ctx.store(qs, debug["LG"][b, kc], LG[kc][:, :], [f"g1_lg{kc}"])
            for vc in range(K2):
                ctx.store(qs, debug["OF"][b, vc], OF[vc][:, :], [f"g1_of{vc}"])



def emit_ffn1(ctx, xn, xn32, wi, bi, wo, bo, lora_A, lora_B, gate_w, gate_b,
              base_sp, u_sp, loraBT, gout, x1in=None, s1in=None, outp=None):
    p, qs, qw = ctx.p, ctx.qs, ctx.qw
    NF = 32
    W64 = [ctx.sb(f"f1_w{dc}", [128, 4096], BF16) for dc in range(NC)]
    wk64 = [f"f1_w{dc}" for dc in range(NC)]
    WSTG = [ctx.sb(f"f1_stg{i}", [128, 1024], BF16) for i in range(2)]
    ATall = ctx.sb("f1_at", [128, 1024], BF16)
    A16 = ctx.sb("f1_a16", [16, 1024], BF16)
    B128 = ctx.sb("f1_b128", [128, 64], BF16)
    BTW = ctx.sb("f1_btw", [16, 4096], BF16)
    GWT = ctx.sb("f1_gwt", [128, 64], F32)
    GW8 = ctx.sb("f1_gw8", [8, 1024], F32)
    gbr, gbr_k = bias_row(ctx, gate_b.rearrange("(o e) -> o e", o=1), 8, "f1gb", dtype=F32)
    biT, biT_k = bias_cols_prep(ctx, bi.rearrange("(n p) -> n p", p=128), 32, "f1bi")
    bor, bor_k = bias_row(ctx, bo.rearrange("(o e) -> o e", o=1), D, "f1bo")
    XT32 = [ctx.sb(f"f1_xt{i}", [128, D], F32) for i in range(2)]
    XF32 = [ctx.sb(f"f1_xf{c}", [128, 128], F32) for c in range(NC)]
    GATES = ctx.sb("f1_gates", [128, 8 * NT * NB], F32)
    SC = ctx.sb("f1_sc", [128, 8], F32)
    m1c, em2, den, rden, nm1, ex1, zc, gv = [SC[:, j:j + 1] for j in range(8)]
    ET = ctx.sb("f1_et", [128, 8], F32)
    EQ = ctx.sb("f1_eq", [128, 8], F32)
    EMK = ctx.sb("f1_emk", [128, 8], F32)
    KP = ctx.sb("f1_kp", [128, 8], F32)
    BB = ctx.sb("f1_bb", [128, T], BF16)
    UTB = ctx.sb("f1_utb", [16, T], BF16)
    BSL = ctx.sb("f1_bsl", [128, 8192], BF16)
    UTe = ctx.sb("f1_ute", [16, 256], BF16)
    HF = [ctx.sb(f"f1_hf{i}", [128, 256], F32) for i in range(2)]
    HB = [ctx.sb(f"f1_hb{i}", [128, 256], BF16) for i in range(4)]
    ACC2 = [ctx.sb(f"f1_acc{i}", [128, D], F32) for i in range(2)]
    GBUF = [ctx.sb(f"f1_gb{i}", [128, D], F32) for i in range(2)]
    if outp is not None:
        X1T = [ctx.sb(f"f1_x1t{i}", [128, D], F32) for i in range(2)]
        S1T = [ctx.sb(f"f1_s1t{i}", [128, D], F32) for i in range(2)]

    xnl = XnLoader(ctx, xn, "f1xn")
    XN, xnk = xnl.t, xnl.k

    def transpose_128(src_ap, src_key, dst_ap, dst_key, pslot):
        bank = pslot % 2
        pt = ctx.PS[3].bitcast(BF16)[:, bank * 1024:bank * 1024 + 128]
        ptk = f"PS3_{bank}"
        p.op("tensor", lambda e: e.transpose(pt, src_ap, ctx.ident_bf[:, :]),
             reads=[src_key, "ident_bf"], writes=[ptk])
        p.op("vector", lambda e: e.tensor_copy(out=dst_ap, in_=pt),
             reads=[ptk], writes=[dst_key])

    for fc in range(NF):
        j = fc % 2
        ctx.dma(qw, wi[fc * 128:(fc + 1) * 128, :], WSTG[j][:, :], f"f1_stg{j}")
        for dc in range(NC):
            transpose_128(WSTG[j][:, dc * 128:(dc + 1) * 128], f"f1_stg{j}",
                          W64[dc][:, fc * 128:(fc + 1) * 128], wk64[dc], fc % 8)
    for ex in range(8):
        ctx.dma(qw, lora_A[ex], A16[:, :], "f1_a16")
        for dc in range(NC):
            bank = dc % 2
            pt = ctx.PS[3].bitcast(BF16)[:, bank * 1024:bank * 1024 + 16]
            ptk = f"PS3_{bank}"
            p.op("tensor", lambda e, dc=dc, pt=pt: e.transpose(
                pt[0:128, 0:16], A16[0:16, dc * 128:(dc + 1) * 128], ctx.ident_bf[0:16, 0:16]),
                reads=["f1_a16", "ident_bf"], writes=[ptk])
            p.op("vector", lambda e, ex=ex, dc=dc, pt=pt: e.tensor_copy(
                out=ATall[:, (ex * 8 + dc) * 16:(ex * 8 + dc) * 16 + 16], in_=pt[0:128, 0:16]),
                reads=[ptk], writes=["f1_at"])
    for ex in range(8):
        for half in range(8):
            ctx.dma(qw, lora_B[ex, half * 512:(half + 1) * 512, :].rearrange("(a p) r -> p a r", p=128),
                    B128.rearrange("p (a r) -> p a r", r=16), "f1_b128")
            for a in range(4):
                bank = a % 2
                pt = ctx.PS[3].bitcast(BF16)[0:16, bank * 1024:bank * 1024 + 128]
                ptk = f"PS3_{bank}"
                p.op("tensor", lambda e, a=a, pt=pt: e.transpose(
                    pt, B128[:, a * 16:(a + 1) * 16], ctx.ident_bf[:, :]),
                    reads=["f1_b128", "ident_bf"], writes=[ptk])
                p.op("vector", lambda e, a=a, pt=pt: e.tensor_copy(
                    out=BTW[0:16, a * 128:(a + 1) * 128], in_=pt),
                    reads=[ptk], writes=["f1_btw"])
            ctx.store(qs, loraBT[ex, :, half * 512:(half + 1) * 512],
                      BTW[0:16, 0:512], ["f1_btw"], writes=[f"f1bt{ex}_{half}"])
    ctx.dma(qs, gate_w, GW8[:, :], "f1_gw8")
    for dc in range(NC):
        pt = ctx.PS[3][:, 512:512 + 8]
        p.op("tensor", lambda e, dc=dc, pt=pt: e.transpose(
            pt[0:128, 0:8], GW8[0:8, dc * 128:(dc + 1) * 128], ctx.ident_f32[0:8, 0:8]),
            reads=["f1_gw8", "ident_f32"], writes=["PS3_1"])
        p.op("vector", lambda e, dc=dc, pt=pt: e.tensor_copy(
            out=GWT[:, dc * 8:dc * 8 + 8], in_=pt[0:128, 0:8]),
            reads=["PS3_1"], writes=["f1_gwt"])

    for b in range(NB):
        xnl.load(b)
        for i in range(NT):
            xti = i % 2
            xtk = f"f1_xt{xti}"
            ctx.dma(qs, xn32[b, i * 128:(i + 1) * 128, :], XT32[xti][:, :], xtk)
            for c in range(NC):
                pt = ctx.PS[3][:, 512 + (c % 4) * 128:512 + (c % 4) * 128 + 128]
                p.op("tensor", lambda e, c=c, xti=xti, pt=pt: e.transpose(
                    pt, XT32[xti][:, c * 128:(c + 1) * 128], ctx.ident_f32[:, :]),
                    reads=[xtk, "ident_f32"], writes=["PS3_1"])
                p.op("vector", lambda e, c=c, pt=pt: e.tensor_copy(out=XF32[c][:, :], in_=pt),
                     reads=["PS3_1"], writes=[f"f1_xf{c}"])
            pg = ctx.PS[3][:, 0:8]
            pgk = "PS3_0"
            for c in range(NC):
                p.op("tensor", lambda e, c=c, pg=pg: e.matmul(
                    pg, XF32[c][:, :], GWT[:, c * 8:c * 8 + 8],
                    start=(c == 0), stop=False),
                    reads=[f"f1_xf{c}", "f1_gwt"], writes=[pgk])
            p.op("tensor", lambda e, pg=pg: e.matmul(pg, ctx.ones_row_f32[0:1, :], gbr[0:1, :],
                                                     start=False, stop=True),
                 reads=["ones_row_f32", gbr_k], writes=[pgk])
            p.op("vector", lambda e, pg=pg: e.reduce_max(m1c, pg, axis=AX.X),
                 reads=[pgk], writes=["f1_m1"])
            p.op("vector", lambda e: e.tensor_scalar(out=nm1, in0=m1c, scalar1=-1.0, scalar2=None, op0=ALU.mult),
                 reads=["f1_m1"], writes=["f1_nm1"])
            p.op("scalar", lambda e, pg=pg: e.activation(out=ET[:, :], in_=pg, func=AF.Exp, bias=nm1,
                                                         accum_out=zc),
                 reads=[pgk, "f1_nm1"], writes=["f1_et", "f1_z"])
            p.op("vector", lambda e: e.reduce_max(ex1, ET[:, :], axis=AX.X),
                 reads=["f1_et"], writes=["f1_ex1"])
            p.op("vector", lambda e: e.tensor_scalar(out=EQ[:, :], in0=ET[:, :], scalar1=ex1, scalar2=None,
                                                     op0=ALU.is_ge),
                 reads=["f1_et", "f1_ex1"], writes=["f1_eq"])
            p.op("vector", lambda e: e.scalar_tensor_tensor(out=EMK[:, :], in0=EQ[:, :], scalar=-2.0, in1=ET[:, :],
                                                            op0=ALU.mult, op1=ALU.add),
                 reads=["f1_eq", "f1_et"], writes=["f1_emk"])
            p.op("vector", lambda e: e.reduce_max(em2, EMK[:, :], axis=AX.X),
                 reads=["f1_emk"], writes=["f1_em2"])
            p.op("vector", lambda e: e.tensor_tensor(out=den, in0=ex1, in1=em2, op=ALU.add),
                 reads=["f1_ex1", "f1_em2"], writes=["f1_den"])
            p.op("vector", lambda e: e.reciprocal(rden, zc), reads=["f1_z"], writes=["f1_rz"])
            p.op("vector", lambda e: e.tensor_mul(gv, den, rden),
                 reads=["f1_den", "f1_rz"], writes=["f1_gv"])
            p.op("vector", lambda e: e.tensor_scalar(out=KP[:, :], in0=ET[:, :], scalar1=em2, scalar2=None,
                                                     op0=ALU.is_ge),
                 reads=["f1_et", "f1_em2"], writes=["f1_kp"])
            p.op("vector", lambda e, b=b, i=i: e.tensor_scalar(
                out=GATES[:, (b * NT + i) * 8:(b * NT + i) * 8 + 8], in0=KP[:, :],
                scalar1=gv, scalar2=None, op0=ALU.mult),
                reads=["f1_kp", "f1_gv"], writes=[f"f1_gate{b}_{i}"])
        for fc in range(NF):
            for sl4 in range(4):
                tsl = slice(sl4 * 512, (sl4 + 1) * 512)
                pp, ppk = ctx.psk8(fc * 4 + sl4)
                for dc in range(NC):
                    p.op("tensor", lambda e, dc=dc, fc=fc, pp=pp, tsl=tsl: e.matmul(
                        pp, W64[dc][:, fc * 128:(fc + 1) * 128], XN[dc][:, tsl],
                        start=(dc == 0), stop=(dc == NC - 1)),
                        reads=[wk64[dc], xnk[dc]], writes=[ppk])
                p.op("scalar", lambda e, fc=fc, pp=pp, tsl=tsl: e.activation(
                    out=BB[:, tsl], in_=pp, func=AF.Identity, bias=biT[:, fc:fc + 1]),
                    reads=[ppk, biT_k], writes=["f1_bb"])
            ctx.store(qs, base_sp[b, fc], BB[:, :], ["f1_bb"], writes=[f"f1base{b}_{fc}"])
        for ex in range(8):
            for sl4 in range(4):
                tsl = slice(sl4 * 512, (sl4 + 1) * 512)
                pu = ctx.PS[2][0:16, 0:512]
                for dc in range(NC):
                    p.op("tensor", lambda e, dc=dc, ex=ex, pu=pu, tsl=tsl: e.matmul(
                        pu, ATall[:, (ex * 8 + dc) * 16:(ex * 8 + dc) * 16 + 16], XN[dc][:, tsl],
                        start=(dc == 0), stop=(dc == NC - 1)),
                        reads=["f1_at", xnk[dc]], writes=["PS2_0"])
                p.op("vector", lambda e, pu=pu, tsl=tsl: e.tensor_copy(out=UTB[0:16, tsl], in_=pu),
                     reads=["PS2_0"], writes=["f1_utb"])
            ctx.store(qs, u_sp[b, ex], UTB[0:16, :], ["f1_utb"], writes=[f"f1u{b}_{ex}"])

    WOT = [(W64[fc // 4][:, (fc % 4) * 1024:(fc % 4) * 1024 + 1024], wk64[fc // 4]) for fc in range(NF)]
    for dc in range(NC):
        j = dc % 2
        for q4 in range(4):
            ctx.dma(qw, wo[dc * 128:(dc + 1) * 128, q4 * 1024:(q4 + 1) * 1024], WSTG[j][:, :], f"f1_stg{j}")
            for fq in range(8):
                fc = q4 * 8 + fq
                dst, dstk = WOT[fc]
                transpose_128(WSTG[j][:, fq * 128:(fq + 1) * 128], f"f1_stg{j}",
                              dst[:, dc * 128:(dc + 1) * 128], dstk, fq)
    for b in range(NB):
        for sl8 in range(8):
            t0 = sl8 * 256
            p.dma(qs, lambda e, b=b, t0=t0: e.dma_start(
                out=BSL.rearrange("p (f t) -> p f t", t=256),
                in_=base_sp[b, :, :, t0:t0 + 256].rearrange("f p t -> p f t")),
                reads=[f"f1base{b}_{fc}" for fc in range(NF)], writes=["f1_bsl"])
            if outp is not None:
                for ts2 in range(2):
                    sl2 = slice(t0 + ts2 * 128, t0 + (ts2 + 1) * 128)
                    ctx.dma(qs, x1in[b, sl2, :], X1T[ts2][:, :], f"f1_x1t{ts2}")
                    ctx.dma(qs, s1in[b, sl2, :], S1T[ts2][:, :], f"f1_s1t{ts2}")
            for ex in range(8):
                p.dma(qw, lambda e, ex=ex: e.dma_start(out=BTW[0:16, :], in_=loraBT[ex]),
                      reads=[f"f1bt{ex}_{h2}" for h2 in range(8)], writes=["f1_btw"])
                p.dma(qs, lambda e, b=b, ex=ex, t0=t0: e.dma_start(
                    out=UTe[0:16, :], in_=u_sp[b, ex, :, t0:t0 + 256]),
                    reads=[f"f1u{b}_{ex}"], writes=["f1_ute"])

                def emit_lora(fc):
                    sl4 = fc % 4
                    pl = ctx.PS[2 + sl4 // 2][:, (fc % 2) * 512:(fc % 2) * 512 + 256]
                    plk = f"PS{2 + sl4 // 2}_{fc % 2}"
                    p.op("tensor", lambda e, fc=fc, pl=pl: e.matmul(
                        pl, BTW[0:16, fc * 128:(fc + 1) * 128], UTe[0:16, :],
                        start=True, stop=True),
                        reads=["f1_btw", "f1_ute"], writes=[plk])
                    jf = fc % 2
                    p.op("vector", lambda e, fc=fc, pl=pl, jf=jf: e.tensor_tensor(
                        out=HF[jf][:, :], in0=BSL[:, fc * 256:(fc + 1) * 256], in1=pl, op=ALU.add),
                        reads=["f1_bsl", plk], writes=[f"f1_hf{jf}"])
                    p.op("scalar", lambda e, j=fc % 4, jf=jf: e.activation(
                        out=HB[j][:, :], in_=HF[jf][:, :], func=AF.Relu),
                        reads=[f"f1_hf{jf}"], writes=[f"f1_hb{fc % 4}"])

                emit_lora(0)
                emit_lora(1)
                for fc in range(NF):
                    if fc + 2 < NF:
                        emit_lora(fc + 2)
                    for ts2 in range(2):
                        py = ctx.PS[ts2][:, :]
                        pyk = [f"PS{ts2}_0", f"PS{ts2}_1"]
                        for nh2 in range(2):
                            p.op("tensor", lambda e, fc=fc, j=fc % 4, ts2=ts2, nh2=nh2, py=py: e.matmul(
                                py[:, nh2 * 512:(nh2 + 1) * 512],
                                HB[j][:, ts2 * 128:(ts2 + 1) * 128],
                                WOT[fc][0][:, nh2 * 512:(nh2 + 1) * 512],
                                start=(fc == 0), stop=False),
                                reads=[f"f1_hb{fc % 4}", WOT[fc][1]], writes=[pyk[nh2]])
                for ts2 in range(2):
                    py = ctx.PS[ts2][:, :]
                    pyk = [f"PS{ts2}_0", f"PS{ts2}_1"]
                    for nh2 in range(2):
                        p.op("tensor", lambda e, ts2=ts2, nh2=nh2, py=py: e.matmul(
                            py[:, nh2 * 512:(nh2 + 1) * 512], ctx.ones_row_bf[0:1, :],
                            bor[0:1, nh2 * 512:(nh2 + 1) * 512], start=False, stop=True),
                            reads=["ones_row_bf", bor_k], writes=[pyk[nh2]])
                    i2 = sl8 * 2 + ts2
                    gcol = GATES[:, (b * NT + i2) * 8 + ex:(b * NT + i2) * 8 + ex + 1]
                    gkey = f"f1_gate{b}_{i2}"
                    if ex == 0:
                        p.op("vector", lambda e, ts2=ts2, py=py, gcol=gcol: e.tensor_scalar(
                            out=ACC2[ts2][:, :], in0=py, scalar1=gcol, scalar2=None, op0=ALU.mult),
                            reads=pyk + [gkey], writes=[f"f1_acc{ts2}"])
                    else:
                        p.op("vector", lambda e, ts2=ts2, py=py, gcol=gcol: e.scalar_tensor_tensor(
                            out=ACC2[ts2][:, :], in0=py, scalar=gcol, in1=ACC2[ts2][:, :],
                            op0=ALU.mult, op1=ALU.add),
                            reads=pyk + [gkey, f"f1_acc{ts2}"], writes=[f"f1_acc{ts2}"])
            for ts2 in range(2):
                p.op("scalar", lambda e, ts2=ts2: e.activation(out=GBUF[ts2][:, :], in_=ACC2[ts2][:, :],
                                                               func=AF.Silu),
                     reads=[f"f1_acc{ts2}"], writes=[f"f1_gbuf{ts2}"])
                if outp is None:
                    ctx.store(qs, gout[b, t0 + ts2 * 128:t0 + (ts2 + 1) * 128, :],
                              GBUF[ts2][:, :], [f"f1_gbuf{ts2}"])
                else:
                    p.op("vector", lambda e, ts2=ts2: e.tensor_mul(
                        GBUF[ts2][:, :], GBUF[ts2][:, :], S1T[ts2][:, :]),
                        reads=[f"f1_gbuf{ts2}", f"f1_s1t{ts2}"], writes=[f"f1_gbuf{ts2}"])
                    p.op("vector", lambda e, ts2=ts2: e.tensor_add(
                        GBUF[ts2][:, :], GBUF[ts2][:, :], X1T[ts2][:, :]),
                        reads=[f"f1_gbuf{ts2}", f"f1_x1t{ts2}"], writes=[f"f1_gbuf{ts2}"])
                    ctx.store(qs, outp[b, t0 + ts2 * 128:t0 + (ts2 + 1) * 128, :],
                              GBUF[ts2][:, :], [f"f1_gbuf{ts2}"])



N_CORES = 8


def _new_ctx():
    nc = bass.Bass("TRN2", target_bir_lowering=False, debug=False)
    ctx = Ctx(nc)
    return ctx


def build_ln():
    ctx = _new_ctx()
    nc = ctx.nc
    xin = nc.dram_tensor("xin", [NB, T, D], F32, kind="ExternalInput").ap()
    xout = nc.dram_tensor("xout", [NB, T, D], BF16, kind="ExternalOutput").ap()
    xout32 = nc.dram_tensor("xout32", [NB, T, D], F32, kind="ExternalOutput").ap()
    ctx.shared()
    emit_ln(ctx, xin, xout, xout32)
    ctx.p.build_block()
    return ctx.nc


def build_ffn0():
    ctx = _new_ctx()
    nc = ctx.nc
    xn = nc.dram_tensor("xn", [NB, T, D], BF16, kind="ExternalInput").ap()
    xn32 = nc.dram_tensor("xn32", [NB, T, D], F32, kind="ExternalInput").ap()
    Wg = nc.dram_tensor("Wg", [D, 8], F32, kind="ExternalInput").ap()
    bg = nc.dram_tensor("bg", [8], F32, kind="ExternalInput").ap()
    W1 = nc.dram_tensor("W1", [8, D, 2048], F32, kind="ExternalInput").ap()
    b1 = nc.dram_tensor("b1", [8, 2048], F32, kind="ExternalInput").ap()
    W2 = nc.dram_tensor("W2", [8, 2048, D], F32, kind="ExternalInput").ap()
    b2 = nc.dram_tensor("b2", [8, D], F32, kind="ExternalInput").ap()
    g = nc.dram_tensor("g", [NB, T, D], F32, kind="ExternalOutput").ap()
    ctx.shared()
    emit_ffn0(ctx, xn, xn32, Wg, bg, W1, b1, W2, b2, g)
    ctx.p.build_block()
    return ctx.nc


def build_attn0():
    ctx = _new_ctx()
    nc = ctx.nc
    xn = nc.dram_tensor("xn", [NB, T, D], BF16, kind="ExternalInput").ap()
    wq = nc.dram_tensor("wq", [D, D], F32, kind="ExternalInput").ap()
    wk = nc.dram_tensor("wk", [D, D], F32, kind="ExternalInput").ap()
    wv = nc.dram_tensor("wv", [D, D], F32, kind="ExternalInput").ap()
    wo = nc.dram_tensor("wo", [D, D], F32, kind="ExternalInput").ap()
    wqb = nc.dram_tensor("wqb", [D], F32, kind="ExternalInput").ap()
    wkb = nc.dram_tensor("wkb", [D], F32, kind="ExternalInput").ap()
    wvb = nc.dram_tensor("wvb", [D], F32, kind="ExternalInput").ap()
    wob = nc.dram_tensor("wob", [D], F32, kind="ExternalInput").ap()
    thr = nc.dram_tensor("thr", [16], F32, kind="ExternalInput").ap()
    sharp = nc.dram_tensor("sharp", [16], F32, kind="ExternalInput").ap()
    s = nc.dram_tensor("s", [NB, T, D], F32, kind="ExternalOutput").ap()
    ctx.shared()
    emit_attn0(ctx, xn, wq, wk, wv, wo, wqb, wkb, wvb, wob, thr, sharp, s)
    ctx.p.build_block()
    return ctx.nc


def build_gla(debug=False):
    ctx = _new_ctx()
    nc = ctx.nc
    xn = nc.dram_tensor("xn", [NB, T, D], BF16, kind="ExternalInput").ap()
    Wq = nc.dram_tensor("Wq", [D, 256], F32, kind="ExternalInput").ap()
    bq = nc.dram_tensor("bq", [256], F32, kind="ExternalInput").ap()
    Wk = nc.dram_tensor("Wk", [D, 256], F32, kind="ExternalInput").ap()
    bk = nc.dram_tensor("bk", [256], F32, kind="ExternalInput").ap()
    Wv = nc.dram_tensor("Wv", [D, 256], F32, kind="ExternalInput").ap()
    bv = nc.dram_tensor("bv", [256], F32, kind="ExternalInput").ap()
    Wg = nc.dram_tensor("Wg", [D, 256], F32, kind="ExternalInput").ap()
    bg = nc.dram_tensor("bg", [256], F32, kind="ExternalInput").ap()
    Wo = nc.dram_tensor("Wo", [256, D], F32, kind="ExternalInput").ap()
    bo = nc.dram_tensor("bo", [D], F32, kind="ExternalInput").ap()
    s = nc.dram_tensor("s", [NB, T, D], F32, kind="ExternalOutput").ap()
    dbg = None
    if debug:
        dbg = {
            "LG": nc.dram_tensor("dbg_LG", [NB, 2, 128, T], F32, kind="ExternalOutput").ap(),
            "OF": nc.dram_tensor("dbg_OF", [NB, 2, 128, T], BF16, kind="ExternalOutput").ap(),
            "S": nc.dram_tensor("dbg_S", [4, 2, 128, 256], F32, kind="ExternalOutput").ap(),
            "QT": nc.dram_tensor("dbg_QT", [4, 2, 128, 64], BF16, kind="ExternalOutput").ap(),
            "F2": nc.dram_tensor("dbg_F2", [4, 2, 128, 64], BF16, kind="ExternalOutput").ap(),
            "KTq": nc.dram_tensor("dbg_KT", [4, 2, 128, 64], BF16, kind="ExternalOutput").ap(),
            "COLS": nc.dram_tensor("dbg_COLS", [4, 128, 8], F32, kind="ExternalOutput").ap(),
            "AT": nc.dram_tensor("dbg_AT", [4, 64, 64], BF16, kind="ExternalOutput").ap(),
        }
    ctx.shared()
    emit_gla(ctx, xn, Wq, bq, Wk, bk, Wv, bv, Wg, bg, Wo, bo, s, debug=dbg)
    ctx.p.build_block()
    return ctx.nc


def build_ffn1():
    ctx = _new_ctx()
    nc = ctx.nc
    xn = nc.dram_tensor("xn", [NB, T, D], BF16, kind="ExternalInput").ap()
    xn32 = nc.dram_tensor("xn32", [NB, T, D], F32, kind="ExternalInput").ap()
    wi = nc.dram_tensor("wi", [4096, D], F32, kind="ExternalInput").ap()
    bi = nc.dram_tensor("bi", [4096], F32, kind="ExternalInput").ap()
    wo = nc.dram_tensor("wo", [D, 4096], F32, kind="ExternalInput").ap()
    bo = nc.dram_tensor("bo", [D], F32, kind="ExternalInput").ap()
    lora_A = nc.dram_tensor("lora_A", [8, 16, D], F32, kind="ExternalInput").ap()
    lora_B = nc.dram_tensor("lora_B", [8, 4096, 16], F32, kind="ExternalInput").ap()
    gate_w = nc.dram_tensor("gate_w", [8, D], F32, kind="ExternalInput").ap()
    gate_b = nc.dram_tensor("gate_b", [8], F32, kind="ExternalInput").ap()
    g = nc.dram_tensor("g", [NB, T, D], F32, kind="ExternalOutput").ap()
    base_sp = nc.dram_tensor("base_sp", [NB, 32, 128, T], BF16, kind="Internal").ap()
    u_sp = nc.dram_tensor("u_sp", [NB, 8, 16, T], BF16, kind="Internal").ap()
    loraBT = nc.dram_tensor("loraBT", [8, 16, 4096], BF16, kind="Internal").ap()
    ctx.shared()
    emit_ffn1(ctx, xn, xn32, wi, bi, wo, bo, lora_A, lora_B, gate_w, gate_b,
              base_sp, u_sp, loraBT, g)
    ctx.p.build_block()
    return ctx.nc


def build_comb():
    ctx = _new_ctx()
    nc = ctx.nc
    xin = nc.dram_tensor("xin", [NB, T, D], F32, kind="ExternalInput").ap()
    sin = nc.dram_tensor("sin", [NB, T, D], F32, kind="ExternalInput").ap()
    gin = nc.dram_tensor("gin", [NB, T, D], F32, kind="ExternalInput").ap()
    xout = nc.dram_tensor("xout", [NB, T, D], F32, kind="ExternalOutput").ap()
    ctx.shared()
    emit_comb(ctx, xin, sin, gin, xout)
    ctx.p.build_block()
    return ctx.nc


def build_fused():
    ctx = _new_ctx()
    nc = ctx.nc

    def inp(name, shape, dt=F32):
        return nc.dram_tensor(name, shape, dt, kind="ExternalInput").ap()

    X = inp("L0_attn_x", [NB, T, D])
    wq = inp("L0_attn_wq_w", [D, D]); wqb = inp("L0_attn_wq_b", [D])
    wk = inp("L0_attn_wk_w", [D, D]); wkb = inp("L0_attn_wk_b", [D])
    wv = inp("L0_attn_wv_w", [D, D]); wvb = inp("L0_attn_wv_b", [D])
    wo = inp("L0_attn_wo_w", [D, D]); wob = inp("L0_attn_wo_b", [D])
    thr = inp("L0_attn_learned_threshold", [16])
    sharp = inp("L0_attn_gate_sharpness", [16])
    Wg0 = inp("L0_ffn_Wg", [D, 8]); bg0 = inp("L0_ffn_bg", [8])
    W10 = inp("L0_ffn_W1", [8, D, 2048]); b10 = inp("L0_ffn_b1", [8, 2048])
    W20 = inp("L0_ffn_W2", [8, 2048, D]); b20 = inp("L0_ffn_b2", [8, D])
    Wq1 = inp("L1_attn_Wq", [D, 256]); bq1 = inp("L1_attn_bq", [256])
    Wk1 = inp("L1_attn_Wk", [D, 256]); bk1 = inp("L1_attn_bk", [256])
    Wv1 = inp("L1_attn_Wv", [D, 256]); bv1 = inp("L1_attn_bv", [256])
    Wg1 = inp("L1_attn_Wg", [D, 256]); bg1 = inp("L1_attn_bg", [256])
    Wo1 = inp("L1_attn_Wo", [256, D]); bo1 = inp("L1_attn_bo", [D])
    wi1 = inp("L1_ffn_wi", [4096, D]); bi1 = inp("L1_ffn_bi", [4096])
    wo1 = inp("L1_ffn_wo", [D, 4096]); boo1 = inp("L1_ffn_bo", [D])
    lA1 = inp("L1_ffn_lora_A", [8, 16, D]); lB1 = inp("L1_ffn_lora_B", [8, 4096, 16])
    gw1 = inp("L1_ffn_gate_w", [8, D]); gb1 = inp("L1_ffn_gate_b", [8])
    OUT = nc.dram_tensor("out", [NB, T, D], F32, kind="ExternalOutput").ap()

    scr = lambda name, shape, dt: nc.dram_tensor(name, shape, dt, kind="Internal").ap()
    xn0 = scr("xn0", [NB, T, D], BF16)
    xn0_32 = scr("xn0_32", [NB, T, D], F32)
    g_sp = scr("g_sp", [NB, T, D], F32)
    s_sp = scr("s_sp", [NB, T, D], F32)
    x1_sp = scr("x1_sp", [NB, T, D], F32)
    xn1 = scr("xn1", [NB, T, D], BF16)
    xn1_32 = scr("xn1_32", [NB, T, D], F32)
    base_sp = scr("base_sp", [NB, 32, 128, T], BF16)
    u_sp = scr("u_sp", [NB, 8, 16, T], BF16)
    loraBT = scr("loraBT", [8, 16, 4096], BF16)

    ctx.shared()

    def stage(fn, *args, **kw):
        ctx.push_scope()
        fn(ctx, *args, **kw)
        ctx.pop_scope()

    stage(emit_ln, X, xn0, xn0_32)
    stage(emit_ffn0, xn0, xn0_32, Wg0, bg0, W10, b10, W20, b20, g_sp)
    stage(emit_attn0, xn0, wq, wk, wv, wo, wqb, wkb, wvb, wob, thr, sharp, s_sp)
    stage(emit_comb_ln, X, s_sp, g_sp, x1_sp, xn1, xn1_32)
    stage(emit_gla, xn1, Wq1, bq1, Wk1, bk1, Wv1, bv1, Wg1, bg1, Wo1, bo1, s_sp)
    stage(emit_ffn1, xn1, xn1_32, wi1, bi1, wo1, boo1, lA1, lB1, gw1, gb1,
          base_sp, u_sp, loraBT, None, x1in=x1_sp, s1in=s_sp, outp=OUT)
    ctx.p.build_block()
    return ctx.nc


def kernel_fused(**inputs):
    x = np.ascontiguousarray(inputs["L0_attn_x"])
    nc = _get_nc("fused", build_fused)
    wnames = [n for n in inputs if n != "L0_attn_x"]
    in_maps = []
    for c in range(N_CORES):
        m = {n: inputs[n] for n in wnames}
        m["L0_attn_x"] = x[2 * c:2 * c + 2]
        in_maps.append(m)
    res = run_bass_kernel_spmd(nc, in_maps, core_ids=list(range(N_CORES)))
    return np.concatenate([np.asarray(r["out"]) for r in res.results], axis=0)


_NC_CACHE = {}


def _get_nc(name, builder):
    if name not in _NC_CACHE:
        _NC_CACHE[name] = builder()
    return _NC_CACHE[name]


def kernel(**inputs):
    return kernel_fused(**inputs)


def kernel_unfused(**inputs):
    ins = inputs
    x = np.ascontiguousarray(ins["L0_attn_x"])
    xs = [x[2 * c:2 * c + 2] for c in range(N_CORES)]
    cores = list(range(N_CORES))

    def launch(name, builder, per_core, shared):
        nc = _get_nc(name, builder)
        in_maps = [dict(per_core[c], **shared) for c in range(N_CORES)]
        res = run_bass_kernel_spmd(nc, in_maps, core_ids=cores)
        return res.results

    r = launch("ln", build_ln, [{"xin": xs[c]} for c in range(N_CORES)], {})
    xn0 = [r[c]["xout"] for c in range(N_CORES)]
    xn0_32 = [r[c]["xout32"] for c in range(N_CORES)]

    r = launch("ffn0", build_ffn0,
               [{"xn": xn0[c], "xn32": xn0_32[c]} for c in range(N_CORES)],
               {"Wg": ins["L0_ffn_Wg"], "bg": ins["L0_ffn_bg"],
                "W1": ins["L0_ffn_W1"], "b1": ins["L0_ffn_b1"],
                "W2": ins["L0_ffn_W2"], "b2": ins["L0_ffn_b2"]})
    g0 = [r[c]["g"] for c in range(N_CORES)]

    r = launch("attn0", build_attn0,
               [{"xn": xn0[c]} for c in range(N_CORES)],
               {"wq": ins["L0_attn_wq_w"], "wk": ins["L0_attn_wk_w"],
                "wv": ins["L0_attn_wv_w"], "wo": ins["L0_attn_wo_w"],
                "wqb": ins["L0_attn_wq_b"], "wkb": ins["L0_attn_wk_b"],
                "wvb": ins["L0_attn_wv_b"], "wob": ins["L0_attn_wo_b"],
                "thr": ins["L0_attn_learned_threshold"],
                "sharp": ins["L0_attn_gate_sharpness"]})
    s0 = [r[c]["s"] for c in range(N_CORES)]

    r = launch("comb", build_comb,
               [{"xin": xs[c], "sin": s0[c], "gin": g0[c]} for c in range(N_CORES)], {})
    x1 = [r[c]["xout"] for c in range(N_CORES)]

    r = launch("ln", build_ln, [{"xin": x1[c]} for c in range(N_CORES)], {})
    xn1 = [r[c]["xout"] for c in range(N_CORES)]
    xn1_32 = [r[c]["xout32"] for c in range(N_CORES)]

    r = launch("gla", build_gla,
               [{"xn": xn1[c]} for c in range(N_CORES)],
               {"Wq": ins["L1_attn_Wq"], "bq": ins["L1_attn_bq"],
                "Wk": ins["L1_attn_Wk"], "bk": ins["L1_attn_bk"],
                "Wv": ins["L1_attn_Wv"], "bv": ins["L1_attn_bv"],
                "Wg": ins["L1_attn_Wg"], "bg": ins["L1_attn_bg"],
                "Wo": ins["L1_attn_Wo"], "bo": ins["L1_attn_bo"]})
    s1 = [r[c]["s"] for c in range(N_CORES)]

    r = launch("ffn1", build_ffn1,
               [{"xn": xn1[c], "xn32": xn1_32[c]} for c in range(N_CORES)],
               {"wi": ins["L1_ffn_wi"], "bi": ins["L1_ffn_bi"],
                "wo": ins["L1_ffn_wo"], "bo": ins["L1_ffn_bo"],
                "lora_A": ins["L1_ffn_lora_A"], "lora_B": ins["L1_ffn_lora_B"],
                "gate_w": ins["L1_ffn_gate_w"], "gate_b": ins["L1_ffn_gate_b"]})
    g1 = [r[c]["g"] for c in range(N_CORES)]

    r = launch("comb", build_comb,
               [{"xin": x1[c], "sin": s1[c], "gin": g1[c]} for c in range(N_CORES)], {})
    out = np.concatenate([np.asarray(r[c]["xout"]) for c in range(N_CORES)], axis=0)
    return out
